# version 11
# baseline (speedup 1.0000x reference)
import numpy as np
import ml_dtypes
from contextlib import ExitStack

import concourse.bass as bass
import concourse.mybir as mybir
from concourse.bass_utils import run_bass_kernel_spmd

F32 = mybir.dt.float32
BF16 = mybir.dt.bfloat16
U32 = mybir.dt.uint32
AF = mybir.ActivationFunctionType
ALU = mybir.AluOpType
AX = mybir.AxisListType
BD16 = ml_dtypes.bfloat16

D = 1024
TM = 2048
TF = 4096
DH = 64
LAMBDA_INIT = 0.7 - 0.6 * float(np.exp(-0.3))
EPS_LN = 1e-5
EPS_FEAT = 1e-4
EPS_DEN = 0.01
C0 = float(DH) ** -0.25
LN_RATIO = float(np.log(64.0 ** -0.5))
NEGBIG = -1e9
BIGI = 1e4
GROUPS = [[0, 1], [2, 3], [4, 5], [6, 7]]

ENGINES = ("sync", "gpsimd", "tensor", "vector", "scalar")

FUSED = True
SIM_SAFE = False

_CONST_VALS = []


def _c(v):
    v = float(v)
    if v not in _CONST_VALS:
        _CONST_VALS.append(v)
    return v


for _v in (1.0 / D, EPS_LN, LAMBDA_INIT, C0, LN_RATIO, EPS_FEAT, EPS_DEN,
           1e-30, -1.0, 1.0, 0.125, -0.125, 1.0 / 128.0, BIGI,
           1.0 - LAMBDA_INIT, -0.5 * C0 * C0, -C0, 0.0, 2.0, 3.0, 4.0, 5.0,
           6.0, 7.0):
    _c(_v)


def _full(t):
    return t[tuple(slice(None) for _ in range(len(t.shape)))]


class Sem:
    def __init__(self, st, nc, name):
        self.h = st.enter_context(nc.semaphore(name))
        self.n = 0

    def bump(self, k=1):
        self.n += k
        return self.n


class Ph:

    def __init__(self, bd):
        self.bd = bd
        self.ops = {e: [] for e in ENGINES}
        self.pending = {e: [] for e in ENGINES}

    def barrier(self, marks=None):
        if marks is None:
            marks = self.bd.snapshot()
        for e in ENGINES:
            self.pending[e] = self.pending[e] + list(marks)

    def on(self, eng, fn, wait=None, inc=None, inc_k=1, inc2=None, inc2_k=1):
        w = self.pending[eng] + list(wait or [])
        self.pending[eng] = []
        if SIM_SAFE and inc is not None and inc_k == 16:
            w.append((inc, inc.n))
        if inc is not None:
            inc.bump(inc_k)
        if inc2 is not None:
            inc2.bump(inc2_k)
        self.ops[eng].append((fn, w, inc, inc_k, inc2, inc2_k))

    def emit(self):
        nc = self.bd.nc
        with nc.Block() as blk:
            for eng in ENGINES:
                ops = self.ops[eng]
                if not ops:
                    continue

                def mk(ops):
                    def closure(e):
                        hi = {}
                        for fn, waits, sem, k, sem2, k2 in ops:
                            for (s, v) in waits:
                                if v > 0 and hi.get(id(s), -1) < v:
                                    e.wait_ge(s.h, v)
                                    hi[id(s)] = v
                            ins = fn(e)
                            if sem is not None:
                                ins = ins.then_inc(sem.h, k)
                            if sem2 is not None:
                                ins.then_inc(sem2.h, k2)
                    return closure

                getattr(blk, eng)(mk(ops))


class Bld:
    def __init__(self, nc, st, io):
        self.nc = nc
        self.st = st
        self.io = io
        self.sb = {}
        self.AD = Sem(st, nc, "AD")
        self.GD = Sem(st, nc, "GD")
        self.MM = Sem(st, nc, "MM")
        self.VE = Sem(st, nc, "VE")
        self.SE = Sem(st, nc, "SE")
        self.WF = Sem(st, nc, "WF")
        self.allsems = [self.AD, self.GD, self.MM, self.VE, self.SE, self.WF]
        self.ccsems = []
        self.ev = {}
        self.nphase = 0

    def snapshot(self):
        return [(s, s.n) for s in self.allsems if s.n > 0] + \
               [(s, s.n) for s in self.ccsems if s.n > 0]

    def tag(self):
        self.nphase += 1
        return f"z{self.nphase}_"

    def ccsem(self):
        s = Sem(self.st, self.nc, f"CC{len(self.ccsems)}")
        self.ccsems.append(s)
        return s


PRELOAD_TENS = [
    ("consts", None, F32), ("ident", [128, 128], BF16),
    ("identf", [128, 128], F32), ("iotam", [128, 128], F32),
    ("masks", [128, 4, 512], F32),
    ("l0_bq", [128, 4], F32), ("l0_bk", [128, 4], F32),
    ("l0_bvr", [1, 512], F32), ("l0_bo", [128, 8], F32),
    ("s1c", [128, 32], F32), ("s2c", [128, 32], F32),
    ("sh1c", [128, 32], F32), ("sh2c", [128, 32], F32),
    ("s3c", [128, 8], F32), ("coloff", [1, 1], U32),
    ("lq1", [8, 64], F32), ("lk1", [8, 64], F32),
    ("lq2", [8, 64], F32), ("lk2", [8, 64], F32),
]


def emit_preload(bd):
    nc, io, sb = bd.nc, bd.io, bd.sb
    ph = Ph(bd)
    for nm, shp, dt in PRELOAD_TENS:
        if shp is None:
            shp = [128, max(1, len(_CONST_VALS))]
        sb[nm] = bd.st.enter_context(nc.sbuf_tensor("c_" + nm, shp, dt))
    sb["projT"] = bd.st.enter_context(nc.sbuf_tensor("c_projT", [128, 4, 64], BF16))
    sb["ones_f"] = bd.st.enter_context(nc.sbuf_tensor("c_onesf", [128, 130], F32))
    sb["ones_b"] = bd.st.enter_context(nc.sbuf_tensor("c_onesb", [128, 130], BF16))
    sb["lamrow"] = bd.st.enter_context(nc.sbuf_tensor("c_lamrow", [1, 8], F32))

    for i, v in enumerate(_CONST_VALS):
        nc.const_aps.register(F32, v, sb["consts"][:, i:i + 1])

    for nm, _, _ in PRELOAD_TENS:
        ph.on("sync", lambda e, nm=nm: e.dma_start(out=_full(bd.sb[nm]), in_=_full(io[nm])),
              inc=bd.AD, inc_k=16)
    ph.on("gpsimd", lambda e: e.dma_start(out=_full(sb["projT"]), in_=_full(io["l0_projT"])),
          inc=bd.GD, inc_k=16)
    ph.on("vector", lambda e: e.memset(sb["ones_f"][:, :], 1.0), inc=bd.VE)
    ph.on("vector", lambda e: e.memset(sb["ones_b"][:, :], 1.0), inc=bd.VE)

    st2 = ExitStack()
    lsc = st2.enter_context(nc.sbuf_tensor("pre_lsc", [16, 4], F32))
    lpad = st2.enter_context(nc.sbuf_tensor("pre_lpad", [16, 128], BF16))
    pst = st2.enter_context(nc.psum_tensor("pre_pst", [128, 128], BF16))
    ph.barrier()
    for j, (a, bnm) in enumerate((("lq1", "lk1"), ("lq2", "lk2"))):
        ph.on("vector", lambda e, a=a, bnm=bnm: e.tensor_mul(sb[a][0:8, :], sb[a][0:8, :], sb[bnm][0:8, :]),
              inc=bd.VE)
        ph.on("vector", lambda e, a=a, j=j: e.tensor_reduce(lsc[0:8, j:j + 1], sb[a][0:8, :], AX.X, ALU.add),
              wait=[(bd.VE, bd.VE.n)], inc=bd.VE)
        ph.on("scalar", lambda e, j=j: e.activation(lsc[0:8, j:j + 1], lsc[0:8, j:j + 1], AF.Exp),
              wait=[(bd.VE, bd.VE.n)], inc=bd.SE)
    ph.on("vector", lambda e: e.tensor_sub(lsc[0:8, 0:1], lsc[0:8, 0:1], lsc[0:8, 1:2]),
          wait=[(bd.SE, bd.SE.n)], inc=bd.VE)
    ph.on("vector", lambda e: e.tensor_scalar(lsc[0:8, 0:1], lsc[0:8, 0:1], _c(LAMBDA_INIT), None, ALU.add),
          wait=[(bd.VE, bd.VE.n)], inc=bd.VE)
    ph.on("vector", lambda e: e.memset(lpad[:, :], 0.0), inc=bd.VE)
    ph.on("vector", lambda e: e.tensor_copy(lpad[0:8, 0:1], lsc[0:8, 0:1]),
          wait=[(bd.VE, bd.VE.n)], inc=bd.VE)
    ph.on("tensor", lambda e: e.transpose(pst[0:128, 0:16], lpad[0:16, :], sb["ident"][0:16, 0:16]),
          wait=[(bd.VE, bd.VE.n)], inc=bd.MM)
    ph.on("vector", lambda e: e.tensor_copy(sb["lamrow"][0:1, 0:8], pst[0:1, 0:8]),
          wait=[(bd.MM, bd.MM.n)], inc=bd.VE)
    ph.emit()
    st2.close()
    bd.ev["preload"] = bd.snapshot()


def emit_ln(bd, src_name, dst_name):
    nc, io, sb = bd.nc, bd.io, bd.sb
    ph = Ph(bd)
    ph.barrier(bd.ev.get(src_name, []) + bd.snapshot())
    st2 = ExitStack()
    zt = bd.tag()
    xin = [st2.enter_context(nc.sbuf_tensor(zt + f"ln_x{k}", [128, TM], F32)) for k in range(8)]
    xsq = st2.enter_context(nc.sbuf_tensor(zt + zt + "ln_xsq", [128, 512], F32))
    rows = st2.enter_context(nc.sbuf_tensor(zt + "ln_rows", [1, 5 * TM], F32))
    tmp = st2.enter_context(nc.sbuf_tensor(zt + "ln_tmp", [128, 512], F32))
    xno = st2.enter_context(nc.sbuf_tensor(zt + "ln_out", [128, 8, TM], BF16))
    ps_m = st2.enter_context(nc.psum_tensor(zt + "ln_psm", [1, 512], F32))
    ps_s = st2.enter_context(nc.psum_tensor(zt + "ln_pss", [1, 512], F32))
    ps_b1 = st2.enter_context(nc.psum_tensor(zt + "ln_psb1", [128, 512], F32))
    ps_b2 = st2.enter_context(nc.psum_tensor(zt + "ln_psb2", [128, 512], F32))
    src = io[src_name]
    ones_col = sb["ones_f"][:, 0:1]
    ones_row = sb["ones_f"][0:1, 0:128]

    xw = []
    for k in range(8):
        ph.on("sync", lambda e, k=k: e.dma_start(out=xin[k][:, :], in_=src[k * 128:(k + 1) * 128, :]),
              inc=bd.AD, inc_k=16)
        xw.append((bd.AD, bd.AD.n))

    evac_prev = None
    norm_prev = None
    for nt in range(4):
        sl = slice(nt * 512, (nt + 1) * 512)
        wbank = [(bd.VE, evac_prev)] if evac_prev else []
        for k in range(8):
            ph.on("tensor", lambda e, k=k, sl=sl: e.matmul(
                ps_m[0:1, :], ones_col, xin[k][:, sl], start=(k == 0), stop=(k == 7)),
                wait=[xw[k]] + wbank, inc=(bd.MM if k == 7 else None))
        for k in range(8):
            wsq = [xw[k]]
            if nt > 0 or k > 0:
                wsq.append((bd.MM, bd.MM.n))
            ph.on("scalar", lambda e, k=k, sl=sl: e.activation(xsq[:, :], xin[k][:, sl], AF.Square),
                  wait=wsq, inc=bd.SE)
            ph.on("tensor", lambda e, k=k: e.matmul(
                ps_s[0:1, :], ones_col, xsq[:, :], start=(k == 0), stop=(k == 7)),
                wait=[(bd.SE, bd.SE.n)] + wbank, inc=bd.MM)
        mm_stats = bd.MM.n
        o = nt * 512
        mrow = rows[0:1, o:o + 512]
        srow = rows[0:1, TM + o:TM + o + 512]
        vrow = rows[0:1, 2 * TM + o:2 * TM + o + 512]
        rrow = rows[0:1, 3 * TM + o:3 * TM + o + 512]
        mrr = rows[0:1, 4 * TM + o:4 * TM + o + 512]
        ph.on("vector", lambda e, mrow=mrow: e.tensor_scalar(mrow, ps_m[0:1, :], _c(1.0 / D), None, ALU.mult),
              wait=[(bd.MM, mm_stats)], inc=bd.VE)
        ph.on("vector", lambda e, srow=srow: e.tensor_scalar(srow, ps_s[0:1, :], _c(1.0 / D), None, ALU.mult),
              inc=bd.VE)
        ph.on("vector", lambda e, vrow=vrow, mrow=mrow: e.tensor_mul(vrow, mrow, mrow),
              wait=[(bd.VE, bd.VE.n)], inc=bd.VE)
        ph.on("vector", lambda e, vrow=vrow, srow=srow: e.tensor_sub(vrow, srow, vrow),
              wait=[(bd.VE, bd.VE.n)], inc=bd.VE)
        evac_prev = bd.VE.n
        ph.on("scalar", lambda e, vrow=vrow, rrow=rrow: e.activation(rrow, vrow, AF.Sqrt, bias=_c(EPS_LN)),
              wait=[(bd.VE, bd.VE.n)], inc=bd.SE)
        ph.on("vector", lambda e, rrow=rrow: e.reciprocal(rrow, rrow), wait=[(bd.SE, bd.SE.n)], inc=bd.VE)
        ph.on("vector", lambda e, mrr=mrr, mrow=mrow, rrow=rrow: e.tensor_mul(mrr, mrow, rrow),
              wait=[(bd.VE, bd.VE.n)], inc=bd.VE)
        rows_done = bd.VE.n
        wbc = [(bd.VE, rows_done)]
        if norm_prev:
            wbc.append((bd.VE, norm_prev))
        ph.on("tensor", lambda e, rrow=rrow: e.matmul(ps_b1[:, :], ones_row, rrow, start=True, stop=True),
              wait=wbc, inc=None)
        ph.on("tensor", lambda e, mrr=mrr: e.matmul(ps_b2[:, :], ones_row, mrr, start=True, stop=True),
              inc=bd.MM)
        bc = bd.MM.n
        for k in range(8):
            ph.on("vector", lambda e, k=k, sl=sl: e.tensor_mul(tmp[:, :], xin[k][:, sl], ps_b1[:, :]),
                  wait=[(bd.MM, bc), (bd.VE, bd.VE.n)], inc=bd.VE)
            ph.on("vector", lambda e, k=k, sl=sl: e.tensor_sub(xno[:, k, sl], tmp[:, :], ps_b2[:, :]),
                  wait=[(bd.VE, bd.VE.n)], inc=bd.VE)
        norm_prev = bd.VE.n
    done = bd.VE.n
    for k in range(8):
        ph.on("sync", lambda e, k=k: e.dma_start(
            out=io[dst_name][k * 128:(k + 1) * 128, :], in_=xno[:, k, :]),
            wait=[(bd.VE, done)], inc=bd.AD, inc_k=16)
    ph.emit()
    st2.close()
    bd.ev[dst_name] = [(bd.AD, bd.AD.n)]


def emit_combine(bd, x_name, s_chunks, g_name, out_name):
    nc, io = bd.nc, bd.io
    ph = Ph(bd)
    ph.barrier(bd.ev.get(x_name, []) + bd.ev.get(g_name, []) + bd.snapshot())
    st2 = ExitStack()
    zt = bd.tag()
    xc = [st2.enter_context(nc.sbuf_tensor(zt + f"cb_x{j}", [128, TM], F32)) for j in range(2)]
    gc = [st2.enter_context(nc.sbuf_tensor(zt + f"cb_g{j}", [128, TM], BF16)) for j in range(2)]
    sg = st2.enter_context(nc.sbuf_tensor(zt + "cb_sg", [128, TM], F32))
    oc = [st2.enter_context(nc.sbuf_tensor(zt + f"cb_o{j}", [128, TM], F32)) for j in range(2)]
    st_done = []
    for k in range(8):
        j = k % 2
        w = [(bd.AD, st_done[k - 2])] if k >= 2 else []
        ph.on("sync", lambda e, k=k, j=j: e.dma_start(out=xc[j][:, :], in_=io[x_name][k * 128:(k + 1) * 128, :]),
              wait=w, inc=bd.AD, inc_k=16)
        ph.on("sync", lambda e, k=k, j=j: e.dma_start(out=gc[j][:, :], in_=io[g_name][k * 128:(k + 1) * 128, :]),
              inc=bd.AD, inc_k=16)
        adn = bd.AD.n
        ph.on("vector", lambda e, k=k, j=j: e.tensor_mul(sg[:, :], s_chunks(k), gc[j][:, :]),
              wait=[(bd.AD, adn), (bd.VE, bd.VE.n)], inc=bd.VE)
        ph.on("vector", lambda e, k=k, j=j: e.tensor_add(oc[j][:, :], xc[j][:, :], sg[:, :]),
              wait=[(bd.VE, bd.VE.n)], inc=bd.VE)
        ph.on("sync", lambda e, k=k, j=j: e.dma_start(out=io[out_name][k * 128:(k + 1) * 128, :], in_=oc[j][:, :]),
              wait=[(bd.VE, bd.VE.n)], inc=bd.AD, inc_k=16)
        st_done.append(bd.AD.n)
    ph.emit()
    st2.close()
    bd.ev[out_name] = [(bd.AD, bd.AD.n)]


def emit_ffn0(bd):
    nc, io, sb = bd.nc, bd.io, bd.sb
    ph = Ph(bd)
    ph.barrier(bd.ev.get("ag0in", []) + bd.snapshot())
    st2 = ExitStack()
    TP = 1024
    xnp = st2.enter_context(nc.sbuf_tensor("f0_xn", [128, 8, TP], BF16))
    h1 = st2.enter_context(nc.sbuf_tensor("f0_h1", [128, 32, TP], BF16))
    h2 = st2.enter_context(nc.sbuf_tensor("f0_h2", [128, 32, TP], BF16))
    wa = [st2.enter_context(nc.sbuf_tensor(f"f0_wa{j}", [128, 8, 128], BF16)) for j in range(2)]
    wb = [st2.enter_context(nc.sbuf_tensor(f"f0_wb{j}", [128, 8, 128], BF16)) for j in range(2)]
    wh = [st2.enter_context(nc.sbuf_tensor(f"f0_wh{j}", [128, 32, 128], BF16)) for j in range(2)]
    ea = [st2.enter_context(nc.sbuf_tensor(f"f0_ea{j}", [128, 512], BF16)) for j in range(2)]
    eb = [st2.enter_context(nc.sbuf_tensor(f"f0_eb{j}", [128, 512], BF16)) for j in range(2)]
    go = st2.enter_context(nc.sbuf_tensor("f0_go", [128, TP], BF16))
    psA = [st2.enter_context(nc.psum_tensor(f"f0_psA{j}", [128, 512], F32)) for j in range(2)]
    psB = [st2.enter_context(nc.psum_tensor(f"f0_psB{j}", [128, 512], F32)) for j in range(2)]

    for p in range(2):
        ph.barrier()
        for k in range(8):
            ph.on("sync", lambda e, k=k, p=p: e.dma_start(
                out=xnp[:, k, :], in_=io["ag0in"][k * 128:(k + 1) * 128, p * TP:(p + 1) * TP]),
                inc=bd.AD, inc_k=16)
        xn_ready = (bd.AD, bd.AD.n)

        gd_marks = []
        wf_base = bd.WF.n
        for i in range(32):
            w = [(bd.WF, wf_base + 2 * (i - 1))] if i >= 2 else []
            ph.on("gpsimd", lambda e, i=i: e.dma_start(
                out=_full(wa[i % 2]),
                in_=io["w1T"][:, i * 128:(i + 1) * 128].rearrange("(a p) o -> p a o", p=128)),
                wait=w, inc=bd.GD, inc_k=16)
            ph.on("gpsimd", lambda e, i=i: e.dma_start(
                out=_full(wb[i % 2]),
                in_=io["w2T"][:, i * 128:(i + 1) * 128].rearrange("(a p) o -> p a o", p=128)),
                inc=bd.GD, inc_k=16)
            gd_marks.append(bd.GD.n)
        ev_marks = []
        for i in range(32):
            for nt in range(2):
                j = nt
                sl = slice(nt * 512, (nt + 1) * 512)
                wq = [(bd.GD, gd_marks[i]), xn_ready]
                idx = len(ev_marks) - 2
                if idx >= 0:
                    wq.append((bd.SE, ev_marks[idx][0]))
                    wq.append((bd.VE, ev_marks[idx][1]))
                for k in range(8):
                    ph.on("tensor", lambda e, i=i, k=k, sl=sl, j=j: e.matmul(
                        psA[j][:, :], wa[i % 2][:, k, :], xnp[:, k, sl],
                        start=(k == 0), stop=(k == 7)), wait=wq if k == 0 else [], inc=None)
                for k in range(8):
                    last = (k == 7)
                    ph.on("tensor", lambda e, i=i, k=k, sl=sl, j=j: e.matmul(
                        psB[j][:, :], wb[i % 2][:, k, :], xnp[:, k, sl],
                        start=(k == 0), stop=(k == 7)),
                        inc=(bd.MM if last else None))
                mmn = bd.MM.n
                if nt == 1:
                    ph.on("tensor", lambda e: e.nop(), wait=[(bd.MM, mmn)], inc=bd.WF, inc_k=2)
                ph.on("scalar", lambda e, i=i, j=j: e.activation(
                    ea[j][:, :], psA[j][:, :], AF.Silu, scale=sb["s1c"][:, i:i + 1]),
                    wait=[(bd.MM, mmn)], inc=bd.SE)
                ph.on("vector", lambda e, i=i, j=j: e.tensor_scalar(
                    eb[j][:, :], psB[j][:, :], sb["s2c"][:, i:i + 1], None, ALU.mult),
                    wait=[(bd.MM, mmn)], inc=bd.VE)
                ph.on("vector", lambda e, i=i, j=j, sl=sl: e.tensor_mul(
                    h1[:, i, sl], ea[j][:, :], eb[j][:, :]),
                    wait=[(bd.SE, bd.SE.n), (bd.VE, bd.VE.n)], inc=bd.VE)
                ev_marks.append((bd.SE.n, bd.VE.n))
        h_done = (bd.VE, bd.VE.n)

        for (wTn, scn, srcb, dstb) in (("h1T", "sh1c", h1, h2), ("h2T", "sh2c", h2, h1)):
            ph.barrier()
            gd_marks = []
            wf_base = bd.WF.n
            for i in range(32):
                w = [(bd.WF, wf_base + (i - 1))] if i >= 2 else []
                ph.on("gpsimd", lambda e, i=i, wTn=wTn: e.dma_start(
                    out=_full(wh[i % 2]),
                    in_=io[wTn][:, i * 128:(i + 1) * 128].rearrange("(a p) o -> p a o", p=128)),
                    wait=w, inc=bd.GD, inc_k=16)
                gd_marks.append(bd.GD.n)
            se_marks = []
            for i in range(32):
                for nt in range(2):
                    j = nt
                    sl = slice(nt * 512, (nt + 1) * 512)
                    wq = [(bd.GD, gd_marks[i]), h_done]
                    idx = len(se_marks) - 2
                    if idx >= 0:
                        wq.append((bd.SE, se_marks[idx]))
                    for k in range(32):
                        last = (k == 31)
                        ph.on("tensor", lambda e, i=i, k=k, sl=sl, j=j, srcb=srcb: e.matmul(
                            psA[j][:, :], wh[i % 2][:, k, :], srcb[:, k, sl],
                            start=(k == 0), stop=(k == 31)),
                            wait=wq if k == 0 else [], inc=(bd.MM if last else None))
                    mmn = bd.MM.n
                    if nt == 1:
                        ph.on("tensor", lambda e: e.nop(), wait=[(bd.MM, mmn)], inc=bd.WF)
                    ph.on("scalar", lambda e, i=i, j=j, sl=sl, scn=scn, dstb=dstb: e.activation(
                        dstb[:, i, sl], psA[j][:, :], AF.Gelu, scale=sb[scn][:, i:i + 1]),
                        wait=[(bd.MM, mmn)], inc=bd.SE)
                    se_marks.append(bd.SE.n)
            h_done = (bd.SE, bd.SE.n)

        ph.barrier()
        gd_marks = []
        wf_base = bd.WF.n
        for i in range(8):
            w = [(bd.WF, wf_base + (i - 1))] if i >= 2 else []
            ph.on("gpsimd", lambda e, i=i: e.dma_start(
                out=_full(wh[i % 2]),
                in_=io["w3T"][:, i * 128:(i + 1) * 128].rearrange("(a p) o -> p a o", p=128)),
                wait=w, inc=bd.GD, inc_k=16)
            gd_marks.append(bd.GD.n)
        for i in range(8):
            st_prev = bd.AD.n
            for nt in range(2):
                j = nt
                sl = slice(nt * 512, (nt + 1) * 512)
                wq = [(bd.GD, gd_marks[i]), h_done]
                if i > 0:
                    wq.append((bd.AD, st_prev))
                for k in range(32):
                    last = (k == 31)
                    ph.on("tensor", lambda e, i=i, k=k, sl=sl, j=j: e.matmul(
                        psA[j][:, :], wh[i % 2][:, k, :], h1[:, k, sl],
                        start=(k == 0), stop=(k == 31)),
                        wait=wq if k == 0 else [], inc=(bd.MM if last else None))
                mmn = bd.MM.n
                if nt == 1:
                    ph.on("tensor", lambda e: e.nop(), wait=[(bd.MM, mmn)], inc=bd.WF)
                ph.on("scalar", lambda e, i=i, j=j, sl=sl: e.activation(
                    go[:, sl], psA[j][:, :], AF.Silu, scale=sb["s3c"][:, i:i + 1]),
                    wait=[(bd.MM, mmn)] + ([(bd.AD, st_prev)] if i > 0 else []), inc=bd.SE)
            ph.on("sync", lambda e, i=i, p=p: e.dma_start(
                out=io["g0"][i * 128:(i + 1) * 128, p * TP:(p + 1) * TP], in_=go[:, :]),
                wait=[(bd.SE, bd.SE.n)], inc=bd.AD, inc_k=16)
    ph.emit()
    st2.close()
    bd.ev["g0"] = [(bd.AD, bd.AD.n)]


def emit_proj_out(bd, y_name, wT_name, n_f, bias_tile_name, s_sb):
    nc, io, sb = bd.nc, bd.io, bd.sb

    def ysrc(fc):
        if y_name == "y0in":
            return io[f"y0outp{fc % 4}"], (fc // 4) * 128
        if fc < 8:
            return io[f"y1outp{fc // 2}"], (fc % 2) * 128
        return io[f"y1outp{(fc - 8) // 2}"], 256 + ((fc - 8) % 2) * 128

    ph = Ph(bd)
    yw = []
    for fc in range(n_f):
        t, _ = ysrc(fc)
        yw += bd.ev.get(t.name, [])
    ph.barrier(yw + bd.snapshot())
    st2 = ExitStack()
    zt = bd.tag()
    yb = st2.enter_context(nc.sbuf_tensor(zt + "po_y", [128, n_f, TM], BF16))
    wt = [st2.enter_context(nc.sbuf_tensor(zt + f"po_w{j}", [128, n_f, 128], BF16)) for j in range(2)]
    ps = [st2.enter_context(nc.psum_tensor(zt + f"po_ps{j}", [128, 512], F32)) for j in range(2)]

    def yload(e, fc):
        t, rb = ysrc(fc)
        with e.register(zt + f"ro{fc}") as r:
            e.reg_load(r, sb["coloff"][0:1, 0:1])
            e.reg_alu(r, r, rb * TF, ALU.add)
            return e.dma_start(out=yb[:, fc, :], in_=bass.AP(t, r, [[TF, 128], [1, TM]]))

    for fc in range(n_f):
        ph.on("sync", lambda e, fc=fc: yload(e, fc), inc=bd.AD, inc_k=16)
    y_ready = (bd.AD, bd.AD.n)
    gd_marks = []
    wf_base = bd.WF.n
    for i in range(8):
        w = [(bd.WF, wf_base + (i - 1))] if i >= 2 else []
        ph.on("gpsimd", lambda e, i=i: e.dma_start(
            out=_full(wt[i % 2]),
            in_=io[wT_name][:, i * 128:(i + 1) * 128].rearrange("(a p) o -> p a o", p=128)),
            wait=w, inc=bd.GD, inc_k=16)
        gd_marks.append(bd.GD.n)
    ev_marks = []
    for i in range(8):
        for nt in range(4):
            j = nt % 2
            sl = slice(nt * 512, (nt + 1) * 512)
            wq = [(bd.GD, gd_marks[i]), y_ready]
            idx = len(ev_marks) - 2
            if idx >= 0:
                wq.append((bd.SE, ev_marks[idx]))
            for k in range(n_f):
                last = (k == n_f - 1)
                ph.on("tensor", lambda e, i=i, k=k, sl=sl, j=j: e.matmul(
                    ps[j][:, :], wt[i % 2][:, k, :], yb[:, k, sl],
                    start=(k == 0), stop=(k == n_f - 1)),
                    wait=wq if k == 0 else [], inc=(bd.MM if last else None))
            mmn = bd.MM.n
            if nt == 3:
                ph.on("tensor", lambda e: e.nop(), wait=[(bd.MM, mmn)], inc=bd.WF)
            if bias_tile_name is not None:
                ph.on("vector", lambda e, i=i, j=j, sl=sl: e.tensor_scalar(
                    s_sb[:, i, sl], ps[j][:, :], sb[bias_tile_name][:, i:i + 1], None, ALU.add),
                    wait=[(bd.MM, mmn)], inc=bd.SE)
            else:
                ph.on("scalar", lambda e, i=i, j=j, sl=sl: e.activation(
                    s_sb[:, i, sl], ps[j][:, :], AF.Copy),
                    wait=[(bd.MM, mmn)], inc=bd.SE)
            ev_marks.append(bd.SE.n)
    ph.emit()
    st2.close()
    return (bd.SE, bd.SE.n)


A0_DEBUG = False


def emit_attn0(bd):
    nc, io, sb = bd.nc, bd.io, bd.sb
    dbg = {}
    if A0_DEBUG:
        for nm, shp, dt in (("dbg_qf", [128, TF], BF16), ("dbg_kf", [128, TF], BF16),
                            ("dbg_vt", [128, 32 * 130], BF16), ("dbg_qp1", [64, TF], BF16),
                            ("dbg_kpt", [128, 32 * 64], BF16), ("dbg_kv", [64, 65], BF16),
                            ("dbg_nm", [128, 8], F32), ("dbg_bq2", [128, 32], F32), ("dbg_qmx", [128, 32], F32), ("dbg_qsd", [128, 32], F32), ("dbg_qd", [128, 32 * 64], F32)):
            dbg[nm] = nc.dram_tensor(nm, shp, dt, kind="ExternalOutput")
    ph = Ph(bd)
    ph.barrier(bd.ev.get("ag0out", []) + bd.snapshot())
    st2 = ExitStack()
    xnf = st2.enter_context(nc.sbuf_tensor("a0_xn", [128, 8, TF], BF16))
    qf = st2.enter_context(nc.sbuf_tensor("a0_qf", [128, TF], BF16))
    kf = st2.enter_context(nc.sbuf_tensor("a0_kf", [128, TF], BF16))
    vt = st2.enter_context(nc.sbuf_tensor("a0_vt", [128, 32, 130], BF16))
    wq = st2.enter_context(nc.sbuf_tensor("a0_wq", [128, 8, 128], BF16))
    wk = st2.enter_context(nc.sbuf_tensor("a0_wk", [128, 8, 128], BF16))
    wv = st2.enter_context(nc.sbuf_tensor("a0_wv", [128, 8, 128], BF16))
    sq2 = st2.enter_context(nc.sbuf_tensor("a0_sq2", [128, 1024], BF16))
    nm = st2.enter_context(nc.sbuf_tensor("a0_nm", [128, 8], F32))
    kd = st2.enter_context(nc.sbuf_tensor("a0_kd", [128, 32, 64], F32))
    qd = st2.enter_context(nc.sbuf_tensor("a0_qd", [128, 32, 64], F32))
    ksd = st2.enter_context(nc.sbuf_tensor("a0_ksd", [128, 36], F32))
    qsd = st2.enter_context(nc.sbuf_tensor("a0_qsd", [128, 32], F32))
    qmx = st2.enter_context(nc.sbuf_tensor("a0_qmx", [128, 32], F32))
    kmx = st2.enter_context(nc.sbuf_tensor("a0_kmx", [128, 32], F32))
    bq2 = st2.enter_context(nc.sbuf_tensor("a0_bq2", [128, 32], F32))
    bk2 = st2.enter_context(nc.sbuf_tensor("a0_bk2", [128, 32], F32))
    qkvs = st2.enter_context(nc.sbuf_tensor("a0_qkvs", [128, 32, 65], F32))
    tscr = st2.enter_context(nc.sbuf_tensor("a0_tscr", [128, 128], F32))
    grow = st2.enter_context(nc.sbuf_tensor("a0_grow", [1, 2], F32))
    td2 = [st2.enter_context(nc.sbuf_tensor(f"a0_td{j}", [128, 64], BF16)) for j in range(2)]
    qp1 = st2.enter_context(nc.sbuf_tensor("a0_qp1", [64, TF], BF16))
    kpt = st2.enter_context(nc.sbuf_tensor("a0_kpt", [128, 32, 64], BF16))
    kvsb = st2.enter_context(nc.sbuf_tensor("a0_kv", [64, 65], BF16))
    bvb = st2.enter_context(nc.sbuf_tensor("a0_bvb", [128, 128], F32))
    yf = st2.enter_context(nc.sbuf_tensor("a0_yf", [128, TF], BF16))

    ps1 = st2.enter_context(nc.psum_tensor("a0_ps1", [128, 512], F32))
    ps2 = st2.enter_context(nc.psum_tensor("a0_ps2", [128, 512], F32))
    psg = st2.enter_context(nc.psum_tensor("a0_psg", [128, 128], F32))
    psd = st2.enter_context(nc.psum_tensor("a0_psd", [128, 512], F32))
    psk = st2.enter_context(nc.psum_tensor("a0_psk", [128, 80], F32))
    pst = st2.enter_context(nc.psum_tensor("a0_pst", [128, 128], BF16))

    for k in range(8):
        ph.on("sync", lambda e, k=k: e.dma_start(
            out=xnf[:, k, 0:TM], in_=io["ag0out"][k * 128:(k + 1) * 128, :]),
            inc=bd.AD, inc_k=16)
        ph.on("sync", lambda e, k=k: e.dma_start(
            out=xnf[:, k, TM:TF], in_=io["ag0out"][D + k * 128:D + (k + 1) * 128, :]),
            inc=bd.AD, inc_k=16)
    xn_ready = (bd.AD, bd.AD.n)

    for hp in range(4):
        ph.barrier()
        for wn, wsb in (("l0_wqT", wq), ("l0_wkT", wk), ("l0_wvT", wv)):
            ph.on("gpsimd", lambda e, wn=wn, wsb=wsb, hp=hp: e.dma_start(
                out=_full(wsb),
                in_=io[wn][:, hp * 128:(hp + 1) * 128].rearrange("(a p) o -> p a o", p=128)),
                inc=bd.GD, inc_k=16)
        w_ready = (bd.GD, bd.GD.n)
        ph.on("vector", lambda e: e.memset(vt[:, :, 64:65], 1.0), inc=bd.VE)
        ph.on("vector", lambda e: e.memset(vt[:, :, 129:130], 1.0), inc=bd.VE)
        ph.on("tensor", lambda e, hp=hp: e.matmul(
            psg[:, :], sb["ones_f"][0:1, 0:128], sb["l0_bvr"][0:1, hp * 128:(hp + 1) * 128],
            start=True, stop=True), inc=bd.MM)
        ph.on("vector", lambda e: e.tensor_copy(bvb[:, :], psg[:, :]), wait=[(bd.MM, bd.MM.n)], inc=bd.VE)

        for (wsb, dst, bcoln) in ((wq, qf, "l0_bq"), (wk, kf, "l0_bk")):
            for nt in range(8):
                sl = slice(nt * 512, (nt + 1) * 512)
                pj = ps1 if nt % 2 == 0 else ps2
                wt = [w_ready, xn_ready, (bd.SE, bd.SE.n)]
                for k in range(8):
                    ph.on("tensor", lambda e, k=k, sl=sl, pj=pj, wsb=wsb: e.matmul(
                        pj[:, :], wsb[:, k, :], xnf[:, k, sl],
                        start=(k == 0), stop=(k == 7)),
                        wait=wt if k == 0 else [], inc=(bd.MM if k == 7 else None))
                ph.on("vector", lambda e, sl=sl, pj=pj, dst=dst, bcoln=bcoln, hp=hp: e.tensor_scalar(
                    dst[:, sl], pj[:, :], sb[bcoln][:, hp:hp + 1], None, ALU.add),
                    wait=[(bd.MM, bd.MM.n)], inc=bd.SE)
        for c in range(32):
            wt = [w_ready, xn_ready, (bd.VE, bd.VE.n)]
            for k in range(8):
                ph.on("tensor", lambda e, c=c, k=k: e.matmul(
                    psg[:, :], xnf[:, k, c * 128:(c + 1) * 128], wv[:, k, :],
                    start=(k == 0), stop=(k == 7)),
                    wait=wt if k == 0 else [], inc=(bd.MM if k == 7 else None))
            mmn = bd.MM.n
            ph.on("vector", lambda e, c=c: e.tensor_tensor(
                vt[:, c, 0:64], psg[:, 0:64], bvb[:, 0:64], ALU.add),
                wait=[(bd.MM, mmn)], inc=None)
            ph.on("vector", lambda e, c=c: e.tensor_tensor(
                vt[:, c, 65:129], psg[:, 64:128], bvb[:, 64:128], ALU.add),
                inc=bd.VE)
        proj_done = bd.snapshot()

        for hl in range(2):
            rows = slice(hl * 64, (hl + 1) * 64)
            ph.barrier(proj_done)
            for (srcf, dstd, dsts, dstm) in ((qf, qd, qsd, qmx), (kf, kd, ksd, kmx)):
                for g in range(4):
                    ph.on("scalar", lambda e, g=g, rows=rows, srcf=srcf: e.activation(
                        sq2[rows, :], srcf[rows, g * 1024:(g + 1) * 1024], AF.Square),
                        wait=[(bd.MM, bd.MM.n)], inc=bd.SE)
                    sqd = bd.SE.n
                    for c8 in range(8):
                        c = g * 8 + c8
                        cs = slice(c * 128, (c + 1) * 128)
                        ph.on("tensor", lambda e, cs=cs, rows=rows, hp=hp, srcf=srcf, c8=c8: e.matmul(
                            psd[:, c8 * 64:(c8 + 1) * 64], srcf[rows, cs], sb["projT"][rows, hp, :],
                            start=(c8 == 0), stop=(c8 == 7), skip_group_check=True),
                            wait=([(bd.VE, bd.VE.n), (bd.SE, sqd)] if c8 == 0 else []), inc=None)
                    for c8 in range(8):
                        ph.on("tensor", lambda e, rows=rows, c8=c8: e.matmul(
                            psk[:, c8:c8 + 1], sq2[rows, c8 * 128:(c8 + 1) * 128],
                            sb["ones_b"][rows, 0:1],
                            start=(c8 == 0), stop=(c8 == 7), skip_group_check=True),
                            inc=(bd.MM if c8 == 7 else None))
                    mms = bd.MM.n
                    ph.on("vector", lambda e, g=g, dstd=dstd: e.tensor_copy(
                        dstd[:, g * 8:(g + 1) * 8, :].rearrange("p a b -> p (a b)"),
                        psd[:, 0:512]),
                        wait=[(bd.MM, mms)], inc=None)
                    ph.on("vector", lambda e, g=g, dsts=dsts: e.tensor_copy(
                        dsts[:, g * 8:(g + 1) * 8], psk[:, 0:8]), inc=None)
                    ph.on("vector", lambda e, g=g, dstm=dstm: e.tensor_reduce(
                        dstm[:, g * 8:(g + 1) * 8],
                        psd[:, 0:512].rearrange("p (a b) -> p a b", a=8), AX.X, ALU.max),
                        inc=bd.VE)
            ph.on("vector", lambda e: e.tensor_scalar(
                bq2[:, :], qmx[:, :], _c(-C0), _c(LN_RATIO), ALU.mult, ALU.add),
                wait=[(bd.VE, bd.VE.n)], inc=bd.VE)
            ph.on("vector", lambda e: e.scalar_tensor_tensor(
                bq2[:, :], qsd[:, :], _c(-0.5 * C0 * C0), bq2[:, :], ALU.mult, ALU.add),
                wait=[(bd.VE, bd.VE.n)], inc=bd.VE)
            bq_done = bd.VE.n
            ph.on("vector", lambda e: e.tensor_reduce(
                tscr[:, 0:1], kmx[:, :], AX.X, ALU.max),
                wait=[(bd.VE, bd.VE.n)], inc=bd.VE)
            ph.on("tensor", lambda e: e.transpose(
                psg[:, :], tscr[:, :], sb["identf"][:, :]),
                wait=[(bd.VE, bd.VE.n)], inc=bd.MM)
            ph.on("vector", lambda e: e.tensor_reduce(
                grow[0:1, 0:1], psg[0:1, 0:128], AX.X, ALU.max),
                wait=[(bd.MM, bd.MM.n)], inc=bd.VE)
            ph.on("tensor", lambda e: e.matmul(
                psk[:, 8:9], sb["ones_f"][0:1, 0:128], grow[0:1, 0:1], start=True, stop=True),
                wait=[(bd.VE, bd.VE.n)], inc=bd.MM)
            ph.on("vector", lambda e: e.tensor_scalar(
                nm[:, 5:6], psk[:, 8:9], _c(-C0), _c(LN_RATIO), ALU.mult, ALU.add),
                wait=[(bd.MM, bd.MM.n)], inc=bd.VE)
            ph.on("vector", lambda e: e.tensor_scalar(
                bk2[:, :], ksd[:, 0:32], _c(-0.5 * C0 * C0), None, ALU.mult),
                wait=[(bd.VE, bd.VE.n)], inc=bd.VE)
            ph.on("vector", lambda e: e.tensor_scalar(
                bk2[:, :], bk2[:, :], nm[:, 5:6], None, ALU.add),
                wait=[(bd.VE, bd.VE.n)], inc=bd.VE)
            bk_done = bd.VE.n
            pendt = None
            for c in range(32):
                cs = slice(c * 128, (c + 1) * 128)
                par = c % 2
                w = [(bd.VE, bq_done)]
                if c >= 2:
                    w.append((bd.MM, bd.MM.n))
                ph.on("scalar", lambda e, c=c, par=par: e.activation(
                    td2[par][:, 0:64], qd[:, c, :], AF.Exp, bias=bq2[:, c:c + 1], scale=_c(C0)),
                    wait=w, inc=bd.SE)
                sen = bd.SE.n
                if pendt is not None:
                    ph.on("tensor", lambda e, pendt=pendt: e.transpose(
                        pst[0:64, 0:128], td2[pendt[1]][0:128, 0:64], sb["ident"][:, :]),
                        wait=[(bd.SE, pendt[2]), (bd.VE, bd.VE.n)], inc=bd.MM)
                    ph.on("vector", lambda e, pendt=pendt: e.tensor_scalar(
                        qp1[0:64, pendt[0] * 128:(pendt[0] + 1) * 128], pst[0:64, 0:128],
                        _c(EPS_FEAT), None, ALU.add),
                        wait=[(bd.MM, bd.MM.n)], inc=bd.VE)
                pendt = (c, par, sen)
            ph.on("tensor", lambda e, pendt=pendt: e.transpose(
                pst[0:64, 0:128], td2[pendt[1]][0:128, 0:64], sb["ident"][:, :]),
                wait=[(bd.SE, pendt[2]), (bd.VE, bd.VE.n)], inc=bd.MM)
            ph.on("vector", lambda e, pendt=pendt: e.tensor_scalar(
                qp1[0:64, pendt[0] * 128:(pendt[0] + 1) * 128], pst[0:64, 0:128],
                _c(EPS_FEAT), None, ALU.add),
                wait=[(bd.MM, bd.MM.n)], inc=bd.VE)
            for c in range(32):
                par = c % 2
                w = [(bd.VE, bk_done)]
                if c < 2:
                    w.append((bd.MM, bd.MM.n))
                else:
                    w.append((bd.VE, bd.VE.n))
                ph.on("scalar", lambda e, c=c, par=par: e.activation(
                    td2[par][:, 0:64], kd[:, c, :], AF.Exp, bias=bk2[:, c:c + 1], scale=_c(C0)),
                    wait=w, inc=bd.SE)
                ph.on("vector", lambda e, c=c, par=par: e.tensor_scalar(
                    kpt[:, c, :], td2[par][:, 0:64], _c(EPS_FEAT), None, ALU.add),
                    wait=[(bd.SE, bd.SE.n)], inc=bd.VE)
            kv_w = [(bd.VE, bd.VE.n)]
            for c in range(32):
                ph.on("tensor", lambda e, c=c, hl=hl: e.matmul(
                    psk[0:64, 8:73], kpt[:, c, :], vt[:, c, hl * 65:hl * 65 + 65],
                    start=(c == 0), stop=(c == 31)),
                    wait=kv_w if c == 0 else [], inc=(bd.MM if c == 31 else None))
            ph.on("vector", lambda e: e.tensor_copy(kvsb[0:64, :], psk[0:64, 8:73]),
                  wait=[(bd.MM, bd.MM.n)], inc=bd.VE)
            kv_done = (bd.VE, bd.VE.n)
            for g in range(8):
                for c4 in range(4):
                    c = g * 4 + c4
                    cs = slice(c * 128, (c + 1) * 128)
                    ph.on("tensor", lambda e, cs=cs, c4=c4: e.matmul(
                        psd[:, c4 * 65:c4 * 65 + 65], qp1[0:64, cs], kvsb[0:64, 0:65],
                        start=(c4 == 0), stop=(c4 == 3), skip_group_check=True),
                        wait=([kv_done, (bd.VE, bd.VE.n)] if c4 == 0 else []),
                        inc=(bd.MM if c4 == 3 else None))
                ph.on("vector", lambda e, g=g: e.tensor_copy(
                    qkvs[:, g * 4:(g + 1) * 4, :].rearrange("p a b -> p (a b)"), psd[:, 0:260]),
                    wait=[(bd.MM, bd.MM.n)], inc=bd.VE)
            ph.on("vector", lambda e: e.tensor_scalar(
                qmx[:, :], qkvs[:, :, 64], _c(EPS_DEN), None, ALU.max),
                wait=[(bd.VE, bd.VE.n)], inc=bd.VE)
            ph.on("vector", lambda e: e.reciprocal(qmx[:, :], qmx[:, :]),
                  wait=[(bd.VE, bd.VE.n)], inc=bd.VE)
            r_done = bd.VE.n
            pendt = None
            for c in range(32):
                par = c % 2
                w = [(bd.VE, r_done)]
                if c >= 2:
                    w.append((bd.MM, bd.MM.n))
                ph.on("vector", lambda e, c=c, par=par: e.tensor_scalar(
                    td2[par][:, 0:64], qkvs[:, c, 0:64], qmx[:, c:c + 1], None, ALU.mult),
                    wait=w, inc=bd.VE)
                ven = bd.VE.n
                if pendt is not None:
                    ph.on("tensor", lambda e, pendt=pendt: e.transpose(
                        pst[0:64, 0:128], td2[pendt[1]][0:128, 0:64], sb["ident"][:, :]),
                        wait=[(bd.VE, pendt[2]), (bd.VE, bd.VE.n)], inc=bd.MM)
                    ph.on("vector", lambda e, pendt=pendt, rows=rows: e.tensor_copy(
                        yf[rows, pendt[0] * 128:(pendt[0] + 1) * 128], pst[0:64, 0:128]),
                        wait=[(bd.MM, bd.MM.n)], inc=bd.VE)
                pendt = (c, par, ven)
            ph.on("tensor", lambda e, pendt=pendt: e.transpose(
                pst[0:64, 0:128], td2[pendt[1]][0:128, 0:64], sb["ident"][:, :]),
                wait=[(bd.VE, pendt[2]), (bd.VE, bd.VE.n)], inc=bd.MM)
            ph.on("vector", lambda e, pendt=pendt, rows=rows: e.tensor_copy(
                yf[rows, pendt[0] * 128:(pendt[0] + 1) * 128], pst[0:64, 0:128]),
                wait=[(bd.MM, bd.MM.n)], inc=bd.VE)
        ph.on("sync", lambda e, hp=hp: e.dma_start(
            out=io["y0in"][hp * 128:(hp + 1) * 128, :], in_=yf[:, :]),
            wait=[(bd.VE, bd.VE.n)], inc=bd.AD, inc_k=16)
        bd.ev[f"y0p{hp}"] = [(bd.AD, bd.AD.n)]
        if A0_DEBUG and hp == 0:
            ph.barrier()
            for dnm, src in (("dbg_qf", qf[:, :]), ("dbg_kf", kf[:, :]),
                             ("dbg_vt", vt[:, :, :].rearrange("p a b -> p (a b)")),
                             ("dbg_qp1", qp1[0:64, :]),
                             ("dbg_kpt", kpt[:, :, :].rearrange("p a b -> p (a b)")),
                             ("dbg_kv", kvsb[0:64, :]), ("dbg_nm", nm[:, :]), ("dbg_bq2", bq2[:, :]), ("dbg_qmx", qmx[:, :]), ("dbg_qsd", qsd[:, :]), ("dbg_qd", qd[:, :, :].rearrange("p a b -> p (a b)"))):
                ph.on("sync", lambda e, nm2=dnm, src=src: e.dma_start(
                    out=_full(dbg[nm2]), in_=src), inc=bd.AD, inc_k=16)
            ph.barrier()
    ph.emit()
    st2.close()
    bd.ev["y0in"] = [(bd.AD, bd.AD.n)]


MO_DEBUG = False


def emit_moe(bd):
    nc, io, sb = bd.nc, bd.io, bd.sb
    dbg = {}
    if MO_DEBUG:
        for nm2, shp, dt in (("dbgm_idxrow", [1, TM], BF16), ("dbgm_idxt", [128, 16], BF16),
                             ("dbgm_sel", [128, TM], BF16), ("dbgm_logits", [128, 128], BF16),
                             ("dbgm_xe", [128, TM], BF16), ("dbgm_macc", [128, TM], BF16)):
            dbg[nm2] = nc.dram_tensor(nm2, shp, dt, kind="ExternalOutput")
    ph = Ph(bd)
    ph.barrier(bd.ev.get("ag1in", []) + bd.snapshot())
    st2 = ExitStack()
    xnb = [st2.enter_context(nc.sbuf_tensor(f"mo_xn{j}", [128, TM], BF16)) for j in range(2)]
    rt = st2.enter_context(nc.sbuf_tensor("mo_rt", [128, 8, 8], BF16))
    idxt = st2.enter_context(nc.sbuf_tensor("mo_idxt", [128, 16], BF16))
    idxs = st2.enter_context(nc.sbuf_tensor("mo_idxs", [16, 128], BF16))
    idxrow = st2.enter_context(nc.sbuf_tensor("mo_idxrow", [1, TM], BF16))
    mcol = st2.enter_context(nc.sbuf_tensor("mo_mcol", [128, 16], F32))
    ge = st2.enter_context(nc.sbuf_tensor("mo_ge", [128, 128], F32))
    sel = st2.enter_context(nc.sbuf_tensor("mo_sel", [128, TM], BF16))
    xe = st2.enter_context(nc.sbuf_tensor("mo_xe", [128, 8, TM], BF16))
    msb = st2.enter_context(nc.sbuf_tensor("mo_m", [128, 22, TM], BF16))
    g1a = st2.enter_context(nc.sbuf_tensor("mo_g1", [128, 8, TM], BF16))
    eg = [st2.enter_context(nc.sbuf_tensor(f"mo_eg{j}", [128, 512], BF16)) for j in range(2)]
    eu = [st2.enter_context(nc.sbuf_tensor(f"mo_eu{j}", [128, 512], BF16)) for j in range(2)]
    wg = [st2.enter_context(nc.sbuf_tensor(f"mo_wg{j}", [128, 8, 128], BF16)) for j in range(2)]
    wu = [st2.enter_context(nc.sbuf_tensor(f"mo_wu{j}", [128, 8, 128], BF16)) for j in range(2)]
    wd = [st2.enter_context(nc.sbuf_tensor(f"mo_wd{j}", [128, 22, 128], BF16)) for j in range(2)]
    psr = st2.enter_context(nc.psum_tensor("mo_psr", [128, 128], F32))
    pss = st2.enter_context(nc.psum_tensor("mo_pss", [128, 512], F32))
    pstb = st2.enter_context(nc.psum_tensor("mo_pstb", [128, 128], BF16))
    psA = [st2.enter_context(nc.psum_tensor(f"mo_psA{j}", [128, 512], F32)) for j in range(2)]
    psB = [st2.enter_context(nc.psum_tensor(f"mo_psB{j}", [128, 512], F32)) for j in range(2)]

    ph.on("gpsimd", lambda e: e.dma_start(
        out=_full(rt), in_=io["routerT"].rearrange("(a p) o -> p a o", p=128)),
        inc=bd.GD, inc_k=16)
    rt_ready = (bd.GD, bd.GD.n)
    for k in range(8):
        ph.on("sync", lambda e, k=k: e.dma_start(
            out=xe[:, k, :], in_=io["ag1in"][k * 128:(k + 1) * 128, :]),
            inc=bd.AD, inc_k=16)
    xn_r = (bd.AD, bd.AD.n)
    for nch in range(16):
        for k in range(8):
            ph.on("tensor", lambda e, k=k, nch=nch: e.matmul(
                psr[:, nch * 8:(nch + 1) * 8], xe[:, k, nch * 128:(nch + 1) * 128],
                rt[:, k, :], start=(k == 0), stop=(k == 7),
                skip_group_check=True),
                wait=([rt_ready, xn_r] if (nch == 0 and k == 0) else []),
                inc=(bd.MM if (nch == 15 and k == 7) else None))
    mm_r = bd.MM.n
    for nch in range(16):
        ph.on("vector", lambda e, nch=nch: e.tensor_reduce(
            mcol[:, nch:nch + 1], psr[:, nch * 8:(nch + 1) * 8], AX.X, ALU.max),
            wait=[(bd.MM, mm_r)], inc=(bd.VE if nch == 15 else None))
    for nch in range(16):
        ph.on("vector", lambda e, nch=nch: e.tensor_scalar(
            ge[:, nch * 8:(nch + 1) * 8], psr[:, nch * 8:(nch + 1) * 8],
            mcol[:, nch:nch + 1], None, ALU.is_ge),
            wait=[(bd.VE, bd.VE.n)], inc=(bd.VE if nch == 15 else None))
    ph.on("vector", lambda e: e.tensor_mul(ge[:, :], ge[:, :], sb["iotam"][:, :]),
          wait=[(bd.VE, bd.VE.n)], inc=bd.VE)
    ph.on("vector", lambda e: e.tensor_scalar(ge[:, :], ge[:, :], _c(BIGI), None, ALU.add),
          wait=[(bd.VE, bd.VE.n)], inc=bd.VE)
    for nch in range(16):
        ph.on("vector", lambda e, nch=nch: e.tensor_reduce(
            idxt[:, nch:nch + 1], ge[:, nch * 8:(nch + 1) * 8], AX.X, ALU.min),
            wait=[(bd.VE, bd.VE.n)], inc=(bd.VE if nch == 15 else None))
    ph.on("tensor", lambda e: e.transpose(pstb[0:16, 0:128], idxt[:, :], sb["ident"][:, :]),
          wait=[(bd.VE, bd.VE.n)], inc=bd.MM)
    ph.on("vector", lambda e: e.tensor_copy(idxs[:, :], pstb[0:16, 0:128]),
          wait=[(bd.MM, bd.MM.n)], inc=bd.VE)
    ph.on("sync", lambda e: e.dma_start(
        out=io["idxd"][0:1, :].rearrange("a (p f) -> (a p) f", p=16), in_=idxs[:, :]),
        wait=[(bd.VE, bd.VE.n)], inc=bd.AD, inc_k=16)
    ph.on("sync", lambda e: e.dma_start(out=idxrow[0:1, :], in_=io["idxd"][0:1, :]),
          wait=[(bd.AD, bd.AD.n)], inc=bd.AD, inc_k=16)
    idx_ready = (bd.AD, bd.AD.n)
    if MO_DEBUG:
        ph.barrier()
        ph.on("vector", lambda e: e.tensor_copy(sel[:, 0:128], psr[:, :]), inc=bd.VE)
        ph.barrier()
        for nm2, srcap in (("dbgm_idxrow", idxrow[0:1, :]), ("dbgm_idxt", idxt[:, :]),
                           ("dbgm_logits", sel[:, 0:128])):
            ph.on("sync", lambda e, nm2=nm2, srcap=srcap: e.dma_start(
                out=_full(dbg[nm2]), in_=srcap), inc=bd.AD, inc_k=16)
        ph.barrier()

    for ex in range(8):
        ph.barrier()
        for nt in range(4):
            ph.on("tensor", lambda e, nt=nt: e.matmul(
                pss[:, :], sb["ones_b"][0:1, 0:128],
                idxrow[0:1, nt * 512:(nt + 1) * 512], start=True, stop=True),
                wait=[idx_ready, (bd.VE, bd.VE.n)], inc=bd.MM)
            ph.on("vector", lambda e, nt=nt, ex=ex: e.tensor_scalar(
                sel[:, nt * 512:(nt + 1) * 512], pss[:, :], float(ex), None, ALU.is_equal),
                wait=[(bd.MM, bd.MM.n)], inc=bd.VE)
        for k in range(8):
            ph.on("sync", lambda e, k=k: e.dma_start(
                out=xe[:, k, :], in_=io["ag1in"][k * 128:(k + 1) * 128, :]),
                wait=[(bd.MM, bd.MM.n)], inc=bd.AD, inc_k=16)
            ph.on("vector", lambda e, k=k: e.tensor_mul(xe[:, k, :], xe[:, k, :], sel[:, :]),
                  wait=[(bd.AD, bd.AD.n), (bd.VE, bd.VE.n)], inc=bd.VE)
        xe_ready = (bd.VE, bd.VE.n)
        if MO_DEBUG and ex == 2:
            ph.barrier()
            ph.on("sync", lambda e: e.dma_start(out=_full(dbg["dbgm_sel"]), in_=sel[:, :]),
                  inc=bd.AD, inc_k=16)
            ph.on("sync", lambda e: e.dma_start(out=_full(dbg["dbgm_xe"]), in_=xe[:, 0, :]),
                  inc=bd.AD, inc_k=16)
            ph.barrier()
        gd_marks = []
        wf_base = bd.WF.n
        for i in range(22):
            w = [(bd.WF, wf_base + 2 * (i - 1))] if i >= 2 else []
            ph.on("gpsimd", lambda e, i=i, ex=ex: e.dma_start(
                out=_full(wg[i % 2]),
                in_=io["gateT"][ex, :, i * 128:(i + 1) * 128].rearrange("(a p) o -> p a o", p=128)),
                wait=w, inc=bd.GD, inc_k=16)
            ph.on("gpsimd", lambda e, i=i, ex=ex: e.dma_start(
                out=_full(wu[i % 2]),
                in_=io["upT"][ex, :, i * 128:(i + 1) * 128].rearrange("(a p) o -> p a o", p=128)),
                inc=bd.GD, inc_k=16)
            gd_marks.append(bd.GD.n)
        ev_marks = []
        for i in range(22):
            for nt in range(4):
                j = nt % 2
                sl = slice(nt * 512, (nt + 1) * 512)
                wq = [(bd.GD, gd_marks[i]), xe_ready]
                idx = len(ev_marks) - 2
                if idx >= 0:
                    wq.append((bd.SE, ev_marks[idx][0]))
                    wq.append((bd.VE, ev_marks[idx][1]))
                for k in range(8):
                    ph.on("tensor", lambda e, i=i, k=k, sl=sl, j=j: e.matmul(
                        psA[j][:, :], wg[i % 2][:, k, :], xe[:, k, sl],
                        start=(k == 0), stop=(k == 7)),
                        wait=wq if k == 0 else [], inc=None)
                for k in range(8):
                    last = (k == 7)
                    ph.on("tensor", lambda e, i=i, k=k, sl=sl, j=j: e.matmul(
                        psB[j][:, :], wu[i % 2][:, k, :], xe[:, k, sl],
                        start=(k == 0), stop=(k == 7)),
                        inc=(bd.MM if last else None))
                mmn = bd.MM.n
                if nt == 3:
                    ph.on("tensor", lambda e: e.nop(), wait=[(bd.MM, mmn)], inc=bd.WF, inc_k=2)
                ph.on("scalar", lambda e, j=j: e.activation(
                    eg[j][:, :], psA[j][:, :], AF.Silu),
                    wait=[(bd.MM, mmn)], inc=bd.SE)
                ph.on("vector", lambda e, j=j: e.tensor_copy(eu[j][:, :], psB[j][:, :]),
                      wait=[(bd.MM, mmn)], inc=bd.VE)
                ph.on("vector", lambda e, i=i, j=j, sl=sl: e.tensor_mul(
                    msb[:, i, sl], eg[j][:, :], eu[j][:, :]),
                    wait=[(bd.SE, bd.SE.n), (bd.VE, bd.VE.n)], inc=bd.VE)
                ev_marks.append((bd.SE.n, bd.VE.n))
        m_ready = (bd.VE, bd.VE.n)
        gd_marks = []
        wf_base = bd.WF.n
        for i in range(8):
            w = [(bd.WF, wf_base + (i - 1))] if i >= 2 else []
            ph.on("gpsimd", lambda e, i=i, ex=ex: e.dma_start(
                out=_full(wd[i % 2]),
                in_=io["downT"][ex, :, i * 128:(i + 1) * 128].rearrange("(a p) o -> p a o", p=128)),
                wait=w, inc=bd.GD, inc_k=16)
            gd_marks.append(bd.GD.n)
        ve_marks = []
        for i in range(8):
            for nt in range(4):
                j = nt % 2
                sl = slice(nt * 512, (nt + 1) * 512)
                wq = [(bd.GD, gd_marks[i]), m_ready]
                idx = len(ve_marks) - 2
                if idx >= 0:
                    wq.append((bd.VE, ve_marks[idx]))
                for k in range(22):
                    last = (k == 21)
                    ph.on("tensor", lambda e, i=i, k=k, sl=sl, j=j: e.matmul(
                        psA[j][:, :], wd[i % 2][:, k, :], msb[:, k, sl],
                        start=(k == 0), stop=(k == 21)),
                        wait=wq if k == 0 else [], inc=(bd.MM if last else None))
                mmn = bd.MM.n
                if nt == 3:
                    ph.on("tensor", lambda e: e.nop(), wait=[(bd.MM, mmn)], inc=bd.WF)
                if ex == 0:
                    ph.on("vector", lambda e, i=i, sl=sl, j=j: e.tensor_copy(
                        g1a[:, i, sl], psA[j][:, :]),
                        wait=[(bd.MM, mmn)], inc=bd.VE)
                else:
                    ph.on("vector", lambda e, i=i, sl=sl, j=j: e.tensor_add(
                        g1a[:, i, sl], g1a[:, i, sl], psA[j][:, :]),
                        wait=[(bd.MM, mmn)], inc=bd.VE)
                ve_marks.append(bd.VE.n)
    ph.barrier()
    for i in range(8):
        ph.on("scalar", lambda e, i=i: e.activation(g1a[:, i, :], g1a[:, i, :], AF.Silu),
              inc=bd.SE)
        ph.on("sync", lambda e, i=i: e.dma_start(
            out=io["g1"][i * 128:(i + 1) * 128, :], in_=g1a[:, i, :]),
            wait=[(bd.SE, bd.SE.n)], inc=bd.AD, inc_k=16)
    ph.emit()
    st2.close()
    bd.ev["g1"] = [(bd.AD, bd.AD.n)]


def emit_attn1(bd):
    nc, io, sb = bd.nc, bd.io, bd.sb
    ph = Ph(bd)
    ph.barrier(bd.ev.get("ag1out", []) + bd.snapshot())
    st2 = ExitStack()
    xnf = st2.enter_context(nc.sbuf_tensor("a1_xn", [128, 8, TF], BF16))
    qk = {}
    for nmx in ("q1", "q2", "k1", "k2"):
        qk[nmx] = st2.enter_context(nc.sbuf_tensor(f"a1_{nmx}", [128, TF], BF16))
    vt = st2.enter_context(nc.sbuf_tensor("a1_vt", [128, 32, 260], BF16))
    wt = st2.enter_context(nc.sbuf_tensor("a1_wt", [128, 8, 128], BF16))
    wv = st2.enter_context(nc.sbuf_tensor("a1_wv", [128, 8, 256], BF16))
    sqv = st2.enter_context(nc.sbuf_tensor("a1_sq", [128, 512], F32))
    sqb = st2.enter_context(nc.sbuf_tensor("a1_sqb", [128, TF], BF16))
    nrm = st2.enter_context(nc.sbuf_tensor("a1_nrm", [1, 16], F32))
    ccol = st2.enter_context(nc.sbuf_tensor("a1_ccol", [128, 4], F32))
    lcol = st2.enter_context(nc.sbuf_tensor("a1_lcol", [128, 2], F32))
    ptb = [st2.enter_context(nc.sbuf_tensor(f"a1_ptb{j}", [128, 512], BF16)) for j in range(3)]
    smm = [st2.enter_context(nc.sbuf_tensor(f"a1_smm{j}", [128, 512], F32)) for j in range(3)]
    evs = st2.enter_context(nc.sbuf_tensor("a1_evs", [128, 48], F32))
    yc = [st2.enter_context(nc.sbuf_tensor(f"a1_yc{q}", [128, 128], F32)) for q in range(4)]
    yc2a = st2.enter_context(nc.sbuf_tensor("a1_yc2a", [128, 512], F32))
    ytda = st2.enter_context(nc.sbuf_tensor("a1_ytda", [128, TF], BF16))
    yfh = [st2.enter_context(nc.sbuf_tensor(f"a1_yf{hl}", [128, TF], BF16)) for hl in range(2)]
    psS = [st2.enter_context(nc.psum_tensor(f"a1_psS{j}", [128, 512], F32)) for j in range(3)]
    psY = [st2.enter_context(nc.psum_tensor(f"a1_psY{j}", [128, 512], F32)) for j in range(4)]
    
    psTb = st2.enter_context(nc.psum_tensor("a1_psTb", [128, 256], BF16))

    for k in range(8):
        ph.on("sync", lambda e, k=k: e.dma_start(
            out=xnf[:, k, 0:TM], in_=io["ag1out"][k * 128:(k + 1) * 128, :]),
            inc=bd.AD, inc_k=16)
        ph.on("sync", lambda e, k=k: e.dma_start(
            out=xnf[:, k, TM:TF], in_=io["ag1out"][D + k * 128:D + (k + 1) * 128, :]),
            inc=bd.AD, inc_k=16)
    xn_ready = (bd.AD, bd.AD.n)

    for hp in range(4):
        ph.barrier()
        for nmx, wn in (("q1", "wq1T"), ("q2", "wq2T"), ("k1", "wk1T"), ("k2", "wk2T")):
            ph.on("gpsimd", lambda e, wn=wn, hp=hp: e.dma_start(
                out=_full(wt),
                in_=io[wn][:, hp * 128:(hp + 1) * 128].rearrange("(a p) o -> p a o", p=128)),
                wait=[(bd.MM, bd.MM.n)], inc=bd.GD, inc_k=16)
            gdn = bd.GD.n
            for nt in range(8):
                sl = slice(nt * 512, (nt + 1) * 512)
                pj = psS[nt % 2]
                wq2 = [(bd.GD, gdn), xn_ready, (bd.VE, bd.VE.n)]
                for k in range(8):
                    ph.on("tensor", lambda e, k=k, sl=sl, pj=pj: e.matmul(
                        pj[:, :], wt[:, k, :], xnf[:, k, sl],
                        start=(k == 0), stop=(k == 7)),
                        wait=wq2 if k == 0 else [], inc=(bd.MM if k == 7 else None))
                ph.on("vector", lambda e, sl=sl, pj=pj, nmx=nmx: e.tensor_copy(
                    qk[nmx][:, sl], pj[:, :]),
                    wait=[(bd.MM, bd.MM.n)], inc=bd.VE)
        ph.on("gpsimd", lambda e, hp=hp: e.dma_start(
            out=_full(wv),
            in_=io["wvT"][:, hp * 256:(hp + 1) * 256].rearrange("(a p) o -> p a o", p=128)),
            wait=[(bd.MM, bd.MM.n)], inc=bd.GD, inc_k=16)
        gdv = bd.GD.n
        ph.on("vector", lambda e: e.memset(vt[:, :, 128:129], 1.0), inc=bd.VE)
        ph.on("vector", lambda e: e.memset(vt[:, :, 258:259], 1.0), inc=bd.VE)
        for c in range(32):
            pj = psS[c % 2]
            wq2 = [(bd.GD, gdv), xn_ready, (bd.VE, bd.VE.n)]
            for k in range(8):
                ph.on("tensor", lambda e, c=c, k=k, pj=pj: e.matmul(
                    pj[:, 0:256], xnf[:, k, c * 128:(c + 1) * 128], wv[:, k, :],
                    start=(k == 0), stop=(k == 7)),
                    wait=wq2 if k == 0 else [], inc=(bd.MM if k == 7 else None))
            mmn = bd.MM.n
            ph.on("vector", lambda e, c=c, pj=pj: e.tensor_copy(vt[:, c, 0:128], pj[:, 0:128]),
                  wait=[(bd.MM, mmn)], inc=None)
            ph.on("vector", lambda e, c=c, pj=pj: e.tensor_copy(vt[:, c, 130:258], pj[:, 128:256]),
                  inc=bd.VE)
        ph.barrier()

        for hl in range(2):
            rows = slice(hl * 64, (hl + 1) * 64)
            for si, pairnames in enumerate((("q1", "k1"), ("q2", "k2"))):
                for ti, tn in enumerate(pairnames):
                    slot = 4 * hl + 2 * si + ti
                    ph.on("scalar", lambda e, tn=tn, rows=rows: e.activation(
                        sqb[rows, :], qk[tn][rows, :], AF.Square),
                        wait=[(bd.MM, bd.MM.n), (bd.VE, bd.VE.n)], inc=bd.SE)
                    sq_done = bd.SE.n
                    for nt in range(8):
                        sl = slice(nt * 512, (nt + 1) * 512)
                        ph.on("tensor", lambda e, rows=rows, sl=sl: e.matmul(
                            psS[2][0:1, 0:512], sb["ones_b"][rows, 0:1], sqb[rows, sl],
                            start=True, stop=True),
                            wait=[(bd.SE, sq_done), (bd.VE, bd.VE.n)], inc=bd.MM)
                        tgt = nrm[0:1, slot:slot + 1] if nt == 0 else nrm[0:1, 15:16]
                        ph.on("vector", lambda e, tgt=tgt: e.tensor_reduce(
                            tgt, psS[2][0:1, 0:512], AX.X, ALU.max),
                            wait=[(bd.MM, bd.MM.n)], inc=bd.VE)
                        if nt > 0:
                            ph.on("vector", lambda e, slot=slot: e.tensor_max(
                                nrm[0:1, slot:slot + 1], nrm[0:1, slot:slot + 1], nrm[0:1, 15:16]),
                                wait=[(bd.VE, bd.VE.n)], inc=bd.VE)
            for si in range(2):
                qslot = 4 * hl + 2 * si
                ph.on("scalar", lambda e, qslot=qslot: e.activation(
                    nrm[0:1, qslot:qslot + 1], nrm[0:1, qslot:qslot + 1], AF.Sqrt),
                    wait=[(bd.VE, bd.VE.n)], inc=bd.SE)
                ph.on("scalar", lambda e, qslot=qslot: e.activation(
                    nrm[0:1, qslot + 1:qslot + 2], nrm[0:1, qslot + 1:qslot + 2], AF.Sqrt),
                    wait=[(bd.SE, bd.SE.n)], inc=bd.SE)
                ph.on("vector", lambda e, qslot=qslot: e.tensor_mul(
                    nrm[0:1, qslot:qslot + 1], nrm[0:1, qslot:qslot + 1],
                    nrm[0:1, qslot + 1:qslot + 2]),
                    wait=[(bd.SE, bd.SE.n)], inc=bd.VE)
                ph.on("vector", lambda e, qslot=qslot: e.tensor_scalar(
                    nrm[0:1, qslot:qslot + 1], nrm[0:1, qslot:qslot + 1],
                    _c(-0.125), None, ALU.mult),
                    wait=[(bd.VE, bd.VE.n)], inc=bd.VE)
                ph.on("tensor", lambda e, qslot=qslot: e.matmul(
                    psS[2][0:128, 0:1], sb["ones_f"][0:1, 0:128], nrm[0:1, qslot:qslot + 1],
                    start=True, stop=True),
                    wait=[(bd.VE, bd.VE.n)], inc=bd.MM)
                ph.on("vector", lambda e, hl=hl, si=si: e.tensor_copy(
                    ccol[:, 2 * hl + si:2 * hl + si + 1], psS[2][0:128, 0:1]),
                    wait=[(bd.MM, bd.MM.n)], inc=bd.VE)
            ph.on("tensor", lambda e, hl=hl, hp=hp: e.matmul(
                psS[2][0:128, 1:2], sb["ones_f"][0:1, 0:128],
                sb["lamrow"][0:1, hp * 2 + hl:hp * 2 + hl + 1],
                start=True, stop=True),
                wait=[(bd.VE, bd.VE.n)], inc=bd.MM)
            ph.on("vector", lambda e, hl=hl: e.tensor_copy(
                lcol[:, hl:hl + 1], psS[2][0:128, 1:2]),
                wait=[(bd.MM, bd.MM.n)], inc=bd.VE)
        ph.barrier()

        exp_mark = {0: 0, 1: 0, 2: 0}
        evac_free = 0
        for hl in range(2):
            rows = slice(hl * 64, (hl + 1) * 64)
            vsl = slice(hl * 130, hl * 130 + 129)
            tr_marks = []
            for W in range(8):
                for si, (qn, kn) in enumerate((("q1", "k1"), ("q2", "k2"))):
                    nkt = 4 * W + 4
                    pend = []

                    def emit_pv(kt, sen, nkt, si=si, vsl=vsl):
                        for qs in range(4):
                            bank = psY[2 * si + qs // 2]
                            off = (qs % 2) * 129
                            par = kt % 3
                            ph.on("tensor", lambda e, qs=qs, kt=kt, bank=bank, off=off, par=par, nkt=nkt: e.matmul(
                                bank[:, off:off + 129], ptb[par][:, qs * 128:(qs + 1) * 128],
                                vt[:, kt, vsl],
                                start=(kt == nkt - 1 and qs % 2 == 0), stop=(kt == 0),
                                skip_group_check=True),
                                wait=[(bd.SE, sen)] if qs == 0 else [],
                                inc=(bd.MM if (qs == 3 and kt == 0) else None))

                    for kt in list(range(nkt - 1, -1, -1)):
                        par = kt % 3
                        pj = psS[par]
                        wq2 = []
                        if exp_mark[par]:
                            wq2.append((bd.SE, exp_mark[par]))
                        if kt >= nkt - 3:
                            wq2.append((bd.VE, evac_free))
                            if kt == nkt - 1 and si == 0 and W == 0:
                                wq2.append((bd.VE, bd.VE.n))
                                wq2.append((bd.SE, bd.SE.n))
                        ph.on("tensor", lambda e, kt=kt, W=W, pj=pj, rows=rows, qn=qn, kn=kn: e.matmul(
                            pj[:, :], qk[kn][rows, kt * 128:(kt + 1) * 128],
                            qk[qn][rows, W * 512:(W + 1) * 512],
                            start=True, stop=True),
                            wait=wq2, inc=bd.MM)
                        mmn = bd.MM.n
                        dd = kt - 4 * W
                        if dd >= 0:
                            ph.on("vector", lambda e, dd=dd, pj=pj, par=par: e.tensor_tensor(
                                smm[par][:, :], pj[:, :], sb["masks"][:, dd, :], ALU.add),
                                wait=[(bd.MM, mmn)], inc=bd.VE)
                            ph.on("scalar", lambda e, hl=hl, si=si, par=par: e.activation(
                                ptb[par][:, :], smm[par][:, :], AF.Exp, scale=_c(0.125),
                                bias=ccol[:, 2 * hl + si:2 * hl + si + 1]),
                                wait=[(bd.VE, bd.VE.n)], inc=bd.SE)
                        else:
                            ph.on("scalar", lambda e, pj=pj, hl=hl, si=si, par=par: e.activation(
                                ptb[par][:, :], pj[:, :], AF.Exp, scale=_c(0.125),
                                bias=ccol[:, 2 * hl + si:2 * hl + si + 1]),
                                wait=[(bd.MM, mmn)], inc=bd.SE)
                        exp_mark[par] = bd.SE.n
                        pend.append((kt, bd.SE.n))
                        if len(pend) > 2:
                            k0, s0 = pend.pop(0)
                            emit_pv(k0, s0, nkt)
                    for (k0, s0) in pend:
                        emit_pv(k0, s0, nkt)
                pv_done = bd.MM.n
                for qs in range(4):
                    off = (qs % 2) * 129
                    ph.on("vector", lambda e, qs=qs, off=off: e.tensor_copy(
                        evs[:, qs:qs + 1], psY[qs // 2][:, off + 128:off + 129]),
                        wait=[(bd.MM, pv_done)], inc=None)
                    ph.on("vector", lambda e, qs=qs, off=off: e.tensor_copy(
                        evs[:, 4 + qs:5 + qs], psY[2 + qs // 2][:, off + 128:off + 129]),
                        inc=(bd.VE if qs == 3 else None))
                ph.on("vector", lambda e: e.tensor_scalar(
                    evs[:, 8:16], evs[:, 0:8], _c(1e-30), None, ALU.max),
                    wait=[(bd.VE, bd.VE.n)], inc=bd.VE)
                ph.on("vector", lambda e: e.reciprocal(evs[:, 8:16], evs[:, 8:16]),
                      wait=[(bd.VE, bd.VE.n)], inc=bd.VE)
                ph.on("vector", lambda e, hl=hl: e.tensor_scalar(
                    evs[:, 12:16], evs[:, 12:16], lcol[:, hl:hl + 1], None, ALU.mult),
                    wait=[(bd.VE, bd.VE.n)], inc=bd.VE)
                ph.on("vector", lambda e: e.tensor_scalar(
                    evs[:, 12:16], evs[:, 12:16], _c(-1.0), None, ALU.mult),
                    wait=[(bd.VE, bd.VE.n)], inc=bd.VE)
                rdy = bd.VE.n
                for qs in range(4):
                    off = (qs % 2) * 129
                    ph.on("vector", lambda e, qs=qs, off=off: e.tensor_scalar(
                        yc[qs][:, :], psY[qs // 2][:, off:off + 128], evs[:, 8 + qs:9 + qs], None, ALU.mult),
                        wait=[(bd.VE, rdy)], inc=(bd.VE if qs == 3 else None))
                ycd = bd.VE.n
                for qs in range(4):
                    off = (qs % 2) * 129
                    ph.on("vector", lambda e, qs=qs, off=off: e.scalar_tensor_tensor(
                        yc2a[:, qs * 128:(qs + 1) * 128], psY[2 + qs // 2][:, off:off + 128],
                        evs[:, 12 + qs:13 + qs], yc[qs][:, :],
                        ALU.mult, ALU.add),
                        wait=[(bd.VE, ycd)] if qs == 0 else [], inc=bd.VE)
                evac_free = bd.VE.n
                ph.on("vector", lambda e: e.tensor_reduce(
                    evs[:, 16:20], yc2a[:, :].rearrange("p (a b) -> p a b", a=4), AX.X, ALU.add),
                    wait=[(bd.VE, bd.VE.n)], inc=bd.VE)
                ph.on("vector", lambda e: e.tensor_scalar(
                    evs[:, 16:20], evs[:, 16:20], _c(1.0 / 128.0), None, ALU.mult),
                    wait=[(bd.VE, bd.VE.n)], inc=bd.VE)
                for qs in range(4):
                    ph.on("vector", lambda e, qs=qs: e.tensor_scalar(
                        yc2a[:, qs * 128:(qs + 1) * 128], yc2a[:, qs * 128:(qs + 1) * 128],
                        evs[:, 16 + qs:17 + qs], None, ALU.subtract),
                        wait=[(bd.VE, bd.VE.n)] if qs == 0 else [], inc=(bd.VE if qs == 3 else None))
                ph.on("scalar", lambda e: e.activation(sqv[:, 0:512], yc2a[:, :], AF.Square),
                      wait=[(bd.VE, bd.VE.n), (bd.SE, bd.SE.n)], inc=bd.SE)
                ph.on("vector", lambda e: e.tensor_reduce(
                    evs[:, 20:24], sqv[:, 0:512].rearrange("p (a b) -> p a b", a=4), AX.X, ALU.add),
                    wait=[(bd.SE, bd.SE.n)], inc=bd.VE)
                ph.on("scalar", lambda e: e.activation(
                    evs[:, 24:28], evs[:, 20:24], AF.Sqrt, scale=_c(1.0 / 128.0), bias=_c(EPS_LN)),
                    wait=[(bd.VE, bd.VE.n)], inc=bd.SE)
                ph.on("vector", lambda e: e.reciprocal(evs[:, 24:28], evs[:, 24:28]),
                      wait=[(bd.SE, bd.SE.n)], inc=bd.VE)
                ph.on("vector", lambda e: e.tensor_scalar(
                    evs[:, 24:28], evs[:, 24:28], _c(1.0 - LAMBDA_INIT), None, ALU.mult),
                    wait=[(bd.VE, bd.VE.n)], inc=bd.VE)
                for qs in range(4):
                    ph.on("vector", lambda e, qs=qs, W=W: e.tensor_scalar(
                        ytda[:, W * 512 + qs * 128:W * 512 + (qs + 1) * 128],
                        yc2a[:, qs * 128:(qs + 1) * 128], evs[:, 24 + qs:25 + qs], None, ALU.mult),
                        wait=[(bd.VE, bd.VE.n)] if qs == 0 else [], inc=(bd.VE if qs == 3 else None))
                    tr_marks.append(bd.VE.n)
            for ci in range(32):
                ph.on("tensor", lambda e, ci=ci: e.transpose(
                    psTb[:, 0:128], ytda[:, ci * 128:(ci + 1) * 128], sb["ident"][:, :]),
                    wait=[(bd.VE, tr_marks[ci]), (bd.VE, bd.VE.n)], inc=bd.MM)
                ph.on("vector", lambda e, ci=ci, hl=hl: e.tensor_copy(
                    yfh[hl][:, ci * 128:(ci + 1) * 128], psTb[:, 0:128]),
                    wait=[(bd.MM, bd.MM.n)], inc=bd.VE)
            ph.on("sync", lambda e, hl=hl, hp=hp: e.dma_start(
                out=io["y1in"][(hp * 2 + hl) * 128:(hp * 2 + hl + 1) * 128, :], in_=yfh[hl][:, :]),
                wait=[(bd.VE, bd.VE.n)], inc=bd.AD, inc_k=16)
            if hl == 1:
                bd.ev[f"y1p{hp}"] = [(bd.AD, bd.AD.n)]
    ph.emit()
    st2.close()
    bd.ev["y1in"] = [(bd.AD, bd.AD.n)]


def emit_cc(bd, in_name, out_names):
    ph = Ph(bd)
    if len(out_names) == 1:
        s = bd.ccsem()
        ph.on("gpsimd", lambda e: e.collective_compute(
            "AllGather", ALU.bypass, replica_groups=GROUPS,
            ins=[bd.io[in_name].ap().opt()], outs=[bd.io[out_names[0]].ap().opt()]),
            wait=bd.ev.get(in_name, []), inc=s)
        bd.ev[out_names[0]] = [(s, 1)]
    else:
        nr = bd.io[in_name].shape[0] // 4
        evp = "y0p" if in_name == "y0in" else "y1p"
        for h in range(4):
            s = bd.ccsem()
            ph.on("gpsimd", lambda e, h=h: e.collective_compute(
                "AllGather", ALU.bypass, replica_groups=GROUPS,
                ins=[bd.io[in_name][h * nr:(h + 1) * nr, :].opt()],
                outs=[bd.io[out_names[h]].ap().opt()]),
                wait=bd.ev.get(f"{evp}{h}", []), inc=s)
            bd.ev[out_names[h]] = [(s, 1)]
    ph.emit()


def emit_final_wait(bd):
    ph = Ph(bd)
    ph.on("sync", lambda e: e.nop(), wait=[(bd.AD, bd.AD.n)], inc=None)
    ph.emit()



MID = {
    "ag0in": ([D, TM], BF16), "ag0out": ([2 * D, TM], BF16),
    "g0": ([D, TM], BF16),
    "y0in": ([512, TF], BF16),
    "y0outp0": ([256, TF], BF16), "y0outp1": ([256, TF], BF16),
    "y0outp2": ([256, TF], BF16), "y0outp3": ([256, TF], BF16),
    "x1": ([D, TM], F32),
    "ag1in": ([D, TM], BF16), "ag1out": ([2 * D, TM], BF16),
    "g1": ([D, TM], BF16),
    "y1in": ([1024, TF], BF16),
    "y1outp0": ([512, TF], BF16), "y1outp1": ([512, TF], BF16),
    "y1outp2": ([512, TF], BF16), "y1outp3": ([512, TF], BF16),
}

INPUTS_BASE = [
    ("xT", [D, TM], F32),
    ("l0_wqT", [D, 512], F32), ("l0_wkT", [D, 512], F32), ("l0_wvT", [D, 512], F32),
    ("l0_projT", [64, 8, 64], F32),
    ("l0_woT", [D, D], F32),
    ("w1T", [D, DHID := 4096], F32), ("w2T", [D, 4096], F32),
    ("h1T", [4096, 4096], F32), ("h2T", [4096, 4096], F32),
    ("w3T", [4096, D], F32),
    ("wq1T", [D, 512], F32), ("wq2T", [D, 512], F32),
    ("wk1T", [D, 512], F32), ("wk2T", [D, 512], F32),
    ("wvT", [D, 1024], F32), ("wcT", [2048, D], F32),
    ("routerT", [D, 8], F32),
    ("gateT", [8, D, 2816], F32), ("upT", [8, D, 2816], F32),
    ("downT", [8, 2816, D], F32),
]

PRELOAD_IN = [(nm, ([128, max(1, len(_CONST_VALS))] if shp is None else shp), dt)
              for nm, shp, dt in PRELOAD_TENS] + [("l0_projT", [128, 4, 64], F32)]

PHASE_READS = {
    "p0": ["xT"],
    "p1": ["ag0in", "w1T", "w2T", "h1T", "h2T", "w3T"],
    "p2a": ["ag0out", "l0_wqT", "l0_wkT", "l0_wvT", "l0_projT"],
    "p2b": ["y0outp0", "y0outp1", "y0outp2", "y0outp3", "l0_woT", "g0", "xT"],
    "p3": ["ag1in", "routerT", "gateT", "upT", "downT"],
    "p4a": ["ag1out", "wq1T", "wq2T", "wk1T", "wk2T", "wvT"],
    "p4b": ["y1outp0", "y1outp1", "y1outp2", "y1outp3", "wcT", "g1", "x1"],
}
PHASE_WRITES = {
    "p0": ["ag0in"], "p1": ["g0"], "p2a": ["y0in"], "p2b": ["x1", "ag1in"],
    "p3": ["g1"], "p4a": ["y1in"], "p4b": ["outT"],
}
UNFUSED_LAUNCHES = ["p0", "p1", "p2a", "p2b", "p3", "p4a", "p4b"]
CC_ROUTES = {"ag0in": ["ag0out"], "ag1in": ["ag1out"],
             "y0in": [f"y0outp{h}" for h in range(4)],
             "y1in": [f"y1outp{h}" for h in range(4)]}


def _emit_phase(bd, name):
    if name == "p0":
        emit_ln(bd, "xT", "ag0in")
    elif name == "p1":
        emit_ffn0(bd)
    elif name == "p2a":
        emit_attn0(bd)
    elif name == "p2b":
        st3 = ExitStack()
        s0 = st3.enter_context(bd.nc.sbuf_tensor("s0sb", [128, 8, TM], BF16))
        sdone = emit_proj_out(bd, "y0in", "l0_woT", 8, "l0_bo", s0)
        bd.ev["x_s0"] = [sdone]
        emit_combine(bd, "xT", lambda k: s0[:, k, :], "g0", "x1")
        emit_ln(bd, "x1", "ag1in")
        st3.close()
    elif name == "p3":
        emit_moe(bd)
    elif name == "p4a":
        emit_attn1(bd)
    elif name == "p4b":
        st3 = ExitStack()
        s1 = st3.enter_context(bd.nc.sbuf_tensor("s1sb", [128, 8, TM], BF16))
        emit_proj_out(bd, "y1in", "wcT", 16, None, s1)
        emit_combine(bd, "x1", lambda k: s1[:, k, :], "g1", "outT")
        st3.close()
    else:
        raise ValueError(name)


def build_program(phases, fused):
    nc = bass.Bass("TRN2", target_bir_lowering=False, debug=False)
    st = ExitStack()
    io = {}
    reads = set()
    writes = set()
    for p in phases:
        reads.update(PHASE_READS[p])
        writes.update(PHASE_WRITES[p])
    if fused:
        for a, outs2 in CC_ROUTES.items():
            if a in writes:
                for b in outs2:
                    reads.discard(b)
    base_shapes = {nm: (shp, dt) for nm, shp, dt in INPUTS_BASE + PRELOAD_IN}
    innames = []
    outnames = []
    for nm, shp, dt in PRELOAD_IN:
        io[nm] = nc.dram_tensor(nm, shp, dt, kind="ExternalInput")
        innames.append(nm)
    for nm in sorted(reads):
        if nm in io:
            continue
        if nm in MID:
            shp, dt = MID[nm]
            kind = "Internal" if fused else "ExternalInput"
        else:
            shp, dt = base_shapes[nm]
            kind = "ExternalInput"
        io[nm] = nc.dram_tensor(nm, shp, dt, kind=kind)
        if kind == "ExternalInput":
            innames.append(nm)
    for nm in sorted(writes):
        if nm in io:
            continue
        if nm == "outT":
            io[nm] = nc.dram_tensor(nm, [D, TM], F32, kind="ExternalOutput")
            outnames.append(nm)
            continue
        shp, dt = MID[nm]
        kind = "Internal" if fused else "ExternalOutput"
        io[nm] = nc.dram_tensor(nm, shp, dt, kind=kind)
        if kind == "ExternalOutput":
            outnames.append(nm)
    if fused:
        for a, outs2 in CC_ROUTES.items():
            if a in writes:
                for b in outs2:
                    if b not in io:
                        shp, dt = MID[b]
                        io[b] = nc.dram_tensor(b, shp, dt, kind="Internal")
    if "p3" in phases:
        io["idxd"] = nc.dram_tensor("idxd", [1, TM], BF16, kind="Internal")

    bd = Bld(nc, st, io)
    emit_preload(bd)
    for p in phases:
        _emit_phase(bd, p)
        if fused and p in ("p0", "p2a", "p2b", "p4a"):
            for a in PHASE_WRITES[p]:
                if a in CC_ROUTES:
                    emit_cc(bd, a, CC_ROUTES[a])
    emit_final_wait(bd)
    st.close()
    return nc, innames, outnames



def host_prep(inputs):
    fT = lambda a: np.ascontiguousarray(np.asarray(a, dtype=np.float32).T)
    f = lambda a: np.ascontiguousarray(np.asarray(a, dtype=np.float32))
    shared = {}
    shared["consts"] = np.ascontiguousarray(
        np.tile(np.array(_CONST_VALS, np.float32), (128, 1)))
    shared["ident"] = np.eye(128, dtype=BD16)
    shared["identf"] = np.eye(128, dtype=np.float32)
    shared["iotam"] = np.ascontiguousarray(
        np.tile(np.arange(8, dtype=np.float32) - BIGI, (128, 16)))
    masks = np.zeros((128, 4, 512), np.float32)
    kk = np.arange(128)[:, None]
    qq = np.arange(512)[None, :]
    for dd in range(4):
        masks[:, dd, :] = np.where(kk + 128 * dd <= qq, 0.0, NEGBIG)
    shared["masks"] = masks
    shared["w1T"] = fT(inputs["L0_ffn_w1_w"])
    shared["w2T"] = fT(inputs["L0_ffn_w2_w"])
    shared["h1T"] = fT(inputs["L0_ffn_h1_w"])
    shared["h2T"] = fT(inputs["L0_ffn_h2_w"])
    shared["w3T"] = fT(inputs["L0_ffn_w3_w"])
    shared["s1c"] = f(inputs["L0_ffn_w1_s"]).reshape(32, 128).T.copy()
    shared["s2c"] = f(inputs["L0_ffn_w2_s"]).reshape(32, 128).T.copy()
    shared["sh1c"] = f(inputs["L0_ffn_h1_s"]).reshape(32, 128).T.copy()
    shared["sh2c"] = f(inputs["L0_ffn_h2_s"]).reshape(32, 128).T.copy()
    shared["s3c"] = f(inputs["L0_ffn_w3_s"]).reshape(8, 128).T.copy()
    shared["l0_woT"] = fT(inputs["L0_attn_Wo"])
    shared["l0_bo"] = f(inputs["L0_attn_bo"]).reshape(8, 128).T.copy()
    shared["wcT"] = fT(inputs["L1_attn_Wc"])
    shared["routerT"] = fT(inputs["L1_ffn_router_w"])
    shared["gateT"] = np.ascontiguousarray(
        np.asarray(inputs["L1_ffn_gate_w"], np.float32).transpose(0, 2, 1))
    shared["upT"] = np.ascontiguousarray(
        np.asarray(inputs["L1_ffn_up_w"], np.float32).transpose(0, 2, 1))
    shared["downT"] = np.ascontiguousarray(
        np.asarray(inputs["L1_ffn_down_w"], np.float32).transpose(0, 2, 1))

    x = np.asarray(inputs["L0_attn_query"], np.float32)
    percore = []
    for c in range(8):
        b, half = c // 2, c % 2
        hs = slice(half * 512, (half + 1) * 512)
        hv = slice(half * 1024, (half + 1) * 1024)
        h8 = slice(half * 8, (half + 1) * 8)
        d = {}
        d["xT"] = np.ascontiguousarray(x[b, half * TM:(half + 1) * TM, :].T)
        d["l0_wqT"] = fT(np.asarray(inputs["L0_attn_Wq"], np.float32)[hs, :])
        d["l0_wkT"] = fT(np.asarray(inputs["L0_attn_Wk"], np.float32)[hs, :])
        d["l0_wvT"] = fT(np.asarray(inputs["L0_attn_Wv"], np.float32)[hs, :])
        d["l0_bq"] = f(inputs["L0_attn_bq"])[hs].reshape(4, 128).T.copy()
        d["l0_bk"] = f(inputs["L0_attn_bk"])[hs].reshape(4, 128).T.copy()
        d["l0_bvr"] = f(inputs["L0_attn_bv"])[hs].reshape(1, 512).copy()
        pmy = np.asarray(inputs["L0_attn_proj"], np.float32)[h8]
        pT = np.zeros((128, 4, 64), np.float32)
        for hp in range(4):
            for hl in range(2):
                pT[hl * 64:(hl + 1) * 64, hp, :] = pmy[2 * hp + hl].T
        d["l0_projT"] = pT
        d["wq1T"] = fT(np.asarray(inputs["L1_attn_Wq1"], np.float32)[hs, :])
        d["wq2T"] = fT(np.asarray(inputs["L1_attn_Wq2"], np.float32)[hs, :])
        d["wk1T"] = fT(np.asarray(inputs["L1_attn_Wk1"], np.float32)[hs, :])
        d["wk2T"] = fT(np.asarray(inputs["L1_attn_Wk2"], np.float32)[hs, :])
        d["wvT"] = fT(np.asarray(inputs["L1_attn_Wv"], np.float32)[hv, :])
        d["lq1"] = f(inputs["L1_attn_lq1"])[h8].copy()
        d["lk1"] = f(inputs["L1_attn_lk1"])[h8].copy()
        d["lq2"] = f(inputs["L1_attn_lq2"])[h8].copy()
        d["lk2"] = f(inputs["L1_attn_lk2"])[h8].copy()
        d["coloff"] = np.array([[half * TM]], np.uint32)
        percore.append(d)
    return shared, percore


def _route_cc(vals, produced):
    for a, outs2 in CC_ROUTES.items():
        if a not in produced:
            continue
        for m in range(4):
            if len(outs2) == 1:
                cat = np.concatenate([vals[2 * m][a], vals[2 * m + 1][a]], axis=0)
                vals[2 * m][outs2[0]] = cat
                vals[2 * m + 1][outs2[0]] = cat
            else:
                nr = vals[2 * m][a].shape[0] // 4
                for h in range(4):
                    cat = np.concatenate([vals[2 * m][a][h * nr:(h + 1) * nr],
                                          vals[2 * m + 1][a][h * nr:(h + 1) * nr]], axis=0)
                    vals[2 * m][outs2[h]] = cat
                    vals[2 * m + 1][outs2[h]] = cat


def kernel(**inputs):
    shared, percore = host_prep(inputs)
    outs = [None] * 8
    if FUSED:
        nc, innames, outnames = build_program(UNFUSED_LAUNCHES, fused=True)
        maps = []
        for c in range(8):
            m = {}
            for n in innames:
                v = percore[c].get(n)
                if v is None:
                    v = shared[n]
                m[n] = v
            maps.append(m)
        res = run_bass_kernel_spmd(nc, maps, core_ids=list(range(8)))
        for c in range(8):
            outs[c] = res.results[c]["outT"]
    else:
        vals = [dict() for _ in range(8)]
        for launch in UNFUSED_LAUNCHES:
            nc, innames, outnames = build_program([launch], fused=False)
            maps = []
            for c in range(8):
                m = {}
                for n in innames:
                    v = percore[c].get(n)
                    if v is None:
                        v = shared.get(n)
                    if v is None:
                        v = vals[c][n]
                    m[n] = v
                maps.append(m)
            res = run_bass_kernel_spmd(nc, maps, core_ids=list(range(8)))
            for c in range(8):
                for n in outnames:
                    vals[c][n] = res.results[c][n]
            _route_cc(vals, PHASE_WRITES[launch])
        for c in range(8):
            outs[c] = vals[c]["outT"]
    out = np.empty((4, TF, D), np.float32)
    for c in range(8):
        b, half = c // 2, c % 2
        out[b, half * TM:(half + 1) * TM, :] = outs[c].T
    return out


# version 12
# speedup vs baseline: 1.0045x; 1.0045x over previous
import numpy as np
import ml_dtypes
from contextlib import ExitStack

import concourse.bass as bass
import concourse.mybir as mybir
from concourse.bass_utils import run_bass_kernel_spmd

F32 = mybir.dt.float32
BF16 = mybir.dt.bfloat16
U32 = mybir.dt.uint32
AF = mybir.ActivationFunctionType
ALU = mybir.AluOpType
AX = mybir.AxisListType
BD16 = ml_dtypes.bfloat16

D = 1024
TM = 2048
TF = 4096
DH = 64
LAMBDA_INIT = 0.7 - 0.6 * float(np.exp(-0.3))
EPS_LN = 1e-5
EPS_FEAT = 1e-4
EPS_DEN = 0.01
C0 = float(DH) ** -0.25
LN_RATIO = float(np.log(64.0 ** -0.5))
NEGBIG = -1e9
BIGI = 1e4
GROUPS = [[0, 1], [2, 3], [4, 5], [6, 7]]

ENGINES = ("sync", "gpsimd", "tensor", "vector", "scalar")

FUSED = True
SIM_SAFE = False

_CONST_VALS = []


def _c(v):
    v = float(v)
    if v not in _CONST_VALS:
        _CONST_VALS.append(v)
    return v


for _v in (1.0 / D, EPS_LN, LAMBDA_INIT, C0, LN_RATIO, EPS_FEAT, EPS_DEN,
           1e-30, -1.0, 1.0, 0.125, -0.125, 1.0 / 128.0, BIGI,
           1.0 - LAMBDA_INIT, -0.5 * C0 * C0, -C0, 0.0, 2.0, 3.0, 4.0, 5.0,
           6.0, 7.0):
    _c(_v)


def _full(t):
    return t[tuple(slice(None) for _ in range(len(t.shape)))]


class Sem:
    def __init__(self, st, nc, name):
        self.h = st.enter_context(nc.semaphore(name))
        self.n = 0

    def bump(self, k=1):
        self.n += k
        return self.n


class Ph:

    def __init__(self, bd):
        self.bd = bd
        self.ops = {e: [] for e in ENGINES}
        self.pending = {e: [] for e in ENGINES}

    def barrier(self, marks=None):
        if marks is None:
            marks = self.bd.snapshot()
        for e in ENGINES:
            self.pending[e] = self.pending[e] + list(marks)

    def on(self, eng, fn, wait=None, inc=None, inc_k=1, inc2=None, inc2_k=1):
        w = self.pending[eng] + list(wait or [])
        self.pending[eng] = []
        if SIM_SAFE and inc is not None and inc_k == 16:
            w.append((inc, inc.n))
        if inc is not None:
            inc.bump(inc_k)
        if inc2 is not None:
            inc2.bump(inc2_k)
        self.ops[eng].append((fn, w, inc, inc_k, inc2, inc2_k))

    def emit(self):
        nc = self.bd.nc
        with nc.Block() as blk:
            for eng in ENGINES:
                ops = self.ops[eng]
                if not ops:
                    continue

                def mk(ops):
                    def closure(e):
                        hi = {}
                        for fn, waits, sem, k, sem2, k2 in ops:
                            for (s, v) in waits:
                                if v > 0 and hi.get(id(s), -1) < v:
                                    e.wait_ge(s.h, v)
                                    hi[id(s)] = v
                            ins = fn(e)
                            if sem is not None:
                                ins = ins.then_inc(sem.h, k)
                            if sem2 is not None:
                                ins.then_inc(sem2.h, k2)
                    return closure

                getattr(blk, eng)(mk(ops))


class Bld:
    def __init__(self, nc, st, io):
        self.nc = nc
        self.st = st
        self.io = io
        self.sb = {}
        self.AD = Sem(st, nc, "AD")
        self.GD = Sem(st, nc, "GD")
        self.MM = Sem(st, nc, "MM")
        self.VE = Sem(st, nc, "VE")
        self.SE = Sem(st, nc, "SE")
        self.WF = Sem(st, nc, "WF")
        self.allsems = [self.AD, self.GD, self.MM, self.VE, self.SE, self.WF]
        self.ccsems = []
        self.ev = {}
        self.nphase = 0

    def snapshot(self):
        return [(s, s.n) for s in self.allsems if s.n > 0] + \
               [(s, s.n) for s in self.ccsems if s.n > 0]

    def tag(self):
        self.nphase += 1
        return f"z{self.nphase}_"

    def ccsem(self):
        s = Sem(self.st, self.nc, f"CC{len(self.ccsems)}")
        self.ccsems.append(s)
        return s


PRELOAD_TENS = [
    ("consts", None, F32), ("ident", [128, 128], BF16),
    ("identf", [128, 128], F32), ("iotam", [128, 128], F32),
    ("masks", [128, 4, 512], F32),
    ("l0_bq", [128, 4], F32), ("l0_bk", [128, 4], F32),
    ("l0_bvr", [1, 512], F32), ("l0_bo", [128, 8], F32),
    ("s1c", [128, 32], F32), ("s2c", [128, 32], F32),
    ("sh1c", [128, 32], F32), ("sh2c", [128, 32], F32),
    ("s3c", [128, 8], F32), ("coloff", [1, 1], U32),
    ("lq1", [8, 64], F32), ("lk1", [8, 64], F32),
    ("lq2", [8, 64], F32), ("lk2", [8, 64], F32),
]


def emit_preload(bd):
    nc, io, sb = bd.nc, bd.io, bd.sb
    ph = Ph(bd)
    for nm, shp, dt in PRELOAD_TENS:
        if shp is None:
            shp = [128, max(1, len(_CONST_VALS))]
        sb[nm] = bd.st.enter_context(nc.sbuf_tensor("c_" + nm, shp, dt))
    sb["projT"] = bd.st.enter_context(nc.sbuf_tensor("c_projT", [128, 4, 64], BF16))
    sb["ones_f"] = bd.st.enter_context(nc.sbuf_tensor("c_onesf", [128, 130], F32))
    sb["ones_b"] = bd.st.enter_context(nc.sbuf_tensor("c_onesb", [128, 130], BF16))
    sb["lamrow"] = bd.st.enter_context(nc.sbuf_tensor("c_lamrow", [1, 8], F32))

    for i, v in enumerate(_CONST_VALS):
        nc.const_aps.register(F32, v, sb["consts"][:, i:i + 1])

    for nm, _, _ in PRELOAD_TENS:
        ph.on("sync", lambda e, nm=nm: e.dma_start(out=_full(bd.sb[nm]), in_=_full(io[nm])),
              inc=bd.AD, inc_k=16)
    ph.on("gpsimd", lambda e: e.dma_start(out=_full(sb["projT"]), in_=_full(io["l0_projT"])),
          inc=bd.GD, inc_k=16)
    ph.on("vector", lambda e: e.memset(sb["ones_f"][:, :], 1.0), inc=bd.VE)
    ph.on("vector", lambda e: e.memset(sb["ones_b"][:, :], 1.0), inc=bd.VE)

    st2 = ExitStack()
    lsc = st2.enter_context(nc.sbuf_tensor("pre_lsc", [16, 4], F32))
    lpad = st2.enter_context(nc.sbuf_tensor("pre_lpad", [16, 128], BF16))
    pst = st2.enter_context(nc.psum_tensor("pre_pst", [128, 128], BF16))
    ph.barrier()
    for j, (a, bnm) in enumerate((("lq1", "lk1"), ("lq2", "lk2"))):
        ph.on("vector", lambda e, a=a, bnm=bnm: e.tensor_mul(sb[a][0:8, :], sb[a][0:8, :], sb[bnm][0:8, :]),
              inc=bd.VE)
        ph.on("vector", lambda e, a=a, j=j: e.tensor_reduce(lsc[0:8, j:j + 1], sb[a][0:8, :], AX.X, ALU.add),
              wait=[(bd.VE, bd.VE.n)], inc=bd.VE)
        ph.on("scalar", lambda e, j=j: e.activation(lsc[0:8, j:j + 1], lsc[0:8, j:j + 1], AF.Exp),
              wait=[(bd.VE, bd.VE.n)], inc=bd.SE)
    ph.on("vector", lambda e: e.tensor_sub(lsc[0:8, 0:1], lsc[0:8, 0:1], lsc[0:8, 1:2]),
          wait=[(bd.SE, bd.SE.n)], inc=bd.VE)
    ph.on("vector", lambda e: e.tensor_scalar(lsc[0:8, 0:1], lsc[0:8, 0:1], _c(LAMBDA_INIT), None, ALU.add),
          wait=[(bd.VE, bd.VE.n)], inc=bd.VE)
    ph.on("vector", lambda e: e.memset(lpad[:, :], 0.0), inc=bd.VE)
    ph.on("vector", lambda e: e.tensor_copy(lpad[0:8, 0:1], lsc[0:8, 0:1]),
          wait=[(bd.VE, bd.VE.n)], inc=bd.VE)
    ph.on("tensor", lambda e: e.transpose(pst[0:128, 0:16], lpad[0:16, :], sb["ident"][0:16, 0:16]),
          wait=[(bd.VE, bd.VE.n)], inc=bd.MM)
    ph.on("vector", lambda e: e.tensor_copy(sb["lamrow"][0:1, 0:8], pst[0:1, 0:8]),
          wait=[(bd.MM, bd.MM.n)], inc=bd.VE)
    ph.emit()
    st2.close()
    bd.ev["preload"] = bd.snapshot()


def emit_ln(bd, src_name, dst_name):
    nc, io, sb = bd.nc, bd.io, bd.sb
    ph = Ph(bd)
    ph.barrier(bd.ev.get(src_name, []) + bd.snapshot())
    st2 = ExitStack()
    zt = bd.tag()
    xin = [st2.enter_context(nc.sbuf_tensor(zt + f"ln_x{k}", [128, TM], F32)) for k in range(8)]
    xsq = st2.enter_context(nc.sbuf_tensor(zt + zt + "ln_xsq", [128, 512], F32))
    rows = st2.enter_context(nc.sbuf_tensor(zt + "ln_rows", [1, 5 * TM], F32))
    tmp = st2.enter_context(nc.sbuf_tensor(zt + "ln_tmp", [128, 512], F32))
    xno = st2.enter_context(nc.sbuf_tensor(zt + "ln_out", [128, 8, TM], BF16))
    ps_m = st2.enter_context(nc.psum_tensor(zt + "ln_psm", [1, 512], F32))
    ps_s = st2.enter_context(nc.psum_tensor(zt + "ln_pss", [1, 512], F32))
    ps_b1 = st2.enter_context(nc.psum_tensor(zt + "ln_psb1", [128, 512], F32))
    ps_b2 = st2.enter_context(nc.psum_tensor(zt + "ln_psb2", [128, 512], F32))
    src = io[src_name]
    ones_col = sb["ones_f"][:, 0:1]
    ones_row = sb["ones_f"][0:1, 0:128]

    xw = []
    for k in range(8):
        ph.on("sync", lambda e, k=k: e.dma_start(out=xin[k][:, :], in_=src[k * 128:(k + 1) * 128, :]),
              inc=bd.AD, inc_k=16)
        xw.append((bd.AD, bd.AD.n))

    evac_prev = None
    norm_prev = None
    for nt in range(4):
        sl = slice(nt * 512, (nt + 1) * 512)
        wbank = [(bd.VE, evac_prev)] if evac_prev else []
        for k in range(8):
            ph.on("tensor", lambda e, k=k, sl=sl: e.matmul(
                ps_m[0:1, :], ones_col, xin[k][:, sl], start=(k == 0), stop=(k == 7)),
                wait=[xw[k]] + wbank, inc=(bd.MM if k == 7 else None))
        for k in range(8):
            wsq = [xw[k]]
            if nt > 0 or k > 0:
                wsq.append((bd.MM, bd.MM.n))
            ph.on("scalar", lambda e, k=k, sl=sl: e.activation(xsq[:, :], xin[k][:, sl], AF.Square),
                  wait=wsq, inc=bd.SE)
            ph.on("tensor", lambda e, k=k: e.matmul(
                ps_s[0:1, :], ones_col, xsq[:, :], start=(k == 0), stop=(k == 7)),
                wait=[(bd.SE, bd.SE.n)] + wbank, inc=bd.MM)
        mm_stats = bd.MM.n
        o = nt * 512
        mrow = rows[0:1, o:o + 512]
        srow = rows[0:1, TM + o:TM + o + 512]
        vrow = rows[0:1, 2 * TM + o:2 * TM + o + 512]
        rrow = rows[0:1, 3 * TM + o:3 * TM + o + 512]
        mrr = rows[0:1, 4 * TM + o:4 * TM + o + 512]
        ph.on("vector", lambda e, mrow=mrow: e.tensor_scalar(mrow, ps_m[0:1, :], _c(1.0 / D), None, ALU.mult),
              wait=[(bd.MM, mm_stats)], inc=bd.VE)
        ph.on("vector", lambda e, srow=srow: e.tensor_scalar(srow, ps_s[0:1, :], _c(1.0 / D), None, ALU.mult),
              inc=bd.VE)
        ph.on("vector", lambda e, vrow=vrow, mrow=mrow: e.tensor_mul(vrow, mrow, mrow),
              wait=[(bd.VE, bd.VE.n)], inc=bd.VE)
        ph.on("vector", lambda e, vrow=vrow, srow=srow: e.tensor_sub(vrow, srow, vrow),
              wait=[(bd.VE, bd.VE.n)], inc=bd.VE)
        evac_prev = bd.VE.n
        ph.on("scalar", lambda e, vrow=vrow, rrow=rrow: e.activation(rrow, vrow, AF.Sqrt, bias=_c(EPS_LN)),
              wait=[(bd.VE, bd.VE.n)], inc=bd.SE)
        ph.on("vector", lambda e, rrow=rrow: e.reciprocal(rrow, rrow), wait=[(bd.SE, bd.SE.n)], inc=bd.VE)
        ph.on("vector", lambda e, mrr=mrr, mrow=mrow, rrow=rrow: e.tensor_mul(mrr, mrow, rrow),
              wait=[(bd.VE, bd.VE.n)], inc=bd.VE)
        rows_done = bd.VE.n
        wbc = [(bd.VE, rows_done)]
        if norm_prev:
            wbc.append((bd.VE, norm_prev))
        ph.on("tensor", lambda e, rrow=rrow: e.matmul(ps_b1[:, :], ones_row, rrow, start=True, stop=True),
              wait=wbc, inc=None)
        ph.on("tensor", lambda e, mrr=mrr: e.matmul(ps_b2[:, :], ones_row, mrr, start=True, stop=True),
              inc=bd.MM)
        bc = bd.MM.n
        for k in range(8):
            ph.on("vector", lambda e, k=k, sl=sl: e.tensor_mul(tmp[:, :], xin[k][:, sl], ps_b1[:, :]),
                  wait=[(bd.MM, bc), (bd.VE, bd.VE.n)], inc=bd.VE)
            ph.on("vector", lambda e, k=k, sl=sl: e.tensor_sub(xno[:, k, sl], tmp[:, :], ps_b2[:, :]),
                  wait=[(bd.VE, bd.VE.n)], inc=bd.VE)
        norm_prev = bd.VE.n
    done = bd.VE.n
    for k in range(8):
        ph.on("sync", lambda e, k=k: e.dma_start(
            out=io[dst_name][k * 128:(k + 1) * 128, :], in_=xno[:, k, :]),
            wait=[(bd.VE, done)], inc=bd.AD, inc_k=16)
    ph.emit()
    st2.close()
    bd.ev[dst_name] = [(bd.AD, bd.AD.n)]


def emit_combine(bd, x_name, s_chunks, g_name, out_name):
    nc, io = bd.nc, bd.io
    ph = Ph(bd)
    ph.barrier(bd.ev.get(x_name, []) + bd.ev.get(g_name, []) + bd.snapshot())
    st2 = ExitStack()
    zt = bd.tag()
    xc = [st2.enter_context(nc.sbuf_tensor(zt + f"cb_x{j}", [128, TM], F32)) for j in range(2)]
    gc = [st2.enter_context(nc.sbuf_tensor(zt + f"cb_g{j}", [128, TM], BF16)) for j in range(2)]
    sg = st2.enter_context(nc.sbuf_tensor(zt + "cb_sg", [128, TM], F32))
    oc = [st2.enter_context(nc.sbuf_tensor(zt + f"cb_o{j}", [128, TM], F32)) for j in range(2)]
    st_done = []
    for k in range(8):
        j = k % 2
        w = [(bd.AD, st_done[k - 2])] if k >= 2 else []
        ph.on("sync", lambda e, k=k, j=j: e.dma_start(out=xc[j][:, :], in_=io[x_name][k * 128:(k + 1) * 128, :]),
              wait=w, inc=bd.AD, inc_k=16)
        ph.on("sync", lambda e, k=k, j=j: e.dma_start(out=gc[j][:, :], in_=io[g_name][k * 128:(k + 1) * 128, :]),
              inc=bd.AD, inc_k=16)
        adn = bd.AD.n
        ph.on("vector", lambda e, k=k, j=j: e.tensor_mul(sg[:, :], s_chunks(k), gc[j][:, :]),
              wait=[(bd.AD, adn), (bd.VE, bd.VE.n)], inc=bd.VE)
        ph.on("vector", lambda e, k=k, j=j: e.tensor_add(oc[j][:, :], xc[j][:, :], sg[:, :]),
              wait=[(bd.VE, bd.VE.n)], inc=bd.VE)
        ph.on("sync", lambda e, k=k, j=j: e.dma_start(out=io[out_name][k * 128:(k + 1) * 128, :], in_=oc[j][:, :]),
              wait=[(bd.VE, bd.VE.n)], inc=bd.AD, inc_k=16)
        st_done.append(bd.AD.n)
    ph.emit()
    st2.close()
    bd.ev[out_name] = [(bd.AD, bd.AD.n)]


def emit_ffn0(bd):
    nc, io, sb = bd.nc, bd.io, bd.sb
    ph = Ph(bd)
    ph.barrier(bd.ev.get("ag0in", []) + bd.snapshot())
    st2 = ExitStack()
    TP = 1024
    xnp = st2.enter_context(nc.sbuf_tensor("f0_xn", [128, 8, TP], BF16))
    h1 = st2.enter_context(nc.sbuf_tensor("f0_h1", [128, 32, TP], BF16))
    h2 = st2.enter_context(nc.sbuf_tensor("f0_h2", [128, 32, TP], BF16))
    wa = [st2.enter_context(nc.sbuf_tensor(f"f0_wa{j}", [128, 8, 128], BF16)) for j in range(2)]
    wb = [st2.enter_context(nc.sbuf_tensor(f"f0_wb{j}", [128, 8, 128], BF16)) for j in range(2)]
    wh = [st2.enter_context(nc.sbuf_tensor(f"f0_wh{j}", [128, 32, 128], BF16)) for j in range(2)]
    ea = [st2.enter_context(nc.sbuf_tensor(f"f0_ea{j}", [128, 512], BF16)) for j in range(2)]
    eb = [st2.enter_context(nc.sbuf_tensor(f"f0_eb{j}", [128, 512], BF16)) for j in range(2)]
    go = st2.enter_context(nc.sbuf_tensor("f0_go", [128, TP], BF16))
    psA = [st2.enter_context(nc.psum_tensor(f"f0_psA{j}", [128, 512], F32)) for j in range(2)]
    psB = [st2.enter_context(nc.psum_tensor(f"f0_psB{j}", [128, 512], F32)) for j in range(2)]

    for p in range(2):
        ph.barrier()
        for k in range(8):
            ph.on("sync", lambda e, k=k, p=p: e.dma_start(
                out=xnp[:, k, :], in_=io["ag0in"][k * 128:(k + 1) * 128, p * TP:(p + 1) * TP]),
                inc=bd.AD, inc_k=16)
        xn_ready = (bd.AD, bd.AD.n)

        gd_marks = []
        wf_base = bd.WF.n
        for i in range(32):
            w = [(bd.WF, wf_base + 2 * (i - 1))] if i >= 2 else []
            ph.on("gpsimd", lambda e, i=i: e.dma_start(
                out=_full(wa[i % 2]),
                in_=io["w1T"][:, i * 128:(i + 1) * 128].rearrange("(a p) o -> p a o", p=128)),
                wait=w, inc=bd.GD, inc_k=16)
            ph.on("gpsimd", lambda e, i=i: e.dma_start(
                out=_full(wb[i % 2]),
                in_=io["w2T"][:, i * 128:(i + 1) * 128].rearrange("(a p) o -> p a o", p=128)),
                inc=bd.GD, inc_k=16)
            gd_marks.append(bd.GD.n)
        ev_marks = []
        for i in range(32):
            for nt in range(2):
                j = nt
                sl = slice(nt * 512, (nt + 1) * 512)
                wq = [(bd.GD, gd_marks[i]), xn_ready]
                idx = len(ev_marks) - 2
                if idx >= 0:
                    wq.append((bd.SE, ev_marks[idx][0]))
                    wq.append((bd.VE, ev_marks[idx][1]))
                for k in range(8):
                    ph.on("tensor", lambda e, i=i, k=k, sl=sl, j=j: e.matmul(
                        psA[j][:, :], wa[i % 2][:, k, :], xnp[:, k, sl],
                        start=(k == 0), stop=(k == 7)), wait=wq if k == 0 else [], inc=None)
                for k in range(8):
                    last = (k == 7)
                    ph.on("tensor", lambda e, i=i, k=k, sl=sl, j=j: e.matmul(
                        psB[j][:, :], wb[i % 2][:, k, :], xnp[:, k, sl],
                        start=(k == 0), stop=(k == 7)),
                        inc=(bd.MM if last else None))
                mmn = bd.MM.n
                if nt == 1:
                    ph.on("tensor", lambda e: e.nop(), wait=[(bd.MM, mmn)], inc=bd.WF, inc_k=2)
                ph.on("scalar", lambda e, i=i, j=j: e.activation(
                    ea[j][:, :], psA[j][:, :], AF.Silu, scale=sb["s1c"][:, i:i + 1]),
                    wait=[(bd.MM, mmn)], inc=bd.SE)
                ph.on("vector", lambda e, i=i, j=j: e.tensor_scalar(
                    eb[j][:, :], psB[j][:, :], sb["s2c"][:, i:i + 1], None, ALU.mult),
                    wait=[(bd.MM, mmn)], inc=bd.VE)
                ph.on("vector", lambda e, i=i, j=j, sl=sl: e.tensor_mul(
                    h1[:, i, sl], ea[j][:, :], eb[j][:, :]),
                    wait=[(bd.SE, bd.SE.n), (bd.VE, bd.VE.n)], inc=bd.VE)
                ev_marks.append((bd.SE.n, bd.VE.n))
        h_done = (bd.VE, bd.VE.n)

        for (wTn, scn, srcb, dstb) in (("h1T", "sh1c", h1, h2), ("h2T", "sh2c", h2, h1)):
            ph.barrier()
            gd_marks = []
            wf_base = bd.WF.n
            for i in range(32):
                w = [(bd.WF, wf_base + (i - 1))] if i >= 2 else []
                ph.on("gpsimd", lambda e, i=i, wTn=wTn: e.dma_start(
                    out=_full(wh[i % 2]),
                    in_=io[wTn][:, i * 128:(i + 1) * 128].rearrange("(a p) o -> p a o", p=128)),
                    wait=w, inc=bd.GD, inc_k=16)
                gd_marks.append(bd.GD.n)
            se_marks = []
            for i in range(32):
                for nt in range(2):
                    j = nt
                    sl = slice(nt * 512, (nt + 1) * 512)
                    wq = [(bd.GD, gd_marks[i]), h_done]
                    idx = len(se_marks) - 2
                    if idx >= 0:
                        wq.append((bd.SE, se_marks[idx]))
                    for k in range(32):
                        last = (k == 31)
                        ph.on("tensor", lambda e, i=i, k=k, sl=sl, j=j, srcb=srcb: e.matmul(
                            psA[j][:, :], wh[i % 2][:, k, :], srcb[:, k, sl],
                            start=(k == 0), stop=(k == 31)),
                            wait=wq if k == 0 else [], inc=(bd.MM if last else None))
                    mmn = bd.MM.n
                    if nt == 1:
                        ph.on("tensor", lambda e: e.nop(), wait=[(bd.MM, mmn)], inc=bd.WF)
                    ph.on("scalar", lambda e, i=i, j=j, sl=sl, scn=scn, dstb=dstb: e.activation(
                        dstb[:, i, sl], psA[j][:, :], AF.Gelu, scale=sb[scn][:, i:i + 1]),
                        wait=[(bd.MM, mmn)], inc=bd.SE)
                    se_marks.append(bd.SE.n)
            h_done = (bd.SE, bd.SE.n)

        ph.barrier()
        gd_marks = []
        wf_base = bd.WF.n
        for i in range(8):
            w = [(bd.WF, wf_base + (i - 1))] if i >= 2 else []
            ph.on("gpsimd", lambda e, i=i: e.dma_start(
                out=_full(wh[i % 2]),
                in_=io["w3T"][:, i * 128:(i + 1) * 128].rearrange("(a p) o -> p a o", p=128)),
                wait=w, inc=bd.GD, inc_k=16)
            gd_marks.append(bd.GD.n)
        for i in range(8):
            st_prev = bd.AD.n
            for nt in range(2):
                j = nt
                sl = slice(nt * 512, (nt + 1) * 512)
                wq = [(bd.GD, gd_marks[i]), h_done]
                if i > 0:
                    wq.append((bd.AD, st_prev))
                for k in range(32):
                    last = (k == 31)
                    ph.on("tensor", lambda e, i=i, k=k, sl=sl, j=j: e.matmul(
                        psA[j][:, :], wh[i % 2][:, k, :], h1[:, k, sl],
                        start=(k == 0), stop=(k == 31)),
                        wait=wq if k == 0 else [], inc=(bd.MM if last else None))
                mmn = bd.MM.n
                if nt == 1:
                    ph.on("tensor", lambda e: e.nop(), wait=[(bd.MM, mmn)], inc=bd.WF)
                ph.on("scalar", lambda e, i=i, j=j, sl=sl: e.activation(
                    go[:, sl], psA[j][:, :], AF.Silu, scale=sb["s3c"][:, i:i + 1]),
                    wait=[(bd.MM, mmn)] + ([(bd.AD, st_prev)] if i > 0 else []), inc=bd.SE)
            ph.on("sync", lambda e, i=i, p=p: e.dma_start(
                out=io["g0"][i * 128:(i + 1) * 128, p * TP:(p + 1) * TP], in_=go[:, :]),
                wait=[(bd.SE, bd.SE.n)], inc=bd.AD, inc_k=16)
    ph.emit()
    st2.close()
    bd.ev["g0"] = [(bd.AD, bd.AD.n)]


def emit_proj_out(bd, y_name, wT_name, n_f, bias_tile_name, s_sb):
    nc, io, sb = bd.nc, bd.io, bd.sb

    def ysrc(fc):
        if y_name == "y0in":
            return io[f"y0outp{fc % 4}"], (fc // 4) * 128
        if fc < 8:
            return io[f"y1outp{fc // 2}"], (fc % 2) * 128
        return io[f"y1outp{(fc - 8) // 2}"], 256 + ((fc - 8) % 2) * 128

    ph = Ph(bd)
    yw = []
    for fc in range(n_f):
        t, _ = ysrc(fc)
        yw += bd.ev.get(t.name, [])
    ph.barrier(yw + bd.snapshot())
    st2 = ExitStack()
    zt = bd.tag()
    yb = st2.enter_context(nc.sbuf_tensor(zt + "po_y", [128, n_f, TM], BF16))
    wt = [st2.enter_context(nc.sbuf_tensor(zt + f"po_w{j}", [128, n_f, 128], BF16)) for j in range(2)]
    ps = [st2.enter_context(nc.psum_tensor(zt + f"po_ps{j}", [128, 512], F32)) for j in range(2)]

    def yload(e, fc):
        t, rb = ysrc(fc)
        with e.register(zt + f"ro{fc}") as r:
            e.reg_load(r, sb["coloff"][0:1, 0:1])
            e.reg_alu(r, r, rb * TF, ALU.add)
            return e.dma_start(out=yb[:, fc, :], in_=bass.AP(t, r, [[TF, 128], [1, TM]]))

    for fc in range(n_f):
        ph.on("sync", lambda e, fc=fc: yload(e, fc), inc=bd.AD, inc_k=16)
    y_ready = (bd.AD, bd.AD.n)
    gd_marks = []
    wf_base = bd.WF.n
    for i in range(8):
        w = [(bd.WF, wf_base + (i - 1))] if i >= 2 else []
        ph.on("gpsimd", lambda e, i=i: e.dma_start(
            out=_full(wt[i % 2]),
            in_=io[wT_name][:, i * 128:(i + 1) * 128].rearrange("(a p) o -> p a o", p=128)),
            wait=w, inc=bd.GD, inc_k=16)
        gd_marks.append(bd.GD.n)
    ev_marks = []
    for i in range(8):
        for nt in range(4):
            j = nt % 2
            sl = slice(nt * 512, (nt + 1) * 512)
            wq = [(bd.GD, gd_marks[i]), y_ready]
            idx = len(ev_marks) - 2
            if idx >= 0:
                wq.append((bd.SE, ev_marks[idx]))
            for k in range(n_f):
                last = (k == n_f - 1)
                ph.on("tensor", lambda e, i=i, k=k, sl=sl, j=j: e.matmul(
                    ps[j][:, :], wt[i % 2][:, k, :], yb[:, k, sl],
                    start=(k == 0), stop=(k == n_f - 1)),
                    wait=wq if k == 0 else [], inc=(bd.MM if last else None))
            mmn = bd.MM.n
            if nt == 3:
                ph.on("tensor", lambda e: e.nop(), wait=[(bd.MM, mmn)], inc=bd.WF)
            if bias_tile_name is not None:
                ph.on("vector", lambda e, i=i, j=j, sl=sl: e.tensor_scalar(
                    s_sb[:, i, sl], ps[j][:, :], sb[bias_tile_name][:, i:i + 1], None, ALU.add),
                    wait=[(bd.MM, mmn)], inc=bd.SE)
            else:
                ph.on("scalar", lambda e, i=i, j=j, sl=sl: e.activation(
                    s_sb[:, i, sl], ps[j][:, :], AF.Copy),
                    wait=[(bd.MM, mmn)], inc=bd.SE)
            ev_marks.append(bd.SE.n)
    ph.emit()
    st2.close()
    return (bd.SE, bd.SE.n)


A0_DEBUG = False


def emit_attn0(bd):
    nc, io, sb = bd.nc, bd.io, bd.sb
    dbg = {}
    if A0_DEBUG:
        for nm, shp, dt in (("dbg_qf", [128, TF], BF16), ("dbg_kf", [128, TF], BF16),
                            ("dbg_vt", [128, 32 * 130], BF16), ("dbg_qp1", [64, TF], BF16),
                            ("dbg_kpt", [128, 32 * 64], BF16), ("dbg_kv", [64, 65], BF16),
                            ("dbg_nm", [128, 8], F32), ("dbg_bq2", [128, 32], F32), ("dbg_qmx", [128, 32], F32), ("dbg_qsd", [128, 32], F32), ("dbg_qd", [128, 32 * 64], F32)):
            dbg[nm] = nc.dram_tensor(nm, shp, dt, kind="ExternalOutput")
    ph = Ph(bd)
    ph.barrier(bd.ev.get("ag0out", []) + bd.snapshot())
    st2 = ExitStack()
    xnf = st2.enter_context(nc.sbuf_tensor("a0_xn", [128, 8, TF], BF16))
    qf = st2.enter_context(nc.sbuf_tensor("a0_qf", [128, TF], BF16))
    kf = st2.enter_context(nc.sbuf_tensor("a0_kf", [128, TF], BF16))
    vt = st2.enter_context(nc.sbuf_tensor("a0_vt", [128, 32, 130], BF16))
    wq = st2.enter_context(nc.sbuf_tensor("a0_wq", [128, 8, 128], BF16))
    wk = st2.enter_context(nc.sbuf_tensor("a0_wk", [128, 8, 128], BF16))
    wv = st2.enter_context(nc.sbuf_tensor("a0_wv", [128, 8, 128], BF16))
    sq2 = st2.enter_context(nc.sbuf_tensor("a0_sq2", [128, 1024], BF16))
    nm = st2.enter_context(nc.sbuf_tensor("a0_nm", [128, 8], F32))
    kd = st2.enter_context(nc.sbuf_tensor("a0_kd", [128, 32, 64], F32))
    qd = st2.enter_context(nc.sbuf_tensor("a0_qd", [128, 32, 64], F32))
    ksd = st2.enter_context(nc.sbuf_tensor("a0_ksd", [128, 36], F32))
    qsd = st2.enter_context(nc.sbuf_tensor("a0_qsd", [128, 32], F32))
    qmx = st2.enter_context(nc.sbuf_tensor("a0_qmx", [128, 32], F32))
    kmx = st2.enter_context(nc.sbuf_tensor("a0_kmx", [128, 32], F32))
    bq2 = st2.enter_context(nc.sbuf_tensor("a0_bq2", [128, 32], F32))
    bk2 = st2.enter_context(nc.sbuf_tensor("a0_bk2", [128, 32], F32))
    qkvs = st2.enter_context(nc.sbuf_tensor("a0_qkvs", [128, 32, 65], F32))
    tscr = st2.enter_context(nc.sbuf_tensor("a0_tscr", [128, 128], F32))
    grow = st2.enter_context(nc.sbuf_tensor("a0_grow", [1, 2], F32))
    td2 = [st2.enter_context(nc.sbuf_tensor(f"a0_td{j}", [128, 64], BF16)) for j in range(2)]
    qp1 = st2.enter_context(nc.sbuf_tensor("a0_qp1", [64, TF], BF16))
    kpt = st2.enter_context(nc.sbuf_tensor("a0_kpt", [128, 32, 64], BF16))
    kvsb = st2.enter_context(nc.sbuf_tensor("a0_kv", [64, 65], BF16))
    bvb = st2.enter_context(nc.sbuf_tensor("a0_bvb", [128, 128], F32))
    yf = st2.enter_context(nc.sbuf_tensor("a0_yf", [128, TF], BF16))

    ps1 = st2.enter_context(nc.psum_tensor("a0_ps1", [128, 512], F32))
    ps2 = st2.enter_context(nc.psum_tensor("a0_ps2", [128, 512], F32))
    psg = st2.enter_context(nc.psum_tensor("a0_psg", [128, 128], F32))
    psd = st2.enter_context(nc.psum_tensor("a0_psd", [128, 512], F32))
    psk = st2.enter_context(nc.psum_tensor("a0_psk", [128, 80], F32))
    pst = st2.enter_context(nc.psum_tensor("a0_pst", [128, 128], BF16))

    for k in range(8):
        ph.on("sync", lambda e, k=k: e.dma_start(
            out=xnf[:, k, 0:TM], in_=io["ag0out"][k * 128:(k + 1) * 128, :]),
            inc=bd.AD, inc_k=16)
        ph.on("sync", lambda e, k=k: e.dma_start(
            out=xnf[:, k, TM:TF], in_=io["ag0out"][D + k * 128:D + (k + 1) * 128, :]),
            inc=bd.AD, inc_k=16)
    xn_ready = (bd.AD, bd.AD.n)

    for hp in range(4):
        ph.barrier()
        for wn, wsb in (("l0_wqT", wq), ("l0_wkT", wk), ("l0_wvT", wv)):
            ph.on("gpsimd", lambda e, wn=wn, wsb=wsb, hp=hp: e.dma_start(
                out=_full(wsb),
                in_=io[wn][:, hp * 128:(hp + 1) * 128].rearrange("(a p) o -> p a o", p=128)),
                inc=bd.GD, inc_k=16)
        w_ready = (bd.GD, bd.GD.n)
        ph.on("vector", lambda e: e.memset(vt[:, :, 64:65], 1.0), inc=bd.VE)
        ph.on("vector", lambda e: e.memset(vt[:, :, 129:130], 1.0), inc=bd.VE)
        ph.on("tensor", lambda e, hp=hp: e.matmul(
            psg[:, :], sb["ones_f"][0:1, 0:128], sb["l0_bvr"][0:1, hp * 128:(hp + 1) * 128],
            start=True, stop=True), inc=bd.MM)
        ph.on("vector", lambda e: e.tensor_copy(bvb[:, :], psg[:, :]), wait=[(bd.MM, bd.MM.n)], inc=bd.VE)

        for (wsb, dst, bcoln) in ((wq, qf, "l0_bq"), (wk, kf, "l0_bk")):
            for nt in range(8):
                sl = slice(nt * 512, (nt + 1) * 512)
                pj = ps1 if nt % 2 == 0 else ps2
                wt = [w_ready, xn_ready, (bd.SE, bd.SE.n)]
                for k in range(8):
                    ph.on("tensor", lambda e, k=k, sl=sl, pj=pj, wsb=wsb: e.matmul(
                        pj[:, :], wsb[:, k, :], xnf[:, k, sl],
                        start=(k == 0), stop=(k == 7)),
                        wait=wt if k == 0 else [], inc=(bd.MM if k == 7 else None))
                ph.on("vector", lambda e, sl=sl, pj=pj, dst=dst, bcoln=bcoln, hp=hp: e.tensor_scalar(
                    dst[:, sl], pj[:, :], sb[bcoln][:, hp:hp + 1], None, ALU.add),
                    wait=[(bd.MM, bd.MM.n)], inc=bd.SE)
        for c in range(32):
            wt = [w_ready, xn_ready, (bd.VE, bd.VE.n)]
            for k in range(8):
                ph.on("tensor", lambda e, c=c, k=k: e.matmul(
                    psg[:, :], xnf[:, k, c * 128:(c + 1) * 128], wv[:, k, :],
                    start=(k == 0), stop=(k == 7)),
                    wait=wt if k == 0 else [], inc=(bd.MM if k == 7 else None))
            mmn = bd.MM.n
            ph.on("vector", lambda e, c=c: e.tensor_tensor(
                vt[:, c, 0:64], psg[:, 0:64], bvb[:, 0:64], ALU.add),
                wait=[(bd.MM, mmn)], inc=None)
            ph.on("vector", lambda e, c=c: e.tensor_tensor(
                vt[:, c, 65:129], psg[:, 64:128], bvb[:, 64:128], ALU.add),
                inc=bd.VE)
        proj_done = bd.snapshot()

        for hl in range(2):
            rows = slice(hl * 64, (hl + 1) * 64)
            ph.barrier(proj_done)
            for (srcf, dstd, dsts, dstm) in ((qf, qd, qsd, qmx), (kf, kd, ksd, kmx)):
                for g in range(4):
                    ph.on("scalar", lambda e, g=g, rows=rows, srcf=srcf: e.activation(
                        sq2[rows, :], srcf[rows, g * 1024:(g + 1) * 1024], AF.Square),
                        wait=[(bd.MM, bd.MM.n)], inc=bd.SE)
                    sqd = bd.SE.n
                    for c8 in range(8):
                        c = g * 8 + c8
                        cs = slice(c * 128, (c + 1) * 128)
                        ph.on("tensor", lambda e, cs=cs, rows=rows, hp=hp, srcf=srcf, c8=c8: e.matmul(
                            psd[:, c8 * 64:(c8 + 1) * 64], srcf[rows, cs], sb["projT"][rows, hp, :],
                            start=(c8 == 0), stop=(c8 == 7), skip_group_check=True),
                            wait=([(bd.VE, bd.VE.n), (bd.SE, sqd)] if c8 == 0 else []), inc=None)
                    for c8 in range(8):
                        ph.on("tensor", lambda e, rows=rows, c8=c8: e.matmul(
                            psk[:, c8:c8 + 1], sq2[rows, c8 * 128:(c8 + 1) * 128],
                            sb["ones_b"][rows, 0:1],
                            start=(c8 == 0), stop=(c8 == 7), skip_group_check=True),
                            inc=(bd.MM if c8 == 7 else None))
                    mms = bd.MM.n
                    ph.on("vector", lambda e, g=g, dstd=dstd: e.tensor_copy(
                        dstd[:, g * 8:(g + 1) * 8, :].rearrange("p a b -> p (a b)"),
                        psd[:, 0:512]),
                        wait=[(bd.MM, mms)], inc=None)
                    ph.on("vector", lambda e, g=g, dsts=dsts: e.tensor_copy(
                        dsts[:, g * 8:(g + 1) * 8], psk[:, 0:8]), inc=None)
                    ph.on("vector", lambda e, g=g, dstm=dstm: e.tensor_reduce(
                        dstm[:, g * 8:(g + 1) * 8],
                        psd[:, 0:512].rearrange("p (a b) -> p a b", a=8), AX.X, ALU.max),
                        inc=bd.VE)
            ph.on("vector", lambda e: e.tensor_scalar(
                bq2[:, :], qmx[:, :], _c(-C0), _c(LN_RATIO), ALU.mult, ALU.add),
                wait=[(bd.VE, bd.VE.n)], inc=bd.VE)
            ph.on("vector", lambda e: e.scalar_tensor_tensor(
                bq2[:, :], qsd[:, :], _c(-0.5 * C0 * C0), bq2[:, :], ALU.mult, ALU.add),
                wait=[(bd.VE, bd.VE.n)], inc=bd.VE)
            bq_done = bd.VE.n
            ph.on("vector", lambda e: e.tensor_reduce(
                tscr[:, 0:1], kmx[:, :], AX.X, ALU.max),
                wait=[(bd.VE, bd.VE.n)], inc=bd.VE)
            ph.on("tensor", lambda e: e.transpose(
                psg[:, :], tscr[:, :], sb["identf"][:, :]),
                wait=[(bd.VE, bd.VE.n)], inc=bd.MM)
            ph.on("vector", lambda e: e.tensor_reduce(
                grow[0:1, 0:1], psg[0:1, 0:128], AX.X, ALU.max),
                wait=[(bd.MM, bd.MM.n)], inc=bd.VE)
            ph.on("tensor", lambda e: e.matmul(
                psk[:, 8:9], sb["ones_f"][0:1, 0:128], grow[0:1, 0:1], start=True, stop=True),
                wait=[(bd.VE, bd.VE.n)], inc=bd.MM)
            ph.on("vector", lambda e: e.tensor_scalar(
                nm[:, 5:6], psk[:, 8:9], _c(-C0), _c(LN_RATIO), ALU.mult, ALU.add),
                wait=[(bd.MM, bd.MM.n)], inc=bd.VE)
            ph.on("vector", lambda e: e.tensor_scalar(
                bk2[:, :], ksd[:, 0:32], _c(-0.5 * C0 * C0), None, ALU.mult),
                wait=[(bd.VE, bd.VE.n)], inc=bd.VE)
            ph.on("vector", lambda e: e.tensor_scalar(
                bk2[:, :], bk2[:, :], nm[:, 5:6], None, ALU.add),
                wait=[(bd.VE, bd.VE.n)], inc=bd.VE)
            bk_done = bd.VE.n
            pendt = None
            for c in range(32):
                cs = slice(c * 128, (c + 1) * 128)
                par = c % 2
                w = [(bd.VE, bq_done)]
                if c >= 2:
                    w.append((bd.MM, bd.MM.n))
                ph.on("scalar", lambda e, c=c, par=par: e.activation(
                    td2[par][:, 0:64], qd[:, c, :], AF.Exp, bias=bq2[:, c:c + 1], scale=_c(C0)),
                    wait=w, inc=bd.SE)
                sen = bd.SE.n
                if pendt is not None:
                    ph.on("tensor", lambda e, pendt=pendt: e.transpose(
                        pst[0:64, 0:128], td2[pendt[1]][0:128, 0:64], sb["ident"][:, :]),
                        wait=[(bd.SE, pendt[2]), (bd.VE, bd.VE.n)], inc=bd.MM)
                    ph.on("vector", lambda e, pendt=pendt: e.tensor_scalar(
                        qp1[0:64, pendt[0] * 128:(pendt[0] + 1) * 128], pst[0:64, 0:128],
                        _c(EPS_FEAT), None, ALU.add),
                        wait=[(bd.MM, bd.MM.n)], inc=bd.VE)
                pendt = (c, par, sen)
            ph.on("tensor", lambda e, pendt=pendt: e.transpose(
                pst[0:64, 0:128], td2[pendt[1]][0:128, 0:64], sb["ident"][:, :]),
                wait=[(bd.SE, pendt[2]), (bd.VE, bd.VE.n)], inc=bd.MM)
            ph.on("vector", lambda e, pendt=pendt: e.tensor_scalar(
                qp1[0:64, pendt[0] * 128:(pendt[0] + 1) * 128], pst[0:64, 0:128],
                _c(EPS_FEAT), None, ALU.add),
                wait=[(bd.MM, bd.MM.n)], inc=bd.VE)
            for c in range(32):
                par = c % 2
                w = [(bd.VE, bk_done)]
                if c < 2:
                    w.append((bd.MM, bd.MM.n))
                else:
                    w.append((bd.VE, bd.VE.n))
                ph.on("scalar", lambda e, c=c, par=par: e.activation(
                    td2[par][:, 0:64], kd[:, c, :], AF.Exp, bias=bk2[:, c:c + 1], scale=_c(C0)),
                    wait=w, inc=bd.SE)
                ph.on("vector", lambda e, c=c, par=par: e.tensor_scalar(
                    kpt[:, c, :], td2[par][:, 0:64], _c(EPS_FEAT), None, ALU.add),
                    wait=[(bd.SE, bd.SE.n)], inc=bd.VE)
            kv_w = [(bd.VE, bd.VE.n)]
            for c in range(32):
                ph.on("tensor", lambda e, c=c, hl=hl: e.matmul(
                    psk[0:64, 8:73], kpt[:, c, :], vt[:, c, hl * 65:hl * 65 + 65],
                    start=(c == 0), stop=(c == 31)),
                    wait=kv_w if c == 0 else [], inc=(bd.MM if c == 31 else None))
            ph.on("vector", lambda e: e.tensor_copy(kvsb[0:64, :], psk[0:64, 8:73]),
                  wait=[(bd.MM, bd.MM.n)], inc=bd.VE)
            kv_done = (bd.VE, bd.VE.n)
            for g in range(8):
                for c4 in range(4):
                    c = g * 4 + c4
                    cs = slice(c * 128, (c + 1) * 128)
                    ph.on("tensor", lambda e, cs=cs, c4=c4: e.matmul(
                        psd[:, c4 * 65:c4 * 65 + 65], qp1[0:64, cs], kvsb[0:64, 0:65],
                        start=(c4 == 0), stop=(c4 == 3), skip_group_check=True),
                        wait=([kv_done, (bd.VE, bd.VE.n)] if c4 == 0 else []),
                        inc=(bd.MM if c4 == 3 else None))
                ph.on("vector", lambda e, g=g: e.tensor_copy(
                    qkvs[:, g * 4:(g + 1) * 4, :].rearrange("p a b -> p (a b)"), psd[:, 0:260]),
                    wait=[(bd.MM, bd.MM.n)], inc=bd.VE)
            ph.on("vector", lambda e: e.tensor_scalar(
                qmx[:, :], qkvs[:, :, 64], _c(EPS_DEN), None, ALU.max),
                wait=[(bd.VE, bd.VE.n)], inc=bd.VE)
            ph.on("vector", lambda e: e.reciprocal(qmx[:, :], qmx[:, :]),
                  wait=[(bd.VE, bd.VE.n)], inc=bd.VE)
            r_done = bd.VE.n
            pendt = None
            for c in range(32):
                par = c % 2
                w = [(bd.VE, r_done)]
                if c >= 2:
                    w.append((bd.MM, bd.MM.n))
                ph.on("vector", lambda e, c=c, par=par: e.tensor_scalar(
                    td2[par][:, 0:64], qkvs[:, c, 0:64], qmx[:, c:c + 1], None, ALU.mult),
                    wait=w, inc=bd.VE)
                ven = bd.VE.n
                if pendt is not None:
                    ph.on("tensor", lambda e, pendt=pendt: e.transpose(
                        pst[0:64, 0:128], td2[pendt[1]][0:128, 0:64], sb["ident"][:, :]),
                        wait=[(bd.VE, pendt[2]), (bd.VE, bd.VE.n)], inc=bd.MM)
                    ph.on("vector", lambda e, pendt=pendt, rows=rows: e.tensor_copy(
                        yf[rows, pendt[0] * 128:(pendt[0] + 1) * 128], pst[0:64, 0:128]),
                        wait=[(bd.MM, bd.MM.n)], inc=bd.VE)
                pendt = (c, par, ven)
            ph.on("tensor", lambda e, pendt=pendt: e.transpose(
                pst[0:64, 0:128], td2[pendt[1]][0:128, 0:64], sb["ident"][:, :]),
                wait=[(bd.VE, pendt[2]), (bd.VE, bd.VE.n)], inc=bd.MM)
            ph.on("vector", lambda e, pendt=pendt, rows=rows: e.tensor_copy(
                yf[rows, pendt[0] * 128:(pendt[0] + 1) * 128], pst[0:64, 0:128]),
                wait=[(bd.MM, bd.MM.n)], inc=bd.VE)
        ph.on("sync", lambda e, hp=hp: e.dma_start(
            out=io["y0in"][hp * 128:(hp + 1) * 128, :], in_=yf[:, :]),
            wait=[(bd.VE, bd.VE.n)], inc=bd.AD, inc_k=16)
        bd.ev[f"y0p{hp}"] = [(bd.AD, bd.AD.n)]
        if A0_DEBUG and hp == 0:
            ph.barrier()
            for dnm, src in (("dbg_qf", qf[:, :]), ("dbg_kf", kf[:, :]),
                             ("dbg_vt", vt[:, :, :].rearrange("p a b -> p (a b)")),
                             ("dbg_qp1", qp1[0:64, :]),
                             ("dbg_kpt", kpt[:, :, :].rearrange("p a b -> p (a b)")),
                             ("dbg_kv", kvsb[0:64, :]), ("dbg_nm", nm[:, :]), ("dbg_bq2", bq2[:, :]), ("dbg_qmx", qmx[:, :]), ("dbg_qsd", qsd[:, :]), ("dbg_qd", qd[:, :, :].rearrange("p a b -> p (a b)"))):
                ph.on("sync", lambda e, nm2=dnm, src=src: e.dma_start(
                    out=_full(dbg[nm2]), in_=src), inc=bd.AD, inc_k=16)
            ph.barrier()
    ph.emit()
    st2.close()
    bd.ev["y0in"] = [(bd.AD, bd.AD.n)]


MO_DEBUG = False


def emit_moe(bd):
    nc, io, sb = bd.nc, bd.io, bd.sb
    dbg = {}
    if MO_DEBUG:
        for nm2, shp, dt in (("dbgm_idxrow", [1, TM], BF16), ("dbgm_idxt", [128, 16], BF16),
                             ("dbgm_sel", [128, TM], BF16), ("dbgm_logits", [128, 128], BF16),
                             ("dbgm_xe", [128, TM], BF16), ("dbgm_macc", [128, TM], BF16)):
            dbg[nm2] = nc.dram_tensor(nm2, shp, dt, kind="ExternalOutput")
    ph = Ph(bd)
    ph.barrier(bd.ev.get("ag1in", []) + bd.snapshot())
    st2 = ExitStack()
    xnb = [st2.enter_context(nc.sbuf_tensor(f"mo_xn{j}", [128, TM], BF16)) for j in range(2)]
    rt = st2.enter_context(nc.sbuf_tensor("mo_rt", [128, 8, 8], BF16))
    idxt = st2.enter_context(nc.sbuf_tensor("mo_idxt", [128, 16], BF16))
    idxs = st2.enter_context(nc.sbuf_tensor("mo_idxs", [16, 128], BF16))
    idxrow = st2.enter_context(nc.sbuf_tensor("mo_idxrow", [1, TM], BF16))
    mcol = st2.enter_context(nc.sbuf_tensor("mo_mcol", [128, 16], F32))
    ge = st2.enter_context(nc.sbuf_tensor("mo_ge", [128, 128], F32))
    sel = st2.enter_context(nc.sbuf_tensor("mo_sel", [128, TM], BF16))
    xe = st2.enter_context(nc.sbuf_tensor("mo_xe", [128, 8, TM], BF16))
    msb = st2.enter_context(nc.sbuf_tensor("mo_m", [128, 22, TM], BF16))
    g1a = st2.enter_context(nc.sbuf_tensor("mo_g1", [128, 8, TM], BF16))
    eg = [st2.enter_context(nc.sbuf_tensor(f"mo_eg{j}", [128, 512], BF16)) for j in range(2)]
    eu = [st2.enter_context(nc.sbuf_tensor(f"mo_eu{j}", [128, 512], BF16)) for j in range(2)]
    wg = [st2.enter_context(nc.sbuf_tensor(f"mo_wg{j}", [128, 8, 128], BF16)) for j in range(2)]
    wu = [st2.enter_context(nc.sbuf_tensor(f"mo_wu{j}", [128, 8, 128], BF16)) for j in range(2)]
    wd = [st2.enter_context(nc.sbuf_tensor(f"mo_wd{j}", [128, 22, 128], BF16)) for j in range(2)]
    psr = st2.enter_context(nc.psum_tensor("mo_psr", [128, 128], F32))
    pss = st2.enter_context(nc.psum_tensor("mo_pss", [128, 512], F32))
    pstb = st2.enter_context(nc.psum_tensor("mo_pstb", [128, 128], BF16))
    psA = [st2.enter_context(nc.psum_tensor(f"mo_psA{j}", [128, 512], F32)) for j in range(2)]
    psB = [st2.enter_context(nc.psum_tensor(f"mo_psB{j}", [128, 512], F32)) for j in range(2)]

    ph.on("gpsimd", lambda e: e.dma_start(
        out=_full(rt), in_=io["routerT"].rearrange("(a p) o -> p a o", p=128)),
        inc=bd.GD, inc_k=16)
    rt_ready = (bd.GD, bd.GD.n)
    for k in range(8):
        ph.on("sync", lambda e, k=k: e.dma_start(
            out=xe[:, k, :], in_=io["ag1in"][k * 128:(k + 1) * 128, :]),
            inc=bd.AD, inc_k=16)
    xn_r = (bd.AD, bd.AD.n)
    for nch in range(16):
        for k in range(8):
            ph.on("tensor", lambda e, k=k, nch=nch: e.matmul(
                psr[:, nch * 8:(nch + 1) * 8], xe[:, k, nch * 128:(nch + 1) * 128],
                rt[:, k, :], start=(k == 0), stop=(k == 7),
                skip_group_check=True),
                wait=([rt_ready, xn_r] if (nch == 0 and k == 0) else []),
                inc=(bd.MM if (nch == 15 and k == 7) else None))
    mm_r = bd.MM.n
    for nch in range(16):
        ph.on("vector", lambda e, nch=nch: e.tensor_reduce(
            mcol[:, nch:nch + 1], psr[:, nch * 8:(nch + 1) * 8], AX.X, ALU.max),
            wait=[(bd.MM, mm_r)], inc=(bd.VE if nch == 15 else None))
    for nch in range(16):
        ph.on("vector", lambda e, nch=nch: e.tensor_scalar(
            ge[:, nch * 8:(nch + 1) * 8], psr[:, nch * 8:(nch + 1) * 8],
            mcol[:, nch:nch + 1], None, ALU.is_ge),
            wait=[(bd.VE, bd.VE.n)], inc=(bd.VE if nch == 15 else None))
    ph.on("vector", lambda e: e.tensor_mul(ge[:, :], ge[:, :], sb["iotam"][:, :]),
          wait=[(bd.VE, bd.VE.n)], inc=bd.VE)
    ph.on("vector", lambda e: e.tensor_scalar(ge[:, :], ge[:, :], _c(BIGI), None, ALU.add),
          wait=[(bd.VE, bd.VE.n)], inc=bd.VE)
    for nch in range(16):
        ph.on("vector", lambda e, nch=nch: e.tensor_reduce(
            idxt[:, nch:nch + 1], ge[:, nch * 8:(nch + 1) * 8], AX.X, ALU.min),
            wait=[(bd.VE, bd.VE.n)], inc=(bd.VE if nch == 15 else None))
    ph.on("tensor", lambda e: e.transpose(pstb[0:16, 0:128], idxt[:, :], sb["ident"][:, :]),
          wait=[(bd.VE, bd.VE.n)], inc=bd.MM)
    ph.on("vector", lambda e: e.tensor_copy(idxs[:, :], pstb[0:16, 0:128]),
          wait=[(bd.MM, bd.MM.n)], inc=bd.VE)
    ph.on("sync", lambda e: e.dma_start(
        out=io["idxd"][0:1, :].rearrange("a (p f) -> (a p) f", p=16), in_=idxs[:, :]),
        wait=[(bd.VE, bd.VE.n)], inc=bd.AD, inc_k=16)
    ph.on("sync", lambda e: e.dma_start(out=idxrow[0:1, :], in_=io["idxd"][0:1, :]),
          wait=[(bd.AD, bd.AD.n)], inc=bd.AD, inc_k=16)
    idx_ready = (bd.AD, bd.AD.n)
    if MO_DEBUG:
        ph.barrier()
        ph.on("vector", lambda e: e.tensor_copy(sel[:, 0:128], psr[:, :]), inc=bd.VE)
        ph.barrier()
        for nm2, srcap in (("dbgm_idxrow", idxrow[0:1, :]), ("dbgm_idxt", idxt[:, :]),
                           ("dbgm_logits", sel[:, 0:128])):
            ph.on("sync", lambda e, nm2=nm2, srcap=srcap: e.dma_start(
                out=_full(dbg[nm2]), in_=srcap), inc=bd.AD, inc_k=16)
        ph.barrier()

    for ex in range(8):
        ph.barrier()
        for nt in range(4):
            ph.on("tensor", lambda e, nt=nt: e.matmul(
                pss[:, :], sb["ones_b"][0:1, 0:128],
                idxrow[0:1, nt * 512:(nt + 1) * 512], start=True, stop=True),
                wait=[idx_ready, (bd.VE, bd.VE.n)], inc=bd.MM)
            ph.on("vector", lambda e, nt=nt, ex=ex: e.tensor_scalar(
                sel[:, nt * 512:(nt + 1) * 512], pss[:, :], float(ex), None, ALU.is_equal),
                wait=[(bd.MM, bd.MM.n)], inc=bd.VE)
        for k in range(8):
            ph.on("sync", lambda e, k=k: e.dma_start(
                out=xe[:, k, :], in_=io["ag1in"][k * 128:(k + 1) * 128, :]),
                wait=[(bd.MM, bd.MM.n)], inc=bd.AD, inc_k=16)
            ph.on("vector", lambda e, k=k: e.tensor_mul(xe[:, k, :], xe[:, k, :], sel[:, :]),
                  wait=[(bd.AD, bd.AD.n), (bd.VE, bd.VE.n)], inc=bd.VE)
        xe_ready = (bd.VE, bd.VE.n)
        if MO_DEBUG and ex == 2:
            ph.barrier()
            ph.on("sync", lambda e: e.dma_start(out=_full(dbg["dbgm_sel"]), in_=sel[:, :]),
                  inc=bd.AD, inc_k=16)
            ph.on("sync", lambda e: e.dma_start(out=_full(dbg["dbgm_xe"]), in_=xe[:, 0, :]),
                  inc=bd.AD, inc_k=16)
            ph.barrier()
        gd_marks = []
        wf_base = bd.WF.n
        for i in range(22):
            w = [(bd.WF, wf_base + 2 * (i - 1))] if i >= 2 else []
            ph.on("gpsimd", lambda e, i=i, ex=ex: e.dma_start(
                out=_full(wg[i % 2]),
                in_=io["gateT"][ex, :, i * 128:(i + 1) * 128].rearrange("(a p) o -> p a o", p=128)),
                wait=w, inc=bd.GD, inc_k=16)
            ph.on("gpsimd", lambda e, i=i, ex=ex: e.dma_start(
                out=_full(wu[i % 2]),
                in_=io["upT"][ex, :, i * 128:(i + 1) * 128].rearrange("(a p) o -> p a o", p=128)),
                inc=bd.GD, inc_k=16)
            gd_marks.append(bd.GD.n)
        ev_marks = []
        for i in range(22):
            for nt in range(4):
                j = nt % 2
                sl = slice(nt * 512, (nt + 1) * 512)
                wq = [(bd.GD, gd_marks[i]), xe_ready]
                idx = len(ev_marks) - 2
                if idx >= 0:
                    wq.append((bd.SE, ev_marks[idx][0]))
                    wq.append((bd.VE, ev_marks[idx][1]))
                for k in range(8):
                    ph.on("tensor", lambda e, i=i, k=k, sl=sl, j=j: e.matmul(
                        psA[j][:, :], wg[i % 2][:, k, :], xe[:, k, sl],
                        start=(k == 0), stop=(k == 7)),
                        wait=wq if k == 0 else [], inc=None)
                for k in range(8):
                    last = (k == 7)
                    ph.on("tensor", lambda e, i=i, k=k, sl=sl, j=j: e.matmul(
                        psB[j][:, :], wu[i % 2][:, k, :], xe[:, k, sl],
                        start=(k == 0), stop=(k == 7)),
                        inc=(bd.MM if last else None))
                mmn = bd.MM.n
                if nt == 3:
                    ph.on("tensor", lambda e: e.nop(), wait=[(bd.MM, mmn)], inc=bd.WF, inc_k=2)
                ph.on("scalar", lambda e, j=j: e.activation(
                    eg[j][:, :], psA[j][:, :], AF.Silu),
                    wait=[(bd.MM, mmn)], inc=bd.SE)
                ph.on("vector", lambda e, j=j: e.tensor_copy(eu[j][:, :], psB[j][:, :]),
                      wait=[(bd.MM, mmn)], inc=bd.VE)
                ph.on("vector", lambda e, i=i, j=j, sl=sl: e.tensor_mul(
                    msb[:, i, sl], eg[j][:, :], eu[j][:, :]),
                    wait=[(bd.SE, bd.SE.n), (bd.VE, bd.VE.n)], inc=bd.VE)
                ev_marks.append((bd.SE.n, bd.VE.n))
        m_ready = (bd.VE, bd.VE.n)
        gd_marks = []
        wf_base = bd.WF.n
        for i in range(8):
            w = [(bd.WF, wf_base + (i - 1))] if i >= 2 else []
            ph.on("gpsimd", lambda e, i=i, ex=ex: e.dma_start(
                out=_full(wd[i % 2]),
                in_=io["downT"][ex, :, i * 128:(i + 1) * 128].rearrange("(a p) o -> p a o", p=128)),
                wait=w, inc=bd.GD, inc_k=16)
            gd_marks.append(bd.GD.n)
        ve_marks = []
        for i in range(8):
            for nt in range(4):
                j = nt % 2
                sl = slice(nt * 512, (nt + 1) * 512)
                wq = [(bd.GD, gd_marks[i]), m_ready]
                idx = len(ve_marks) - 2
                if idx >= 0:
                    wq.append((bd.VE, ve_marks[idx]))
                for k in range(22):
                    last = (k == 21)
                    ph.on("tensor", lambda e, i=i, k=k, sl=sl, j=j: e.matmul(
                        psA[j][:, :], wd[i % 2][:, k, :], msb[:, k, sl],
                        start=(k == 0), stop=(k == 21)),
                        wait=wq if k == 0 else [], inc=(bd.MM if last else None))
                mmn = bd.MM.n
                if nt == 3:
                    ph.on("tensor", lambda e: e.nop(), wait=[(bd.MM, mmn)], inc=bd.WF)
                if ex == 0:
                    ph.on("vector", lambda e, i=i, sl=sl, j=j: e.tensor_copy(
                        g1a[:, i, sl], psA[j][:, :]),
                        wait=[(bd.MM, mmn)], inc=bd.VE)
                else:
                    ph.on("vector", lambda e, i=i, sl=sl, j=j: e.tensor_add(
                        g1a[:, i, sl], g1a[:, i, sl], psA[j][:, :]),
                        wait=[(bd.MM, mmn)], inc=bd.VE)
                ve_marks.append(bd.VE.n)
    ph.barrier()
    for i in range(8):
        ph.on("scalar", lambda e, i=i: e.activation(g1a[:, i, :], g1a[:, i, :], AF.Silu),
              inc=bd.SE)
        ph.on("sync", lambda e, i=i: e.dma_start(
            out=io["g1"][i * 128:(i + 1) * 128, :], in_=g1a[:, i, :]),
            wait=[(bd.SE, bd.SE.n)], inc=bd.AD, inc_k=16)
    ph.emit()
    st2.close()
    bd.ev["g1"] = [(bd.AD, bd.AD.n)]


def emit_attn1(bd):
    nc, io, sb = bd.nc, bd.io, bd.sb
    ph = Ph(bd)
    ph.barrier(bd.ev.get("ag1out", []) + bd.snapshot())
    st2 = ExitStack()
    xnf = st2.enter_context(nc.sbuf_tensor("a1_xn", [128, 8, TF], BF16))
    qk = {}
    for nmx in ("q1", "q2", "k1", "k2"):
        qk[nmx] = st2.enter_context(nc.sbuf_tensor(f"a1_{nmx}", [128, TF], BF16))
    vt = st2.enter_context(nc.sbuf_tensor("a1_vt", [128, 32, 260], BF16))
    wt = st2.enter_context(nc.sbuf_tensor("a1_wt", [128, 8, 128], BF16))
    wv = st2.enter_context(nc.sbuf_tensor("a1_wv", [128, 8, 256], BF16))
    sqv = st2.enter_context(nc.sbuf_tensor("a1_sq", [128, 512], F32))
    sqb = st2.enter_context(nc.sbuf_tensor("a1_sqb", [128, TF], BF16))
    nrm = st2.enter_context(nc.sbuf_tensor("a1_nrm", [1, 16], F32))
    ccol = st2.enter_context(nc.sbuf_tensor("a1_ccol", [128, 4], F32))
    lcol = st2.enter_context(nc.sbuf_tensor("a1_lcol", [128, 2], F32))
    ptb = [st2.enter_context(nc.sbuf_tensor(f"a1_ptb{j}", [128, 512], BF16)) for j in range(3)]
    smm = [st2.enter_context(nc.sbuf_tensor(f"a1_smm{j}", [128, 512], F32)) for j in range(3)]
    evs = st2.enter_context(nc.sbuf_tensor("a1_evs", [128, 48], F32))
    yc = [st2.enter_context(nc.sbuf_tensor(f"a1_yc{q}", [128, 128], F32)) for q in range(4)]
    ycb = [st2.enter_context(nc.sbuf_tensor(f"a1_ycb{q}", [128, 128], F32)) for q in range(4)]
    yc2a = st2.enter_context(nc.sbuf_tensor("a1_yc2a", [128, 512], F32))
    ytda = st2.enter_context(nc.sbuf_tensor("a1_ytda", [128, TF], BF16))
    yfh = [st2.enter_context(nc.sbuf_tensor(f"a1_yf{hl}", [128, TF], BF16)) for hl in range(2)]
    psS = [st2.enter_context(nc.psum_tensor(f"a1_psS{j}", [128, 512], F32)) for j in range(3)]
    psY = [st2.enter_context(nc.psum_tensor(f"a1_psY{j}", [128, 512], F32)) for j in range(4)]
    
    psTb = st2.enter_context(nc.psum_tensor("a1_psTb", [128, 256], BF16))

    for k in range(8):
        ph.on("sync", lambda e, k=k: e.dma_start(
            out=xnf[:, k, 0:TM], in_=io["ag1out"][k * 128:(k + 1) * 128, :]),
            inc=bd.AD, inc_k=16)
        ph.on("sync", lambda e, k=k: e.dma_start(
            out=xnf[:, k, TM:TF], in_=io["ag1out"][D + k * 128:D + (k + 1) * 128, :]),
            inc=bd.AD, inc_k=16)
    xn_ready = (bd.AD, bd.AD.n)

    for hp in range(4):
        ph.barrier()
        for nmx, wn in (("q1", "wq1T"), ("q2", "wq2T"), ("k1", "wk1T"), ("k2", "wk2T")):
            ph.on("gpsimd", lambda e, wn=wn, hp=hp: e.dma_start(
                out=_full(wt),
                in_=io[wn][:, hp * 128:(hp + 1) * 128].rearrange("(a p) o -> p a o", p=128)),
                wait=[(bd.MM, bd.MM.n)], inc=bd.GD, inc_k=16)
            gdn = bd.GD.n
            for nt in range(8):
                sl = slice(nt * 512, (nt + 1) * 512)
                pj = psS[nt % 2]
                wq2 = [(bd.GD, gdn), xn_ready, (bd.VE, bd.VE.n)]
                for k in range(8):
                    ph.on("tensor", lambda e, k=k, sl=sl, pj=pj: e.matmul(
                        pj[:, :], wt[:, k, :], xnf[:, k, sl],
                        start=(k == 0), stop=(k == 7)),
                        wait=wq2 if k == 0 else [], inc=(bd.MM if k == 7 else None))
                ph.on("vector", lambda e, sl=sl, pj=pj, nmx=nmx: e.tensor_copy(
                    qk[nmx][:, sl], pj[:, :]),
                    wait=[(bd.MM, bd.MM.n)], inc=bd.VE)
        ph.on("gpsimd", lambda e, hp=hp: e.dma_start(
            out=_full(wv),
            in_=io["wvT"][:, hp * 256:(hp + 1) * 256].rearrange("(a p) o -> p a o", p=128)),
            wait=[(bd.MM, bd.MM.n)], inc=bd.GD, inc_k=16)
        gdv = bd.GD.n
        ph.on("vector", lambda e: e.memset(vt[:, :, 128:129], 1.0), inc=bd.VE)
        ph.on("vector", lambda e: e.memset(vt[:, :, 258:259], 1.0), inc=bd.VE)
        for c in range(32):
            pj = psS[c % 2]
            wq2 = [(bd.GD, gdv), xn_ready, (bd.VE, bd.VE.n)]
            for k in range(8):
                ph.on("tensor", lambda e, c=c, k=k, pj=pj: e.matmul(
                    pj[:, 0:256], xnf[:, k, c * 128:(c + 1) * 128], wv[:, k, :],
                    start=(k == 0), stop=(k == 7)),
                    wait=wq2 if k == 0 else [], inc=(bd.MM if k == 7 else None))
            mmn = bd.MM.n
            ph.on("vector", lambda e, c=c, pj=pj: e.tensor_copy(vt[:, c, 0:128], pj[:, 0:128]),
                  wait=[(bd.MM, mmn)], inc=None)
            ph.on("vector", lambda e, c=c, pj=pj: e.tensor_copy(vt[:, c, 130:258], pj[:, 128:256]),
                  inc=bd.VE)
        ph.barrier()

        for hl in range(2):
            rows = slice(hl * 64, (hl + 1) * 64)
            for si, pairnames in enumerate((("q1", "k1"), ("q2", "k2"))):
                for ti, tn in enumerate(pairnames):
                    slot = 4 * hl + 2 * si + ti
                    ph.on("scalar", lambda e, tn=tn, rows=rows: e.activation(
                        sqb[rows, :], qk[tn][rows, :], AF.Square),
                        wait=[(bd.MM, bd.MM.n), (bd.VE, bd.VE.n)], inc=bd.SE)
                    sq_done = bd.SE.n
                    for nt in range(8):
                        sl = slice(nt * 512, (nt + 1) * 512)
                        ph.on("tensor", lambda e, rows=rows, sl=sl: e.matmul(
                            psS[2][0:1, 0:512], sb["ones_b"][rows, 0:1], sqb[rows, sl],
                            start=True, stop=True),
                            wait=[(bd.SE, sq_done), (bd.VE, bd.VE.n)], inc=bd.MM)
                        tgt = nrm[0:1, slot:slot + 1] if nt == 0 else nrm[0:1, 15:16]
                        ph.on("vector", lambda e, tgt=tgt: e.tensor_reduce(
                            tgt, psS[2][0:1, 0:512], AX.X, ALU.max),
                            wait=[(bd.MM, bd.MM.n)], inc=bd.VE)
                        if nt > 0:
                            ph.on("vector", lambda e, slot=slot: e.tensor_max(
                                nrm[0:1, slot:slot + 1], nrm[0:1, slot:slot + 1], nrm[0:1, 15:16]),
                                wait=[(bd.VE, bd.VE.n)], inc=bd.VE)
            for si in range(2):
                qslot = 4 * hl + 2 * si
                ph.on("scalar", lambda e, qslot=qslot: e.activation(
                    nrm[0:1, qslot:qslot + 1], nrm[0:1, qslot:qslot + 1], AF.Sqrt),
                    wait=[(bd.VE, bd.VE.n)], inc=bd.SE)
                ph.on("scalar", lambda e, qslot=qslot: e.activation(
                    nrm[0:1, qslot + 1:qslot + 2], nrm[0:1, qslot + 1:qslot + 2], AF.Sqrt),
                    wait=[(bd.SE, bd.SE.n)], inc=bd.SE)
                ph.on("vector", lambda e, qslot=qslot: e.tensor_mul(
                    nrm[0:1, qslot:qslot + 1], nrm[0:1, qslot:qslot + 1],
                    nrm[0:1, qslot + 1:qslot + 2]),
                    wait=[(bd.SE, bd.SE.n)], inc=bd.VE)
                ph.on("vector", lambda e, qslot=qslot: e.tensor_scalar(
                    nrm[0:1, qslot:qslot + 1], nrm[0:1, qslot:qslot + 1],
                    _c(-0.125), None, ALU.mult),
                    wait=[(bd.VE, bd.VE.n)], inc=bd.VE)
                ph.on("tensor", lambda e, qslot=qslot: e.matmul(
                    psS[2][0:128, 0:1], sb["ones_f"][0:1, 0:128], nrm[0:1, qslot:qslot + 1],
                    start=True, stop=True),
                    wait=[(bd.VE, bd.VE.n)], inc=bd.MM)
                ph.on("vector", lambda e, hl=hl, si=si: e.tensor_copy(
                    ccol[:, 2 * hl + si:2 * hl + si + 1], psS[2][0:128, 0:1]),
                    wait=[(bd.MM, bd.MM.n)], inc=bd.VE)
            ph.on("tensor", lambda e, hl=hl, hp=hp: e.matmul(
                psS[2][0:128, 1:2], sb["ones_f"][0:1, 0:128],
                sb["lamrow"][0:1, hp * 2 + hl:hp * 2 + hl + 1],
                start=True, stop=True),
                wait=[(bd.VE, bd.VE.n)], inc=bd.MM)
            ph.on("vector", lambda e, hl=hl: e.tensor_copy(
                lcol[:, hl:hl + 1], psS[2][0:128, 1:2]),
                wait=[(bd.MM, bd.MM.n)], inc=bd.VE)
        ph.barrier()

        exp_mark = {0: 0, 1: 0, 2: 0}
        evac_free = 0
        for hl in range(2):
            rows = slice(hl * 64, (hl + 1) * 64)
            vsl = slice(hl * 130, hl * 130 + 129)
            tr_marks = []
            for W in range(8):
                for si, (qn, kn) in enumerate((("q1", "k1"), ("q2", "k2"))):
                    nkt = 4 * W + 4
                    pend = []

                    def emit_pv(kt, sen, nkt, si=si, vsl=vsl):
                        for qs in range(4):
                            bank = psY[2 * si + qs // 2]
                            off = (qs % 2) * 129
                            par = kt % 3
                            ph.on("tensor", lambda e, qs=qs, kt=kt, bank=bank, off=off, par=par, nkt=nkt: e.matmul(
                                bank[:, off:off + 129], ptb[par][:, qs * 128:(qs + 1) * 128],
                                vt[:, kt, vsl],
                                start=(kt == nkt - 1 and qs % 2 == 0), stop=(kt == 0),
                                skip_group_check=True),
                                wait=[(bd.SE, sen)] if qs == 0 else [],
                                inc=(bd.MM if (qs == 3 and kt == 0) else None))

                    for kt in list(range(nkt - 1, -1, -1)):
                        par = kt % 3
                        pj = psS[par]
                        wq2 = []
                        if exp_mark[par]:
                            wq2.append((bd.SE, exp_mark[par]))
                        if kt >= nkt - 3:
                            wq2.append((bd.VE, evac_free))
                            if kt == nkt - 1 and si == 0 and W == 0:
                                wq2.append((bd.VE, bd.VE.n))
                                wq2.append((bd.SE, bd.SE.n))
                        ph.on("tensor", lambda e, kt=kt, W=W, pj=pj, rows=rows, qn=qn, kn=kn: e.matmul(
                            pj[:, :], qk[kn][rows, kt * 128:(kt + 1) * 128],
                            qk[qn][rows, W * 512:(W + 1) * 512],
                            start=True, stop=True),
                            wait=wq2, inc=bd.MM)
                        mmn = bd.MM.n
                        dd = kt - 4 * W
                        if dd >= 0:
                            ph.on("vector", lambda e, dd=dd, pj=pj, par=par: e.tensor_tensor(
                                smm[par][:, :], pj[:, :], sb["masks"][:, dd, :], ALU.add),
                                wait=[(bd.MM, mmn)], inc=bd.VE)
                            ph.on("scalar", lambda e, hl=hl, si=si, par=par: e.activation(
                                ptb[par][:, :], smm[par][:, :], AF.Exp, scale=_c(0.125),
                                bias=ccol[:, 2 * hl + si:2 * hl + si + 1]),
                                wait=[(bd.VE, bd.VE.n)], inc=bd.SE)
                        else:
                            ph.on("scalar", lambda e, pj=pj, hl=hl, si=si, par=par: e.activation(
                                ptb[par][:, :], pj[:, :], AF.Exp, scale=_c(0.125),
                                bias=ccol[:, 2 * hl + si:2 * hl + si + 1]),
                                wait=[(bd.MM, mmn)], inc=bd.SE)
                        exp_mark[par] = bd.SE.n
                        pend.append((kt, bd.SE.n))
                        if len(pend) > 2:
                            k0, s0 = pend.pop(0)
                            emit_pv(k0, s0, nkt)
                    for (k0, s0) in pend:
                        emit_pv(k0, s0, nkt)
                pv_done = bd.MM.n
                for qs in range(4):
                    off = (qs % 2) * 129
                    ph.on("vector", lambda e, qs=qs, off=off: e.tensor_copy(
                        evs[:, qs:qs + 1], psY[qs // 2][:, off + 128:off + 129]),
                        wait=[(bd.MM, pv_done)], inc=None)
                    ph.on("vector", lambda e, qs=qs, off=off: e.tensor_copy(
                        evs[:, 4 + qs:5 + qs], psY[2 + qs // 2][:, off + 128:off + 129]),
                        inc=None)
                    ph.on("vector", lambda e, qs=qs, off=off: e.tensor_copy(
                        yc[qs][:, :], psY[qs // 2][:, off:off + 128]), inc=None)
                    ph.on("vector", lambda e, qs=qs, off=off: e.tensor_copy(
                        ycb[qs][:, :], psY[2 + qs // 2][:, off:off + 128]),
                        inc=(bd.VE if qs == 3 else None))
                evac_free = bd.VE.n
                ph.on("vector", lambda e: e.tensor_scalar(
                    evs[:, 8:16], evs[:, 0:8], _c(1e-30), None, ALU.max),
                    wait=[(bd.VE, bd.VE.n)], inc=bd.VE)
                ph.on("vector", lambda e: e.reciprocal(evs[:, 8:16], evs[:, 8:16]),
                      wait=[(bd.VE, bd.VE.n)], inc=bd.VE)
                ph.on("vector", lambda e, hl=hl: e.tensor_scalar(
                    evs[:, 12:16], evs[:, 12:16], lcol[:, hl:hl + 1], None, ALU.mult),
                    wait=[(bd.VE, bd.VE.n)], inc=bd.VE)
                ph.on("vector", lambda e: e.tensor_scalar(
                    evs[:, 12:16], evs[:, 12:16], _c(-1.0), None, ALU.mult),
                    wait=[(bd.VE, bd.VE.n)], inc=bd.VE)
                rdy = bd.VE.n
                for qs in range(4):
                    ph.on("vector", lambda e, qs=qs: e.tensor_scalar(
                        yc[qs][:, :], yc[qs][:, :], evs[:, 8 + qs:9 + qs], None, ALU.mult),
                        wait=[(bd.VE, rdy)], inc=(bd.VE if qs == 3 else None))
                ycd = bd.VE.n
                for qs in range(4):
                    ph.on("vector", lambda e, qs=qs: e.scalar_tensor_tensor(
                        yc2a[:, qs * 128:(qs + 1) * 128], ycb[qs][:, :],
                        evs[:, 12 + qs:13 + qs], yc[qs][:, :],
                        ALU.mult, ALU.add),
                        wait=[(bd.VE, ycd)] if qs == 0 else [], inc=bd.VE)
                ph.on("vector", lambda e: e.tensor_reduce(
                    evs[:, 16:20], yc2a[:, :].rearrange("p (a b) -> p a b", a=4), AX.X, ALU.add),
                    wait=[(bd.VE, bd.VE.n)], inc=bd.VE)
                ph.on("vector", lambda e: e.tensor_scalar(
                    evs[:, 16:20], evs[:, 16:20], _c(1.0 / 128.0), None, ALU.mult),
                    wait=[(bd.VE, bd.VE.n)], inc=bd.VE)
                for qs in range(4):
                    ph.on("vector", lambda e, qs=qs: e.tensor_scalar(
                        yc2a[:, qs * 128:(qs + 1) * 128], yc2a[:, qs * 128:(qs + 1) * 128],
                        evs[:, 16 + qs:17 + qs], None, ALU.subtract),
                        wait=[(bd.VE, bd.VE.n)] if qs == 0 else [], inc=(bd.VE if qs == 3 else None))
                ph.on("scalar", lambda e: e.activation(sqv[:, 0:512], yc2a[:, :], AF.Square),
                      wait=[(bd.VE, bd.VE.n), (bd.SE, bd.SE.n)], inc=bd.SE)
                ph.on("vector", lambda e: e.tensor_reduce(
                    evs[:, 20:24], sqv[:, 0:512].rearrange("p (a b) -> p a b", a=4), AX.X, ALU.add),
                    wait=[(bd.SE, bd.SE.n)], inc=bd.VE)
                ph.on("scalar", lambda e: e.activation(
                    evs[:, 24:28], evs[:, 20:24], AF.Sqrt, scale=_c(1.0 / 128.0), bias=_c(EPS_LN)),
                    wait=[(bd.VE, bd.VE.n)], inc=bd.SE)
                ph.on("vector", lambda e: e.reciprocal(evs[:, 24:28], evs[:, 24:28]),
                      wait=[(bd.SE, bd.SE.n)], inc=bd.VE)
                ph.on("vector", lambda e: e.tensor_scalar(
                    evs[:, 24:28], evs[:, 24:28], _c(1.0 - LAMBDA_INIT), None, ALU.mult),
                    wait=[(bd.VE, bd.VE.n)], inc=bd.VE)
                for qs in range(4):
                    ph.on("vector", lambda e, qs=qs, W=W: e.tensor_scalar(
                        ytda[:, W * 512 + qs * 128:W * 512 + (qs + 1) * 128],
                        yc2a[:, qs * 128:(qs + 1) * 128], evs[:, 24 + qs:25 + qs], None, ALU.mult),
                        wait=[(bd.VE, bd.VE.n)] if qs == 0 else [], inc=(bd.VE if qs == 3 else None))
                    tr_marks.append(bd.VE.n)
            for ci in range(32):
                ph.on("tensor", lambda e, ci=ci: e.transpose(
                    psTb[:, 0:128], ytda[:, ci * 128:(ci + 1) * 128], sb["ident"][:, :]),
                    wait=[(bd.VE, tr_marks[ci]), (bd.VE, bd.VE.n)], inc=bd.MM)
                ph.on("vector", lambda e, ci=ci, hl=hl: e.tensor_copy(
                    yfh[hl][:, ci * 128:(ci + 1) * 128], psTb[:, 0:128]),
                    wait=[(bd.MM, bd.MM.n)], inc=bd.VE)
            ph.on("sync", lambda e, hl=hl, hp=hp: e.dma_start(
                out=io["y1in"][(hp * 2 + hl) * 128:(hp * 2 + hl + 1) * 128, :], in_=yfh[hl][:, :]),
                wait=[(bd.VE, bd.VE.n)], inc=bd.AD, inc_k=16)
            if hl == 1:
                bd.ev[f"y1p{hp}"] = [(bd.AD, bd.AD.n)]
    ph.emit()
    st2.close()
    bd.ev["y1in"] = [(bd.AD, bd.AD.n)]


def emit_cc(bd, in_name, out_names):
    ph = Ph(bd)
    if len(out_names) == 1:
        s = bd.ccsem()
        ph.on("gpsimd", lambda e: e.collective_compute(
            "AllGather", ALU.bypass, replica_groups=GROUPS,
            ins=[bd.io[in_name].ap().opt()], outs=[bd.io[out_names[0]].ap().opt()]),
            wait=bd.ev.get(in_name, []), inc=s)
        bd.ev[out_names[0]] = [(s, 1)]
    else:
        nr = bd.io[in_name].shape[0] // 4
        evp = "y0p" if in_name == "y0in" else "y1p"
        for h in range(4):
            s = bd.ccsem()
            ph.on("gpsimd", lambda e, h=h: e.collective_compute(
                "AllGather", ALU.bypass, replica_groups=GROUPS,
                ins=[bd.io[in_name][h * nr:(h + 1) * nr, :].opt()],
                outs=[bd.io[out_names[h]].ap().opt()]),
                wait=bd.ev.get(f"{evp}{h}", []), inc=s)
            bd.ev[out_names[h]] = [(s, 1)]
    ph.emit()


def emit_final_wait(bd):
    ph = Ph(bd)
    ph.on("sync", lambda e: e.nop(), wait=[(bd.AD, bd.AD.n)], inc=None)
    ph.emit()



MID = {
    "ag0in": ([D, TM], BF16), "ag0out": ([2 * D, TM], BF16),
    "g0": ([D, TM], BF16),
    "y0in": ([512, TF], BF16),
    "y0outp0": ([256, TF], BF16), "y0outp1": ([256, TF], BF16),
    "y0outp2": ([256, TF], BF16), "y0outp3": ([256, TF], BF16),
    "x1": ([D, TM], F32),
    "ag1in": ([D, TM], BF16), "ag1out": ([2 * D, TM], BF16),
    "g1": ([D, TM], BF16),
    "y1in": ([1024, TF], BF16),
    "y1outp0": ([512, TF], BF16), "y1outp1": ([512, TF], BF16),
    "y1outp2": ([512, TF], BF16), "y1outp3": ([512, TF], BF16),
}

INPUTS_BASE = [
    ("xT", [D, TM], F32),
    ("l0_wqT", [D, 512], F32), ("l0_wkT", [D, 512], F32), ("l0_wvT", [D, 512], F32),
    ("l0_projT", [64, 8, 64], F32),
    ("l0_woT", [D, D], F32),
    ("w1T", [D, DHID := 4096], F32), ("w2T", [D, 4096], F32),
    ("h1T", [4096, 4096], F32), ("h2T", [4096, 4096], F32),
    ("w3T", [4096, D], F32),
    ("wq1T", [D, 512], F32), ("wq2T", [D, 512], F32),
    ("wk1T", [D, 512], F32), ("wk2T", [D, 512], F32),
    ("wvT", [D, 1024], F32), ("wcT", [2048, D], F32),
    ("routerT", [D, 8], F32),
    ("gateT", [8, D, 2816], F32), ("upT", [8, D, 2816], F32),
    ("downT", [8, 2816, D], F32),
]

PRELOAD_IN = [(nm, ([128, max(1, len(_CONST_VALS))] if shp is None else shp), dt)
              for nm, shp, dt in PRELOAD_TENS] + [("l0_projT", [128, 4, 64], F32)]

PHASE_READS = {
    "p0": ["xT"],
    "p1": ["ag0in", "w1T", "w2T", "h1T", "h2T", "w3T"],
    "p2a": ["ag0out", "l0_wqT", "l0_wkT", "l0_wvT", "l0_projT"],
    "p2b": ["y0outp0", "y0outp1", "y0outp2", "y0outp3", "l0_woT", "g0", "xT"],
    "p3": ["ag1in", "routerT", "gateT", "upT", "downT"],
    "p4a": ["ag1out", "wq1T", "wq2T", "wk1T", "wk2T", "wvT"],
    "p4b": ["y1outp0", "y1outp1", "y1outp2", "y1outp3", "wcT", "g1", "x1"],
}
PHASE_WRITES = {
    "p0": ["ag0in"], "p1": ["g0"], "p2a": ["y0in"], "p2b": ["x1", "ag1in"],
    "p3": ["g1"], "p4a": ["y1in"], "p4b": ["outT"],
}
UNFUSED_LAUNCHES = ["p0", "p1", "p2a", "p2b", "p3", "p4a", "p4b"]
CC_ROUTES = {"ag0in": ["ag0out"], "ag1in": ["ag1out"],
             "y0in": [f"y0outp{h}" for h in range(4)],
             "y1in": [f"y1outp{h}" for h in range(4)]}


def _emit_phase(bd, name):
    if name == "p0":
        emit_ln(bd, "xT", "ag0in")
    elif name == "p1":
        emit_ffn0(bd)
    elif name == "p2a":
        emit_attn0(bd)
    elif name == "p2b":
        st3 = ExitStack()
        s0 = st3.enter_context(bd.nc.sbuf_tensor("s0sb", [128, 8, TM], BF16))
        sdone = emit_proj_out(bd, "y0in", "l0_woT", 8, "l0_bo", s0)
        bd.ev["x_s0"] = [sdone]
        emit_combine(bd, "xT", lambda k: s0[:, k, :], "g0", "x1")
        emit_ln(bd, "x1", "ag1in")
        st3.close()
    elif name == "p3":
        emit_moe(bd)
    elif name == "p4a":
        emit_attn1(bd)
    elif name == "p4b":
        st3 = ExitStack()
        s1 = st3.enter_context(bd.nc.sbuf_tensor("s1sb", [128, 8, TM], BF16))
        emit_proj_out(bd, "y1in", "wcT", 16, None, s1)
        emit_combine(bd, "x1", lambda k: s1[:, k, :], "g1", "outT")
        st3.close()
    else:
        raise ValueError(name)


def build_program(phases, fused):
    nc = bass.Bass("TRN2", target_bir_lowering=False, debug=False)
    st = ExitStack()
    io = {}
    reads = set()
    writes = set()
    for p in phases:
        reads.update(PHASE_READS[p])
        writes.update(PHASE_WRITES[p])
    if fused:
        for a, outs2 in CC_ROUTES.items():
            if a in writes:
                for b in outs2:
                    reads.discard(b)
    base_shapes = {nm: (shp, dt) for nm, shp, dt in INPUTS_BASE + PRELOAD_IN}
    innames = []
    outnames = []
    for nm, shp, dt in PRELOAD_IN:
        io[nm] = nc.dram_tensor(nm, shp, dt, kind="ExternalInput")
        innames.append(nm)
    for nm in sorted(reads):
        if nm in io:
            continue
        if nm in MID:
            shp, dt = MID[nm]
            kind = "Internal" if fused else "ExternalInput"
        else:
            shp, dt = base_shapes[nm]
            kind = "ExternalInput"
        io[nm] = nc.dram_tensor(nm, shp, dt, kind=kind)
        if kind == "ExternalInput":
            innames.append(nm)
    for nm in sorted(writes):
        if nm in io:
            continue
        if nm == "outT":
            io[nm] = nc.dram_tensor(nm, [D, TM], F32, kind="ExternalOutput")
            outnames.append(nm)
            continue
        shp, dt = MID[nm]
        kind = "Internal" if fused else "ExternalOutput"
        io[nm] = nc.dram_tensor(nm, shp, dt, kind=kind)
        if kind == "ExternalOutput":
            outnames.append(nm)
    if fused:
        for a, outs2 in CC_ROUTES.items():
            if a in writes:
                for b in outs2:
                    if b not in io:
                        shp, dt = MID[b]
                        io[b] = nc.dram_tensor(b, shp, dt, kind="Internal")
    if "p3" in phases:
        io["idxd"] = nc.dram_tensor("idxd", [1, TM], BF16, kind="Internal")

    bd = Bld(nc, st, io)
    emit_preload(bd)
    for p in phases:
        _emit_phase(bd, p)
        if fused and p in ("p0", "p2a", "p2b", "p4a"):
            for a in PHASE_WRITES[p]:
                if a in CC_ROUTES:
                    emit_cc(bd, a, CC_ROUTES[a])
    emit_final_wait(bd)
    st.close()
    return nc, innames, outnames



def host_prep(inputs):
    fT = lambda a: np.ascontiguousarray(np.asarray(a, dtype=np.float32).T)
    f = lambda a: np.ascontiguousarray(np.asarray(a, dtype=np.float32))
    shared = {}
    shared["consts"] = np.ascontiguousarray(
        np.tile(np.array(_CONST_VALS, np.float32), (128, 1)))
    shared["ident"] = np.eye(128, dtype=BD16)
    shared["identf"] = np.eye(128, dtype=np.float32)
    shared["iotam"] = np.ascontiguousarray(
        np.tile(np.arange(8, dtype=np.float32) - BIGI, (128, 16)))
    masks = np.zeros((128, 4, 512), np.float32)
    kk = np.arange(128)[:, None]
    qq = np.arange(512)[None, :]
    for dd in range(4):
        masks[:, dd, :] = np.where(kk + 128 * dd <= qq, 0.0, NEGBIG)
    shared["masks"] = masks
    shared["w1T"] = fT(inputs["L0_ffn_w1_w"])
    shared["w2T"] = fT(inputs["L0_ffn_w2_w"])
    shared["h1T"] = fT(inputs["L0_ffn_h1_w"])
    shared["h2T"] = fT(inputs["L0_ffn_h2_w"])
    shared["w3T"] = fT(inputs["L0_ffn_w3_w"])
    shared["s1c"] = f(inputs["L0_ffn_w1_s"]).reshape(32, 128).T.copy()
    shared["s2c"] = f(inputs["L0_ffn_w2_s"]).reshape(32, 128).T.copy()
    shared["sh1c"] = f(inputs["L0_ffn_h1_s"]).reshape(32, 128).T.copy()
    shared["sh2c"] = f(inputs["L0_ffn_h2_s"]).reshape(32, 128).T.copy()
    shared["s3c"] = f(inputs["L0_ffn_w3_s"]).reshape(8, 128).T.copy()
    shared["l0_woT"] = fT(inputs["L0_attn_Wo"])
    shared["l0_bo"] = f(inputs["L0_attn_bo"]).reshape(8, 128).T.copy()
    shared["wcT"] = fT(inputs["L1_attn_Wc"])
    shared["routerT"] = fT(inputs["L1_ffn_router_w"])
    shared["gateT"] = np.ascontiguousarray(
        np.asarray(inputs["L1_ffn_gate_w"], np.float32).transpose(0, 2, 1))
    shared["upT"] = np.ascontiguousarray(
        np.asarray(inputs["L1_ffn_up_w"], np.float32).transpose(0, 2, 1))
    shared["downT"] = np.ascontiguousarray(
        np.asarray(inputs["L1_ffn_down_w"], np.float32).transpose(0, 2, 1))

    x = np.asarray(inputs["L0_attn_query"], np.float32)
    percore = []
    for c in range(8):
        b, half = c // 2, c % 2
        hs = slice(half * 512, (half + 1) * 512)
        hv = slice(half * 1024, (half + 1) * 1024)
        h8 = slice(half * 8, (half + 1) * 8)
        d = {}
        d["xT"] = np.ascontiguousarray(x[b, half * TM:(half + 1) * TM, :].T)
        d["l0_wqT"] = fT(np.asarray(inputs["L0_attn_Wq"], np.float32)[hs, :])
        d["l0_wkT"] = fT(np.asarray(inputs["L0_attn_Wk"], np.float32)[hs, :])
        d["l0_wvT"] = fT(np.asarray(inputs["L0_attn_Wv"], np.float32)[hs, :])
        d["l0_bq"] = f(inputs["L0_attn_bq"])[hs].reshape(4, 128).T.copy()
        d["l0_bk"] = f(inputs["L0_attn_bk"])[hs].reshape(4, 128).T.copy()
        d["l0_bvr"] = f(inputs["L0_attn_bv"])[hs].reshape(1, 512).copy()
        pmy = np.asarray(inputs["L0_attn_proj"], np.float32)[h8]
        pT = np.zeros((128, 4, 64), np.float32)
        for hp in range(4):
            for hl in range(2):
                pT[hl * 64:(hl + 1) * 64, hp, :] = pmy[2 * hp + hl].T
        d["l0_projT"] = pT
        d["wq1T"] = fT(np.asarray(inputs["L1_attn_Wq1"], np.float32)[hs, :])
        d["wq2T"] = fT(np.asarray(inputs["L1_attn_Wq2"], np.float32)[hs, :])
        d["wk1T"] = fT(np.asarray(inputs["L1_attn_Wk1"], np.float32)[hs, :])
        d["wk2T"] = fT(np.asarray(inputs["L1_attn_Wk2"], np.float32)[hs, :])
        d["wvT"] = fT(np.asarray(inputs["L1_attn_Wv"], np.float32)[hv, :])
        d["lq1"] = f(inputs["L1_attn_lq1"])[h8].copy()
        d["lk1"] = f(inputs["L1_attn_lk1"])[h8].copy()
        d["lq2"] = f(inputs["L1_attn_lq2"])[h8].copy()
        d["lk2"] = f(inputs["L1_attn_lk2"])[h8].copy()
        d["coloff"] = np.array([[half * TM]], np.uint32)
        percore.append(d)
    return shared, percore


def _route_cc(vals, produced):
    for a, outs2 in CC_ROUTES.items():
        if a not in produced:
            continue
        for m in range(4):
            if len(outs2) == 1:
                cat = np.concatenate([vals[2 * m][a], vals[2 * m + 1][a]], axis=0)
                vals[2 * m][outs2[0]] = cat
                vals[2 * m + 1][outs2[0]] = cat
            else:
                nr = vals[2 * m][a].shape[0] // 4
                for h in range(4):
                    cat = np.concatenate([vals[2 * m][a][h * nr:(h + 1) * nr],
                                          vals[2 * m + 1][a][h * nr:(h + 1) * nr]], axis=0)
                    vals[2 * m][outs2[h]] = cat
                    vals[2 * m + 1][outs2[h]] = cat


def kernel(**inputs):
    shared, percore = host_prep(inputs)
    outs = [None] * 8
    if FUSED:
        nc, innames, outnames = build_program(UNFUSED_LAUNCHES, fused=True)
        maps = []
        for c in range(8):
            m = {}
            for n in innames:
                v = percore[c].get(n)
                if v is None:
                    v = shared[n]
                m[n] = v
            maps.append(m)
        res = run_bass_kernel_spmd(nc, maps, core_ids=list(range(8)))
        for c in range(8):
            outs[c] = res.results[c]["outT"]
    else:
        vals = [dict() for _ in range(8)]
        for launch in UNFUSED_LAUNCHES:
            nc, innames, outnames = build_program([launch], fused=False)
            maps = []
            for c in range(8):
                m = {}
                for n in innames:
                    v = percore[c].get(n)
                    if v is None:
                        v = shared.get(n)
                    if v is None:
                        v = vals[c][n]
                    m[n] = v
                maps.append(m)
            res = run_bass_kernel_spmd(nc, maps, core_ids=list(range(8)))
            for c in range(8):
                for n in outnames:
                    vals[c][n] = res.results[c][n]
            _route_cc(vals, PHASE_WRITES[launch])
        for c in range(8):
            outs[c] = vals[c]["outT"]
    out = np.empty((4, TF, D), np.float32)
    for c in range(8):
        b, half = c // 2, c % 2
        out[b, half * TM:(half + 1) * TM, :] = outs[c].T
    return out


# version 13 (final)
# speedup vs baseline: 1.0282x; 1.0236x over previous
import numpy as np
import ml_dtypes
from contextlib import ExitStack

import concourse.bass as bass
import concourse.mybir as mybir
from concourse.bass_utils import run_bass_kernel_spmd

F32 = mybir.dt.float32
BF16 = mybir.dt.bfloat16
U32 = mybir.dt.uint32
AF = mybir.ActivationFunctionType
ALU = mybir.AluOpType
AX = mybir.AxisListType
BD16 = ml_dtypes.bfloat16

D = 1024
TM = 2048
TF = 4096
DH = 64
LAMBDA_INIT = 0.7 - 0.6 * float(np.exp(-0.3))
EPS_LN = 1e-5
EPS_FEAT = 1e-4
EPS_DEN = 0.01
C0 = float(DH) ** -0.25
LN_RATIO = float(np.log(64.0 ** -0.5))
NEGBIG = -1e9
BIGI = 1e4
GROUPS = [[0, 1], [2, 3], [4, 5], [6, 7]]

ENGINES = ("sync", "gpsimd", "tensor", "vector", "scalar")

FUSED = True
SIM_SAFE = False

_CONST_VALS = []


def _c(v):
    v = float(v)
    if v not in _CONST_VALS:
        _CONST_VALS.append(v)
    return v


for _v in (1.0 / D, EPS_LN, LAMBDA_INIT, C0, LN_RATIO, EPS_FEAT, EPS_DEN,
           1e-30, -1.0, 1.0, 0.125, -0.125, 1.0 / 128.0, BIGI,
           1.0 - LAMBDA_INIT, -0.5 * C0 * C0, -C0, 0.0, 2.0, 3.0, 4.0, 5.0,
           6.0, 7.0):
    _c(_v)


def _full(t):
    return t[tuple(slice(None) for _ in range(len(t.shape)))]


class Sem:
    def __init__(self, st, nc, name):
        self.h = st.enter_context(nc.semaphore(name))
        self.n = 0

    def bump(self, k=1):
        self.n += k
        return self.n


class Ph:

    def __init__(self, bd):
        self.bd = bd
        self.ops = {e: [] for e in ENGINES}
        self.pending = {e: [] for e in ENGINES}

    def barrier(self, marks=None):
        if marks is None:
            marks = self.bd.snapshot()
        for e in ENGINES:
            self.pending[e] = self.pending[e] + list(marks)

    def on(self, eng, fn, wait=None, inc=None, inc_k=1, inc2=None, inc2_k=1):
        w = self.pending[eng] + list(wait or [])
        self.pending[eng] = []
        if SIM_SAFE and inc is not None and inc_k == 16:
            w.append((inc, inc.n))
        if inc is not None:
            inc.bump(inc_k)
        if inc2 is not None:
            inc2.bump(inc2_k)
        self.ops[eng].append((fn, w, inc, inc_k, inc2, inc2_k))

    def emit(self):
        nc = self.bd.nc
        with nc.Block() as blk:
            for eng in ENGINES:
                ops = self.ops[eng]
                if not ops:
                    continue

                def mk(ops):
                    def closure(e):
                        hi = {}
                        for fn, waits, sem, k, sem2, k2 in ops:
                            for (s, v) in waits:
                                if v > 0 and hi.get(id(s), -1) < v:
                                    e.wait_ge(s.h, v)
                                    hi[id(s)] = v
                            ins = fn(e)
                            if sem is not None:
                                ins = ins.then_inc(sem.h, k)
                            if sem2 is not None:
                                ins.then_inc(sem2.h, k2)
                    return closure

                getattr(blk, eng)(mk(ops))


class Bld:
    def __init__(self, nc, st, io):
        self.nc = nc
        self.st = st
        self.io = io
        self.sb = {}
        self.AD = Sem(st, nc, "AD")
        self.GD = Sem(st, nc, "GD")
        self.MM = Sem(st, nc, "MM")
        self.VE = Sem(st, nc, "VE")
        self.SE = Sem(st, nc, "SE")
        self.WF = Sem(st, nc, "WF")
        self.allsems = [self.AD, self.GD, self.MM, self.VE, self.SE, self.WF]
        self.ccsems = []
        self.ev = {}
        self.nphase = 0

    def snapshot(self):
        return [(s, s.n) for s in self.allsems if s.n > 0] + \
               [(s, s.n) for s in self.ccsems if s.n > 0]

    def tag(self):
        self.nphase += 1
        return f"z{self.nphase}_"

    def ccsem(self):
        s = Sem(self.st, self.nc, f"CC{len(self.ccsems)}")
        self.ccsems.append(s)
        return s


PRELOAD_TENS = [
    ("consts", None, F32), ("ident", [128, 128], BF16),
    ("identf", [128, 128], F32), ("iotam", [128, 128], F32),
    ("masks", [128, 4, 512], F32),
    ("l0_bq", [128, 4], F32), ("l0_bk", [128, 4], F32),
    ("l0_bvr", [1, 512], F32), ("l0_bo", [128, 8], F32),
    ("s1c", [128, 32], F32), ("s2c", [128, 32], F32),
    ("sh1c", [128, 32], F32), ("sh2c", [128, 32], F32),
    ("s3c", [128, 8], F32), ("coloff", [1, 1], U32),
    ("lq1", [8, 64], F32), ("lk1", [8, 64], F32),
    ("lq2", [8, 64], F32), ("lk2", [8, 64], F32),
]


def emit_preload(bd):
    nc, io, sb = bd.nc, bd.io, bd.sb
    ph = Ph(bd)
    for nm, shp, dt in PRELOAD_TENS:
        if shp is None:
            shp = [128, max(1, len(_CONST_VALS))]
        sb[nm] = bd.st.enter_context(nc.sbuf_tensor("c_" + nm, shp, dt))
    sb["projT"] = bd.st.enter_context(nc.sbuf_tensor("c_projT", [128, 4, 64], BF16))
    sb["ones_f"] = bd.st.enter_context(nc.sbuf_tensor("c_onesf", [128, 130], F32))
    sb["ones_b"] = bd.st.enter_context(nc.sbuf_tensor("c_onesb", [128, 130], BF16))
    sb["lamrow"] = bd.st.enter_context(nc.sbuf_tensor("c_lamrow", [1, 8], F32))

    for i, v in enumerate(_CONST_VALS):
        nc.const_aps.register(F32, v, sb["consts"][:, i:i + 1])

    for nm, _, _ in PRELOAD_TENS:
        ph.on("sync", lambda e, nm=nm: e.dma_start(out=_full(bd.sb[nm]), in_=_full(io[nm])),
              inc=bd.AD, inc_k=16)
    ph.on("gpsimd", lambda e: e.dma_start(out=_full(sb["projT"]), in_=_full(io["l0_projT"])),
          inc=bd.GD, inc_k=16)
    ph.on("vector", lambda e: e.memset(sb["ones_f"][:, :], 1.0), inc=bd.VE)
    ph.on("vector", lambda e: e.memset(sb["ones_b"][:, :], 1.0), inc=bd.VE)

    st2 = ExitStack()
    lsc = st2.enter_context(nc.sbuf_tensor("pre_lsc", [16, 4], F32))
    lpad = st2.enter_context(nc.sbuf_tensor("pre_lpad", [16, 128], BF16))
    pst = st2.enter_context(nc.psum_tensor("pre_pst", [128, 128], BF16))
    ph.barrier()
    for j, (a, bnm) in enumerate((("lq1", "lk1"), ("lq2", "lk2"))):
        ph.on("vector", lambda e, a=a, bnm=bnm: e.tensor_mul(sb[a][0:8, :], sb[a][0:8, :], sb[bnm][0:8, :]),
              inc=bd.VE)
        ph.on("vector", lambda e, a=a, j=j: e.tensor_reduce(lsc[0:8, j:j + 1], sb[a][0:8, :], AX.X, ALU.add),
              wait=[(bd.VE, bd.VE.n)], inc=bd.VE)
        ph.on("scalar", lambda e, j=j: e.activation(lsc[0:8, j:j + 1], lsc[0:8, j:j + 1], AF.Exp),
              wait=[(bd.VE, bd.VE.n)], inc=bd.SE)
    ph.on("vector", lambda e: e.tensor_sub(lsc[0:8, 0:1], lsc[0:8, 0:1], lsc[0:8, 1:2]),
          wait=[(bd.SE, bd.SE.n)], inc=bd.VE)
    ph.on("vector", lambda e: e.tensor_scalar(lsc[0:8, 0:1], lsc[0:8, 0:1], _c(LAMBDA_INIT), None, ALU.add),
          wait=[(bd.VE, bd.VE.n)], inc=bd.VE)
    ph.on("vector", lambda e: e.memset(lpad[:, :], 0.0), inc=bd.VE)
    ph.on("vector", lambda e: e.tensor_copy(lpad[0:8, 0:1], lsc[0:8, 0:1]),
          wait=[(bd.VE, bd.VE.n)], inc=bd.VE)
    ph.on("tensor", lambda e: e.transpose(pst[0:128, 0:16], lpad[0:16, :], sb["ident"][0:16, 0:16]),
          wait=[(bd.VE, bd.VE.n)], inc=bd.MM)
    ph.on("vector", lambda e: e.tensor_copy(sb["lamrow"][0:1, 0:8], pst[0:1, 0:8]),
          wait=[(bd.MM, bd.MM.n)], inc=bd.VE)
    ph.emit()
    st2.close()
    bd.ev["preload"] = bd.snapshot()


def emit_ln(bd, src_name, dst_name):
    nc, io, sb = bd.nc, bd.io, bd.sb
    ph = Ph(bd)
    ph.barrier(bd.ev.get(src_name, []) + bd.snapshot())
    st2 = ExitStack()
    zt = bd.tag()
    xin = [st2.enter_context(nc.sbuf_tensor(zt + f"ln_x{k}", [128, TM], F32)) for k in range(8)]
    xsq = st2.enter_context(nc.sbuf_tensor(zt + zt + "ln_xsq", [128, 512], F32))
    rows = st2.enter_context(nc.sbuf_tensor(zt + "ln_rows", [1, 5 * TM], F32))
    tmp = st2.enter_context(nc.sbuf_tensor(zt + "ln_tmp", [128, 512], F32))
    xno = st2.enter_context(nc.sbuf_tensor(zt + "ln_out", [128, 8, TM], BF16))
    ps_m = st2.enter_context(nc.psum_tensor(zt + "ln_psm", [1, 512], F32))
    ps_s = st2.enter_context(nc.psum_tensor(zt + "ln_pss", [1, 512], F32))
    ps_b1 = st2.enter_context(nc.psum_tensor(zt + "ln_psb1", [128, 512], F32))
    ps_b2 = st2.enter_context(nc.psum_tensor(zt + "ln_psb2", [128, 512], F32))
    src = io[src_name]
    ones_col = sb["ones_f"][:, 0:1]
    ones_row = sb["ones_f"][0:1, 0:128]

    xw = []
    for k in range(8):
        ph.on("sync", lambda e, k=k: e.dma_start(out=xin[k][:, :], in_=src[k * 128:(k + 1) * 128, :]),
              inc=bd.AD, inc_k=16)
        xw.append((bd.AD, bd.AD.n))

    evac_prev = None
    norm_prev = None
    for nt in range(4):
        sl = slice(nt * 512, (nt + 1) * 512)
        wbank = [(bd.VE, evac_prev)] if evac_prev else []
        for k in range(8):
            ph.on("tensor", lambda e, k=k, sl=sl: e.matmul(
                ps_m[0:1, :], ones_col, xin[k][:, sl], start=(k == 0), stop=(k == 7)),
                wait=[xw[k]] + wbank, inc=(bd.MM if k == 7 else None))
        for k in range(8):
            wsq = [xw[k]]
            if nt > 0 or k > 0:
                wsq.append((bd.MM, bd.MM.n))
            ph.on("scalar", lambda e, k=k, sl=sl: e.activation(xsq[:, :], xin[k][:, sl], AF.Square),
                  wait=wsq, inc=bd.SE)
            ph.on("tensor", lambda e, k=k: e.matmul(
                ps_s[0:1, :], ones_col, xsq[:, :], start=(k == 0), stop=(k == 7)),
                wait=[(bd.SE, bd.SE.n)] + wbank, inc=bd.MM)
        mm_stats = bd.MM.n
        o = nt * 512
        mrow = rows[0:1, o:o + 512]
        srow = rows[0:1, TM + o:TM + o + 512]
        vrow = rows[0:1, 2 * TM + o:2 * TM + o + 512]
        rrow = rows[0:1, 3 * TM + o:3 * TM + o + 512]
        mrr = rows[0:1, 4 * TM + o:4 * TM + o + 512]
        ph.on("vector", lambda e, mrow=mrow: e.tensor_scalar(mrow, ps_m[0:1, :], _c(1.0 / D), None, ALU.mult),
              wait=[(bd.MM, mm_stats)], inc=bd.VE)
        ph.on("vector", lambda e, srow=srow: e.tensor_scalar(srow, ps_s[0:1, :], _c(1.0 / D), None, ALU.mult),
              inc=bd.VE)
        ph.on("vector", lambda e, vrow=vrow, mrow=mrow: e.tensor_mul(vrow, mrow, mrow),
              wait=[(bd.VE, bd.VE.n)], inc=bd.VE)
        ph.on("vector", lambda e, vrow=vrow, srow=srow: e.tensor_sub(vrow, srow, vrow),
              wait=[(bd.VE, bd.VE.n)], inc=bd.VE)
        evac_prev = bd.VE.n
        ph.on("scalar", lambda e, vrow=vrow, rrow=rrow: e.activation(rrow, vrow, AF.Sqrt, bias=_c(EPS_LN)),
              wait=[(bd.VE, bd.VE.n)], inc=bd.SE)
        ph.on("vector", lambda e, rrow=rrow: e.reciprocal(rrow, rrow), wait=[(bd.SE, bd.SE.n)], inc=bd.VE)
        ph.on("vector", lambda e, mrr=mrr, mrow=mrow, rrow=rrow: e.tensor_mul(mrr, mrow, rrow),
              wait=[(bd.VE, bd.VE.n)], inc=bd.VE)
        rows_done = bd.VE.n
        wbc = [(bd.VE, rows_done)]
        if norm_prev:
            wbc.append((bd.VE, norm_prev))
        ph.on("tensor", lambda e, rrow=rrow: e.matmul(ps_b1[:, :], ones_row, rrow, start=True, stop=True),
              wait=wbc, inc=None)
        ph.on("tensor", lambda e, mrr=mrr: e.matmul(ps_b2[:, :], ones_row, mrr, start=True, stop=True),
              inc=bd.MM)
        bc = bd.MM.n
        for k in range(8):
            ph.on("vector", lambda e, k=k, sl=sl: e.tensor_mul(tmp[:, :], xin[k][:, sl], ps_b1[:, :]),
                  wait=[(bd.MM, bc), (bd.VE, bd.VE.n)], inc=bd.VE)
            ph.on("vector", lambda e, k=k, sl=sl: e.tensor_sub(xno[:, k, sl], tmp[:, :], ps_b2[:, :]),
                  wait=[(bd.VE, bd.VE.n)], inc=bd.VE)
        norm_prev = bd.VE.n
    done = bd.VE.n
    for k in range(8):
        ph.on("sync", lambda e, k=k: e.dma_start(
            out=io[dst_name][k * 128:(k + 1) * 128, :], in_=xno[:, k, :]),
            wait=[(bd.VE, done)], inc=bd.AD, inc_k=16)
    ph.emit()
    st2.close()
    bd.ev[dst_name] = [(bd.AD, bd.AD.n)]


def emit_combine(bd, x_name, s_chunks, g_name, out_name):
    nc, io = bd.nc, bd.io
    ph = Ph(bd)
    ph.barrier(bd.ev.get(x_name, []) + bd.ev.get(g_name, []) + bd.snapshot())
    st2 = ExitStack()
    zt = bd.tag()
    xc = [st2.enter_context(nc.sbuf_tensor(zt + f"cb_x{j}", [128, TM], F32)) for j in range(2)]
    gc = [st2.enter_context(nc.sbuf_tensor(zt + f"cb_g{j}", [128, TM], BF16)) for j in range(2)]
    sg = st2.enter_context(nc.sbuf_tensor(zt + "cb_sg", [128, TM], F32))
    oc = [st2.enter_context(nc.sbuf_tensor(zt + f"cb_o{j}", [128, TM], F32)) for j in range(2)]
    st_done = []
    for k in range(8):
        j = k % 2
        w = [(bd.AD, st_done[k - 2])] if k >= 2 else []
        ph.on("sync", lambda e, k=k, j=j: e.dma_start(out=xc[j][:, :], in_=io[x_name][k * 128:(k + 1) * 128, :]),
              wait=w, inc=bd.AD, inc_k=16)
        ph.on("sync", lambda e, k=k, j=j: e.dma_start(out=gc[j][:, :], in_=io[g_name][k * 128:(k + 1) * 128, :]),
              inc=bd.AD, inc_k=16)
        adn = bd.AD.n
        ph.on("vector", lambda e, k=k, j=j: e.tensor_mul(sg[:, :], s_chunks(k), gc[j][:, :]),
              wait=[(bd.AD, adn), (bd.VE, bd.VE.n)], inc=bd.VE)
        ph.on("vector", lambda e, k=k, j=j: e.tensor_add(oc[j][:, :], xc[j][:, :], sg[:, :]),
              wait=[(bd.VE, bd.VE.n)], inc=bd.VE)
        ph.on("sync", lambda e, k=k, j=j: e.dma_start(out=io[out_name][k * 128:(k + 1) * 128, :], in_=oc[j][:, :]),
              wait=[(bd.VE, bd.VE.n)], inc=bd.AD, inc_k=16)
        st_done.append(bd.AD.n)
    ph.emit()
    st2.close()
    bd.ev[out_name] = [(bd.AD, bd.AD.n)]


def emit_ffn0(bd):
    nc, io, sb = bd.nc, bd.io, bd.sb
    ph = Ph(bd)
    ph.barrier(bd.ev.get("ag0in", []) + bd.snapshot())
    st2 = ExitStack()
    TP = 1024
    xnp = st2.enter_context(nc.sbuf_tensor("f0_xn", [128, 8, TP], BF16))
    h1 = st2.enter_context(nc.sbuf_tensor("f0_h1", [128, 32, TP], BF16))
    h2 = st2.enter_context(nc.sbuf_tensor("f0_h2", [128, 32, TP], BF16))
    wa = [st2.enter_context(nc.sbuf_tensor(f"f0_wa{j}", [128, 8, 128], BF16)) for j in range(2)]
    wb = [st2.enter_context(nc.sbuf_tensor(f"f0_wb{j}", [128, 8, 128], BF16)) for j in range(2)]
    wh = [st2.enter_context(nc.sbuf_tensor(f"f0_wh{j}", [128, 32, 128], BF16)) for j in range(2)]
    ea = [st2.enter_context(nc.sbuf_tensor(f"f0_ea{j}", [128, 512], BF16)) for j in range(2)]
    eb = [st2.enter_context(nc.sbuf_tensor(f"f0_eb{j}", [128, 512], BF16)) for j in range(2)]
    go = st2.enter_context(nc.sbuf_tensor("f0_go", [128, TP], BF16))
    psA = [st2.enter_context(nc.psum_tensor(f"f0_psA{j}", [128, 512], F32)) for j in range(2)]
    psB = [st2.enter_context(nc.psum_tensor(f"f0_psB{j}", [128, 512], F32)) for j in range(2)]

    for p in range(2):
        ph.barrier()
        for k in range(8):
            ph.on("sync", lambda e, k=k, p=p: e.dma_start(
                out=xnp[:, k, :], in_=io["ag0in"][k * 128:(k + 1) * 128, p * TP:(p + 1) * TP]),
                inc=bd.AD, inc_k=16)
        xn_ready = (bd.AD, bd.AD.n)

        gd_marks = []
        wf_base = bd.WF.n
        for i in range(32):
            w = [(bd.WF, wf_base + 2 * (i - 1))] if i >= 2 else []
            ph.on("gpsimd", lambda e, i=i: e.dma_start(
                out=_full(wa[i % 2]),
                in_=io["w1T"][:, i * 128:(i + 1) * 128].rearrange("(a p) o -> p a o", p=128)),
                wait=w, inc=bd.GD, inc_k=16)
            ph.on("gpsimd", lambda e, i=i: e.dma_start(
                out=_full(wb[i % 2]),
                in_=io["w2T"][:, i * 128:(i + 1) * 128].rearrange("(a p) o -> p a o", p=128)),
                inc=bd.GD, inc_k=16)
            gd_marks.append(bd.GD.n)
        ev_marks = []
        for i in range(32):
            for nt in range(2):
                j = nt
                sl = slice(nt * 512, (nt + 1) * 512)
                wq = [(bd.GD, gd_marks[i]), xn_ready]
                idx = len(ev_marks) - 2
                if idx >= 0:
                    wq.append((bd.SE, ev_marks[idx][0]))
                    wq.append((bd.VE, ev_marks[idx][1]))
                for k in range(8):
                    ph.on("tensor", lambda e, i=i, k=k, sl=sl, j=j: e.matmul(
                        psA[j][:, :], wa[i % 2][:, k, :], xnp[:, k, sl],
                        start=(k == 0), stop=(k == 7)), wait=wq if k == 0 else [], inc=None)
                for k in range(8):
                    last = (k == 7)
                    ph.on("tensor", lambda e, i=i, k=k, sl=sl, j=j: e.matmul(
                        psB[j][:, :], wb[i % 2][:, k, :], xnp[:, k, sl],
                        start=(k == 0), stop=(k == 7)),
                        inc=(bd.MM if last else None))
                mmn = bd.MM.n
                if nt == 1:
                    ph.on("tensor", lambda e: e.nop(), wait=[(bd.MM, mmn)], inc=bd.WF, inc_k=2)
                ph.on("scalar", lambda e, i=i, j=j: e.activation(
                    ea[j][:, :], psA[j][:, :], AF.Silu, scale=sb["s1c"][:, i:i + 1]),
                    wait=[(bd.MM, mmn)], inc=bd.SE)
                ph.on("vector", lambda e, i=i, j=j: e.tensor_scalar(
                    eb[j][:, :], psB[j][:, :], sb["s2c"][:, i:i + 1], None, ALU.mult),
                    wait=[(bd.MM, mmn)], inc=bd.VE)
                ph.on("vector", lambda e, i=i, j=j, sl=sl: e.tensor_mul(
                    h1[:, i, sl], ea[j][:, :], eb[j][:, :]),
                    wait=[(bd.SE, bd.SE.n), (bd.VE, bd.VE.n)], inc=bd.VE)
                ev_marks.append((bd.SE.n, bd.VE.n))
        h_done = (bd.VE, bd.VE.n)

        for (wTn, scn, srcb, dstb) in (("h1T", "sh1c", h1, h2), ("h2T", "sh2c", h2, h1)):
            ph.barrier()
            gd_marks = []
            wf_base = bd.WF.n
            for i in range(32):
                w = [(bd.WF, wf_base + (i - 1))] if i >= 2 else []
                ph.on("gpsimd", lambda e, i=i, wTn=wTn: e.dma_start(
                    out=_full(wh[i % 2]),
                    in_=io[wTn][:, i * 128:(i + 1) * 128].rearrange("(a p) o -> p a o", p=128)),
                    wait=w, inc=bd.GD, inc_k=16)
                gd_marks.append(bd.GD.n)
            se_marks = []
            for i in range(32):
                for nt in range(2):
                    j = nt
                    sl = slice(nt * 512, (nt + 1) * 512)
                    wq = [(bd.GD, gd_marks[i]), h_done]
                    idx = len(se_marks) - 2
                    if idx >= 0:
                        wq.append((bd.SE, se_marks[idx]))
                    for k in range(32):
                        last = (k == 31)
                        ph.on("tensor", lambda e, i=i, k=k, sl=sl, j=j, srcb=srcb: e.matmul(
                            psA[j][:, :], wh[i % 2][:, k, :], srcb[:, k, sl],
                            start=(k == 0), stop=(k == 31)),
                            wait=wq if k == 0 else [], inc=(bd.MM if last else None))
                    mmn = bd.MM.n
                    if nt == 1:
                        ph.on("tensor", lambda e: e.nop(), wait=[(bd.MM, mmn)], inc=bd.WF)
                    ph.on("scalar", lambda e, i=i, j=j, sl=sl, scn=scn, dstb=dstb: e.activation(
                        dstb[:, i, sl], psA[j][:, :], AF.Gelu, scale=sb[scn][:, i:i + 1]),
                        wait=[(bd.MM, mmn)], inc=bd.SE)
                    se_marks.append(bd.SE.n)
            h_done = (bd.SE, bd.SE.n)

        ph.barrier()
        gd_marks = []
        wf_base = bd.WF.n
        for i in range(8):
            w = [(bd.WF, wf_base + (i - 1))] if i >= 2 else []
            ph.on("gpsimd", lambda e, i=i: e.dma_start(
                out=_full(wh[i % 2]),
                in_=io["w3T"][:, i * 128:(i + 1) * 128].rearrange("(a p) o -> p a o", p=128)),
                wait=w, inc=bd.GD, inc_k=16)
            gd_marks.append(bd.GD.n)
        for i in range(8):
            st_prev = bd.AD.n
            for nt in range(2):
                j = nt
                sl = slice(nt * 512, (nt + 1) * 512)
                wq = [(bd.GD, gd_marks[i]), h_done]
                if i > 0:
                    wq.append((bd.AD, st_prev))
                for k in range(32):
                    last = (k == 31)
                    ph.on("tensor", lambda e, i=i, k=k, sl=sl, j=j: e.matmul(
                        psA[j][:, :], wh[i % 2][:, k, :], h1[:, k, sl],
                        start=(k == 0), stop=(k == 31)),
                        wait=wq if k == 0 else [], inc=(bd.MM if last else None))
                mmn = bd.MM.n
                if nt == 1:
                    ph.on("tensor", lambda e: e.nop(), wait=[(bd.MM, mmn)], inc=bd.WF)
                ph.on("scalar", lambda e, i=i, j=j, sl=sl: e.activation(
                    go[:, sl], psA[j][:, :], AF.Silu, scale=sb["s3c"][:, i:i + 1]),
                    wait=[(bd.MM, mmn)] + ([(bd.AD, st_prev)] if i > 0 else []), inc=bd.SE)
            ph.on("sync", lambda e, i=i, p=p: e.dma_start(
                out=io["g0"][i * 128:(i + 1) * 128, p * TP:(p + 1) * TP], in_=go[:, :]),
                wait=[(bd.SE, bd.SE.n)], inc=bd.AD, inc_k=16)
    ph.emit()
    st2.close()
    bd.ev["g0"] = [(bd.AD, bd.AD.n)]


def emit_proj_out(bd, y_name, wT_name, n_f, bias_tile_name, s_sb):
    nc, io, sb = bd.nc, bd.io, bd.sb

    def ysrc(fc):
        if y_name == "y0in":
            return io[f"y0outp{fc % 4}"], (fc // 4) * 128
        if fc < 8:
            return io[f"y1outp{fc // 2}"], (fc % 2) * 128
        return io[f"y1outp{(fc - 8) // 2}"], 256 + ((fc - 8) % 2) * 128

    ph = Ph(bd)
    yw = []
    for fc in range(n_f):
        t, _ = ysrc(fc)
        yw += bd.ev.get(t.name, [])
    ph.barrier(yw + bd.snapshot())
    st2 = ExitStack()
    zt = bd.tag()
    yb = st2.enter_context(nc.sbuf_tensor(zt + "po_y", [128, n_f, TM], BF16))
    wt = [st2.enter_context(nc.sbuf_tensor(zt + f"po_w{j}", [128, n_f, 128], BF16)) for j in range(2)]
    ps = [st2.enter_context(nc.psum_tensor(zt + f"po_ps{j}", [128, 512], F32)) for j in range(2)]

    def yload(e, fc):
        t, rb = ysrc(fc)
        with e.register(zt + f"ro{fc}") as r:
            e.reg_load(r, sb["coloff"][0:1, 0:1])
            e.reg_alu(r, r, rb * TF, ALU.add)
            return e.dma_start(out=yb[:, fc, :], in_=bass.AP(t, r, [[TF, 128], [1, TM]]))

    for fc in range(n_f):
        ph.on("sync", lambda e, fc=fc: yload(e, fc), inc=bd.AD, inc_k=16)
    y_ready = (bd.AD, bd.AD.n)
    gd_marks = []
    wf_base = bd.WF.n
    for i in range(8):
        w = [(bd.WF, wf_base + (i - 1))] if i >= 2 else []
        ph.on("gpsimd", lambda e, i=i: e.dma_start(
            out=_full(wt[i % 2]),
            in_=io[wT_name][:, i * 128:(i + 1) * 128].rearrange("(a p) o -> p a o", p=128)),
            wait=w, inc=bd.GD, inc_k=16)
        gd_marks.append(bd.GD.n)
    ev_marks = []
    for i in range(8):
        for nt in range(4):
            j = nt % 2
            sl = slice(nt * 512, (nt + 1) * 512)
            wq = [(bd.GD, gd_marks[i]), y_ready]
            idx = len(ev_marks) - 2
            if idx >= 0:
                wq.append((bd.SE, ev_marks[idx]))
            for k in range(n_f):
                last = (k == n_f - 1)
                ph.on("tensor", lambda e, i=i, k=k, sl=sl, j=j: e.matmul(
                    ps[j][:, :], wt[i % 2][:, k, :], yb[:, k, sl],
                    start=(k == 0), stop=(k == n_f - 1)),
                    wait=wq if k == 0 else [], inc=(bd.MM if last else None))
            mmn = bd.MM.n
            if nt == 3:
                ph.on("tensor", lambda e: e.nop(), wait=[(bd.MM, mmn)], inc=bd.WF)
            if bias_tile_name is not None:
                ph.on("vector", lambda e, i=i, j=j, sl=sl: e.tensor_scalar(
                    s_sb[:, i, sl], ps[j][:, :], sb[bias_tile_name][:, i:i + 1], None, ALU.add),
                    wait=[(bd.MM, mmn)], inc=bd.SE)
            else:
                ph.on("scalar", lambda e, i=i, j=j, sl=sl: e.activation(
                    s_sb[:, i, sl], ps[j][:, :], AF.Copy),
                    wait=[(bd.MM, mmn)], inc=bd.SE)
            ev_marks.append(bd.SE.n)
    ph.emit()
    st2.close()
    return (bd.SE, bd.SE.n)


A0_DEBUG = False


def emit_attn0(bd):
    nc, io, sb = bd.nc, bd.io, bd.sb
    dbg = {}
    if A0_DEBUG:
        for nm, shp, dt in (("dbg_qf", [128, TF], BF16), ("dbg_kf", [128, TF], BF16),
                            ("dbg_vt", [128, 32 * 130], BF16), ("dbg_qp1", [64, TF], BF16),
                            ("dbg_kpt", [128, 32 * 64], BF16), ("dbg_kv", [64, 65], BF16),
                            ("dbg_nm", [128, 8], F32), ("dbg_bq2", [128, 32], F32), ("dbg_qmx", [128, 32], F32), ("dbg_qsd", [128, 32], F32), ("dbg_qd", [128, 32 * 64], F32)):
            dbg[nm] = nc.dram_tensor(nm, shp, dt, kind="ExternalOutput")
    ph = Ph(bd)
    ph.barrier(bd.ev.get("ag0out", []) + bd.snapshot())
    st2 = ExitStack()
    xnf = st2.enter_context(nc.sbuf_tensor("a0_xn", [128, 8, TF], BF16))
    qf = st2.enter_context(nc.sbuf_tensor("a0_qf", [128, TF], BF16))
    kf = st2.enter_context(nc.sbuf_tensor("a0_kf", [128, TF], BF16))
    vt = st2.enter_context(nc.sbuf_tensor("a0_vt", [128, 32, 130], BF16))
    wq = st2.enter_context(nc.sbuf_tensor("a0_wq", [128, 8, 128], BF16))
    wk = st2.enter_context(nc.sbuf_tensor("a0_wk", [128, 8, 128], BF16))
    wv = st2.enter_context(nc.sbuf_tensor("a0_wv", [128, 8, 128], BF16))
    sq2 = st2.enter_context(nc.sbuf_tensor("a0_sq2", [128, 1024], BF16))
    nm = st2.enter_context(nc.sbuf_tensor("a0_nm", [128, 8], F32))
    kd = st2.enter_context(nc.sbuf_tensor("a0_kd", [128, 32, 64], F32))
    qd = st2.enter_context(nc.sbuf_tensor("a0_qd", [128, 32, 64], F32))
    ksd = st2.enter_context(nc.sbuf_tensor("a0_ksd", [128, 36], F32))
    qsd = st2.enter_context(nc.sbuf_tensor("a0_qsd", [128, 32], F32))
    qmx = st2.enter_context(nc.sbuf_tensor("a0_qmx", [128, 32], F32))
    kmx = st2.enter_context(nc.sbuf_tensor("a0_kmx", [128, 32], F32))
    bq2 = st2.enter_context(nc.sbuf_tensor("a0_bq2", [128, 32], F32))
    bk2 = st2.enter_context(nc.sbuf_tensor("a0_bk2", [128, 32], F32))
    qkvs = st2.enter_context(nc.sbuf_tensor("a0_qkvs", [128, 32, 65], F32))
    tscr = st2.enter_context(nc.sbuf_tensor("a0_tscr", [128, 128], F32))
    grow = st2.enter_context(nc.sbuf_tensor("a0_grow", [1, 2], F32))
    td2 = [st2.enter_context(nc.sbuf_tensor(f"a0_td{j}", [128, 64], BF16)) for j in range(2)]
    qp1 = st2.enter_context(nc.sbuf_tensor("a0_qp1", [64, TF], BF16))
    kpt = st2.enter_context(nc.sbuf_tensor("a0_kpt", [128, 32, 64], BF16))
    kvsb = st2.enter_context(nc.sbuf_tensor("a0_kv", [64, 65], BF16))
    bvb = st2.enter_context(nc.sbuf_tensor("a0_bvb", [128, 128], F32))
    yf = st2.enter_context(nc.sbuf_tensor("a0_yf", [128, TF], BF16))

    ps1 = st2.enter_context(nc.psum_tensor("a0_ps1", [128, 512], F32))
    ps2 = st2.enter_context(nc.psum_tensor("a0_ps2", [128, 512], F32))
    psg = st2.enter_context(nc.psum_tensor("a0_psg", [128, 128], F32))
    psd = st2.enter_context(nc.psum_tensor("a0_psd", [128, 512], F32))
    psk = st2.enter_context(nc.psum_tensor("a0_psk", [128, 80], F32))
    pst = st2.enter_context(nc.psum_tensor("a0_pst", [128, 128], BF16))

    for k in range(8):
        ph.on("sync", lambda e, k=k: e.dma_start(
            out=xnf[:, k, 0:TM], in_=io["ag0out"][k * 128:(k + 1) * 128, :]),
            inc=bd.AD, inc_k=16)
        ph.on("sync", lambda e, k=k: e.dma_start(
            out=xnf[:, k, TM:TF], in_=io["ag0out"][D + k * 128:D + (k + 1) * 128, :]),
            inc=bd.AD, inc_k=16)
    xn_ready = (bd.AD, bd.AD.n)

    for hp in range(4):
        ph.barrier()
        for wn, wsb in (("l0_wqT", wq), ("l0_wkT", wk), ("l0_wvT", wv)):
            ph.on("gpsimd", lambda e, wn=wn, wsb=wsb, hp=hp: e.dma_start(
                out=_full(wsb),
                in_=io[wn][:, hp * 128:(hp + 1) * 128].rearrange("(a p) o -> p a o", p=128)),
                inc=bd.GD, inc_k=16)
        w_ready = (bd.GD, bd.GD.n)
        ph.on("vector", lambda e: e.memset(vt[:, :, 64:65], 1.0), inc=bd.VE)
        ph.on("vector", lambda e: e.memset(vt[:, :, 129:130], 1.0), inc=bd.VE)
        ph.on("tensor", lambda e, hp=hp: e.matmul(
            psg[:, :], sb["ones_f"][0:1, 0:128], sb["l0_bvr"][0:1, hp * 128:(hp + 1) * 128],
            start=True, stop=True), inc=bd.MM)
        ph.on("vector", lambda e: e.tensor_copy(bvb[:, :], psg[:, :]), wait=[(bd.MM, bd.MM.n)], inc=bd.VE)

        for (wsb, dst, bcoln) in ((wq, qf, "l0_bq"), (wk, kf, "l0_bk")):
            for nt in range(8):
                sl = slice(nt * 512, (nt + 1) * 512)
                pj = ps1 if nt % 2 == 0 else ps2
                wt = [w_ready, xn_ready, (bd.SE, bd.SE.n)]
                for k in range(8):
                    ph.on("tensor", lambda e, k=k, sl=sl, pj=pj, wsb=wsb: e.matmul(
                        pj[:, :], wsb[:, k, :], xnf[:, k, sl],
                        start=(k == 0), stop=(k == 7)),
                        wait=wt if k == 0 else [], inc=(bd.MM if k == 7 else None))
                ph.on("vector", lambda e, sl=sl, pj=pj, dst=dst, bcoln=bcoln, hp=hp: e.tensor_scalar(
                    dst[:, sl], pj[:, :], sb[bcoln][:, hp:hp + 1], None, ALU.add),
                    wait=[(bd.MM, bd.MM.n)], inc=bd.SE)
        for c in range(32):
            wt = [w_ready, xn_ready, (bd.VE, bd.VE.n)]
            for k in range(8):
                ph.on("tensor", lambda e, c=c, k=k: e.matmul(
                    psg[:, :], xnf[:, k, c * 128:(c + 1) * 128], wv[:, k, :],
                    start=(k == 0), stop=(k == 7)),
                    wait=wt if k == 0 else [], inc=(bd.MM if k == 7 else None))
            mmn = bd.MM.n
            ph.on("vector", lambda e, c=c: e.tensor_tensor(
                vt[:, c, 0:64], psg[:, 0:64], bvb[:, 0:64], ALU.add),
                wait=[(bd.MM, mmn)], inc=None)
            ph.on("vector", lambda e, c=c: e.tensor_tensor(
                vt[:, c, 65:129], psg[:, 64:128], bvb[:, 64:128], ALU.add),
                inc=bd.VE)
        proj_done = bd.snapshot()

        for hl in range(2):
            rows = slice(hl * 64, (hl + 1) * 64)
            ph.barrier(proj_done)
            for (srcf, dstd, dsts, dstm) in ((qf, qd, qsd, qmx), (kf, kd, ksd, kmx)):
                for g in range(4):
                    ph.on("scalar", lambda e, g=g, rows=rows, srcf=srcf: e.activation(
                        sq2[rows, :], srcf[rows, g * 1024:(g + 1) * 1024], AF.Square),
                        wait=[(bd.MM, bd.MM.n)], inc=bd.SE)
                    sqd = bd.SE.n
                    for c8 in range(8):
                        c = g * 8 + c8
                        cs = slice(c * 128, (c + 1) * 128)
                        ph.on("tensor", lambda e, cs=cs, rows=rows, hp=hp, srcf=srcf, c8=c8: e.matmul(
                            psd[:, c8 * 64:(c8 + 1) * 64], srcf[rows, cs], sb["projT"][rows, hp, :],
                            start=(c8 == 0), stop=(c8 == 7), skip_group_check=True),
                            wait=([(bd.VE, bd.VE.n), (bd.SE, sqd)] if c8 == 0 else []), inc=None)
                    for c8 in range(8):
                        ph.on("tensor", lambda e, rows=rows, c8=c8: e.matmul(
                            psk[:, c8:c8 + 1], sq2[rows, c8 * 128:(c8 + 1) * 128],
                            sb["ones_b"][rows, 0:1],
                            start=(c8 == 0), stop=(c8 == 7), skip_group_check=True),
                            inc=(bd.MM if c8 == 7 else None))
                    mms = bd.MM.n
                    ph.on("vector", lambda e, g=g, dstd=dstd: e.tensor_copy(
                        dstd[:, g * 8:(g + 1) * 8, :].rearrange("p a b -> p (a b)"),
                        psd[:, 0:512]),
                        wait=[(bd.MM, mms)], inc=None)
                    ph.on("vector", lambda e, g=g, dsts=dsts: e.tensor_copy(
                        dsts[:, g * 8:(g + 1) * 8], psk[:, 0:8]), inc=None)
                    ph.on("vector", lambda e, g=g, dstm=dstm: e.tensor_reduce(
                        dstm[:, g * 8:(g + 1) * 8],
                        psd[:, 0:512].rearrange("p (a b) -> p a b", a=8), AX.X, ALU.max),
                        inc=bd.VE)
            ph.on("vector", lambda e: e.tensor_scalar(
                bq2[:, :], qmx[:, :], _c(-C0), _c(LN_RATIO), ALU.mult, ALU.add),
                wait=[(bd.VE, bd.VE.n)], inc=bd.VE)
            ph.on("vector", lambda e: e.scalar_tensor_tensor(
                bq2[:, :], qsd[:, :], _c(-0.5 * C0 * C0), bq2[:, :], ALU.mult, ALU.add),
                wait=[(bd.VE, bd.VE.n)], inc=bd.VE)
            bq_done = bd.VE.n
            ph.on("vector", lambda e: e.tensor_reduce(
                tscr[:, 0:1], kmx[:, :], AX.X, ALU.max),
                wait=[(bd.VE, bd.VE.n)], inc=bd.VE)
            ph.on("tensor", lambda e: e.transpose(
                psg[:, :], tscr[:, :], sb["identf"][:, :]),
                wait=[(bd.VE, bd.VE.n)], inc=bd.MM)
            ph.on("vector", lambda e: e.tensor_reduce(
                grow[0:1, 0:1], psg[0:1, 0:128], AX.X, ALU.max),
                wait=[(bd.MM, bd.MM.n)], inc=bd.VE)
            ph.on("tensor", lambda e: e.matmul(
                psk[:, 8:9], sb["ones_f"][0:1, 0:128], grow[0:1, 0:1], start=True, stop=True),
                wait=[(bd.VE, bd.VE.n)], inc=bd.MM)
            ph.on("vector", lambda e: e.tensor_scalar(
                nm[:, 5:6], psk[:, 8:9], _c(-C0), _c(LN_RATIO), ALU.mult, ALU.add),
                wait=[(bd.MM, bd.MM.n)], inc=bd.VE)
            ph.on("vector", lambda e: e.tensor_scalar(
                bk2[:, :], ksd[:, 0:32], _c(-0.5 * C0 * C0), None, ALU.mult),
                wait=[(bd.VE, bd.VE.n)], inc=bd.VE)
            ph.on("vector", lambda e: e.tensor_scalar(
                bk2[:, :], bk2[:, :], nm[:, 5:6], None, ALU.add),
                wait=[(bd.VE, bd.VE.n)], inc=bd.VE)
            bk_done = bd.VE.n
            pendt = None
            for c in range(32):
                cs = slice(c * 128, (c + 1) * 128)
                par = c % 2
                w = [(bd.VE, bq_done)]
                if c >= 2:
                    w.append((bd.MM, bd.MM.n))
                ph.on("scalar", lambda e, c=c, par=par: e.activation(
                    td2[par][:, 0:64], qd[:, c, :], AF.Exp, bias=bq2[:, c:c + 1], scale=_c(C0)),
                    wait=w, inc=bd.SE)
                sen = bd.SE.n
                if pendt is not None:
                    ph.on("tensor", lambda e, pendt=pendt: e.transpose(
                        pst[0:64, 0:128], td2[pendt[1]][0:128, 0:64], sb["ident"][:, :]),
                        wait=[(bd.SE, pendt[2]), (bd.VE, bd.VE.n)], inc=bd.MM)
                    ph.on("vector", lambda e, pendt=pendt: e.tensor_scalar(
                        qp1[0:64, pendt[0] * 128:(pendt[0] + 1) * 128], pst[0:64, 0:128],
                        _c(EPS_FEAT), None, ALU.add),
                        wait=[(bd.MM, bd.MM.n)], inc=bd.VE)
                pendt = (c, par, sen)
            ph.on("tensor", lambda e, pendt=pendt: e.transpose(
                pst[0:64, 0:128], td2[pendt[1]][0:128, 0:64], sb["ident"][:, :]),
                wait=[(bd.SE, pendt[2]), (bd.VE, bd.VE.n)], inc=bd.MM)
            ph.on("vector", lambda e, pendt=pendt: e.tensor_scalar(
                qp1[0:64, pendt[0] * 128:(pendt[0] + 1) * 128], pst[0:64, 0:128],
                _c(EPS_FEAT), None, ALU.add),
                wait=[(bd.MM, bd.MM.n)], inc=bd.VE)
            for c in range(32):
                par = c % 2
                w = [(bd.VE, bk_done)]
                if c < 2:
                    w.append((bd.MM, bd.MM.n))
                else:
                    w.append((bd.VE, bd.VE.n))
                ph.on("scalar", lambda e, c=c, par=par: e.activation(
                    td2[par][:, 0:64], kd[:, c, :], AF.Exp, bias=bk2[:, c:c + 1], scale=_c(C0)),
                    wait=w, inc=bd.SE)
                ph.on("vector", lambda e, c=c, par=par: e.tensor_scalar(
                    kpt[:, c, :], td2[par][:, 0:64], _c(EPS_FEAT), None, ALU.add),
                    wait=[(bd.SE, bd.SE.n)], inc=bd.VE)
            kv_w = [(bd.VE, bd.VE.n)]
            for c in range(32):
                ph.on("tensor", lambda e, c=c, hl=hl: e.matmul(
                    psk[0:64, 8:73], kpt[:, c, :], vt[:, c, hl * 65:hl * 65 + 65],
                    start=(c == 0), stop=(c == 31)),
                    wait=kv_w if c == 0 else [], inc=(bd.MM if c == 31 else None))
            ph.on("vector", lambda e: e.tensor_copy(kvsb[0:64, :], psk[0:64, 8:73]),
                  wait=[(bd.MM, bd.MM.n)], inc=bd.VE)
            kv_done = (bd.VE, bd.VE.n)
            for g in range(8):
                for c4 in range(4):
                    c = g * 4 + c4
                    cs = slice(c * 128, (c + 1) * 128)
                    ph.on("tensor", lambda e, cs=cs, c4=c4: e.matmul(
                        psd[:, c4 * 65:c4 * 65 + 65], qp1[0:64, cs], kvsb[0:64, 0:65],
                        start=(c4 == 0), stop=(c4 == 3), skip_group_check=True),
                        wait=([kv_done, (bd.VE, bd.VE.n)] if c4 == 0 else []),
                        inc=(bd.MM if c4 == 3 else None))
                ph.on("vector", lambda e, g=g: e.tensor_copy(
                    qkvs[:, g * 4:(g + 1) * 4, :].rearrange("p a b -> p (a b)"), psd[:, 0:260]),
                    wait=[(bd.MM, bd.MM.n)], inc=bd.VE)
            ph.on("vector", lambda e: e.tensor_scalar(
                qmx[:, :], qkvs[:, :, 64], _c(EPS_DEN), None, ALU.max),
                wait=[(bd.VE, bd.VE.n)], inc=bd.VE)
            ph.on("vector", lambda e: e.reciprocal(qmx[:, :], qmx[:, :]),
                  wait=[(bd.VE, bd.VE.n)], inc=bd.VE)
            r_done = bd.VE.n
            pendt = None
            for c in range(32):
                par = c % 2
                w = [(bd.VE, r_done)]
                if c >= 2:
                    w.append((bd.MM, bd.MM.n))
                ph.on("vector", lambda e, c=c, par=par: e.tensor_scalar(
                    td2[par][:, 0:64], qkvs[:, c, 0:64], qmx[:, c:c + 1], None, ALU.mult),
                    wait=w, inc=bd.VE)
                ven = bd.VE.n
                if pendt is not None:
                    ph.on("tensor", lambda e, pendt=pendt: e.transpose(
                        pst[0:64, 0:128], td2[pendt[1]][0:128, 0:64], sb["ident"][:, :]),
                        wait=[(bd.VE, pendt[2]), (bd.VE, bd.VE.n)], inc=bd.MM)
                    ph.on("vector", lambda e, pendt=pendt, rows=rows: e.tensor_copy(
                        yf[rows, pendt[0] * 128:(pendt[0] + 1) * 128], pst[0:64, 0:128]),
                        wait=[(bd.MM, bd.MM.n)], inc=bd.VE)
                pendt = (c, par, ven)
            ph.on("tensor", lambda e, pendt=pendt: e.transpose(
                pst[0:64, 0:128], td2[pendt[1]][0:128, 0:64], sb["ident"][:, :]),
                wait=[(bd.VE, pendt[2]), (bd.VE, bd.VE.n)], inc=bd.MM)
            ph.on("vector", lambda e, pendt=pendt, rows=rows: e.tensor_copy(
                yf[rows, pendt[0] * 128:(pendt[0] + 1) * 128], pst[0:64, 0:128]),
                wait=[(bd.MM, bd.MM.n)], inc=bd.VE)
        ph.on("sync", lambda e, hp=hp: e.dma_start(
            out=io["y0in"][hp * 128:(hp + 1) * 128, :], in_=yf[:, :]),
            wait=[(bd.VE, bd.VE.n)], inc=bd.AD, inc_k=16)
        bd.ev[f"y0p{hp}"] = [(bd.AD, bd.AD.n)]
        if A0_DEBUG and hp == 0:
            ph.barrier()
            for dnm, src in (("dbg_qf", qf[:, :]), ("dbg_kf", kf[:, :]),
                             ("dbg_vt", vt[:, :, :].rearrange("p a b -> p (a b)")),
                             ("dbg_qp1", qp1[0:64, :]),
                             ("dbg_kpt", kpt[:, :, :].rearrange("p a b -> p (a b)")),
                             ("dbg_kv", kvsb[0:64, :]), ("dbg_nm", nm[:, :]), ("dbg_bq2", bq2[:, :]), ("dbg_qmx", qmx[:, :]), ("dbg_qsd", qsd[:, :]), ("dbg_qd", qd[:, :, :].rearrange("p a b -> p (a b)"))):
                ph.on("sync", lambda e, nm2=dnm, src=src: e.dma_start(
                    out=_full(dbg[nm2]), in_=src), inc=bd.AD, inc_k=16)
            ph.barrier()
    ph.emit()
    st2.close()
    bd.ev["y0in"] = [(bd.AD, bd.AD.n)]


MO_DEBUG = False


def emit_moe(bd):
    nc, io, sb = bd.nc, bd.io, bd.sb
    dbg = {}
    if MO_DEBUG:
        for nm2, shp, dt in (("dbgm_idxrow", [1, TM], BF16), ("dbgm_idxt", [128, 16], BF16),
                             ("dbgm_sel", [128, TM], BF16), ("dbgm_logits", [128, 128], BF16),
                             ("dbgm_xe", [128, TM], BF16), ("dbgm_macc", [128, TM], BF16)):
            dbg[nm2] = nc.dram_tensor(nm2, shp, dt, kind="ExternalOutput")
    ph = Ph(bd)
    ph.barrier(bd.ev.get("ag1in", []) + bd.snapshot())
    st2 = ExitStack()
    xnb = [st2.enter_context(nc.sbuf_tensor(f"mo_xn{j}", [128, TM], BF16)) for j in range(2)]
    rt = st2.enter_context(nc.sbuf_tensor("mo_rt", [128, 8, 8], BF16))
    idxt = st2.enter_context(nc.sbuf_tensor("mo_idxt", [128, 16], BF16))
    idxs = st2.enter_context(nc.sbuf_tensor("mo_idxs", [16, 128], BF16))
    idxrow = st2.enter_context(nc.sbuf_tensor("mo_idxrow", [1, TM], BF16))
    mcol = st2.enter_context(nc.sbuf_tensor("mo_mcol", [128, 16], F32))
    ge = st2.enter_context(nc.sbuf_tensor("mo_ge", [128, 128], F32))
    sel = st2.enter_context(nc.sbuf_tensor("mo_sel", [128, TM], BF16))
    xe = st2.enter_context(nc.sbuf_tensor("mo_xe", [128, 8, TM], BF16))
    msb = st2.enter_context(nc.sbuf_tensor("mo_m", [128, 22, TM], BF16))
    g1a = st2.enter_context(nc.sbuf_tensor("mo_g1", [128, 8, TM], BF16))
    eg = [st2.enter_context(nc.sbuf_tensor(f"mo_eg{j}", [128, 512], BF16)) for j in range(2)]
    eu = [st2.enter_context(nc.sbuf_tensor(f"mo_eu{j}", [128, 512], BF16)) for j in range(2)]
    wg = [st2.enter_context(nc.sbuf_tensor(f"mo_wg{j}", [128, 8, 128], BF16)) for j in range(2)]
    wu = [st2.enter_context(nc.sbuf_tensor(f"mo_wu{j}", [128, 8, 128], BF16)) for j in range(2)]
    wd = [st2.enter_context(nc.sbuf_tensor(f"mo_wd{j}", [128, 22, 128], BF16)) for j in range(2)]
    psr = st2.enter_context(nc.psum_tensor("mo_psr", [128, 128], F32))
    pss = st2.enter_context(nc.psum_tensor("mo_pss", [128, 512], F32))
    pstb = st2.enter_context(nc.psum_tensor("mo_pstb", [128, 128], BF16))
    psA = [st2.enter_context(nc.psum_tensor(f"mo_psA{j}", [128, 512], F32)) for j in range(2)]
    psB = [st2.enter_context(nc.psum_tensor(f"mo_psB{j}", [128, 512], F32)) for j in range(2)]

    ph.on("gpsimd", lambda e: e.dma_start(
        out=_full(rt), in_=io["routerT"].rearrange("(a p) o -> p a o", p=128)),
        inc=bd.GD, inc_k=16)
    rt_ready = (bd.GD, bd.GD.n)
    for k in range(8):
        ph.on("sync", lambda e, k=k: e.dma_start(
            out=xe[:, k, :], in_=io["ag1in"][k * 128:(k + 1) * 128, :]),
            inc=bd.AD, inc_k=16)
    xn_r = (bd.AD, bd.AD.n)
    for nch in range(16):
        for k in range(8):
            ph.on("tensor", lambda e, k=k, nch=nch: e.matmul(
                psr[:, nch * 8:(nch + 1) * 8], xe[:, k, nch * 128:(nch + 1) * 128],
                rt[:, k, :], start=(k == 0), stop=(k == 7),
                skip_group_check=True),
                wait=([rt_ready, xn_r] if (nch == 0 and k == 0) else []),
                inc=(bd.MM if (nch == 15 and k == 7) else None))
    mm_r = bd.MM.n
    for nch in range(16):
        ph.on("vector", lambda e, nch=nch: e.tensor_reduce(
            mcol[:, nch:nch + 1], psr[:, nch * 8:(nch + 1) * 8], AX.X, ALU.max),
            wait=[(bd.MM, mm_r)], inc=(bd.VE if nch == 15 else None))
    for nch in range(16):
        ph.on("vector", lambda e, nch=nch: e.tensor_scalar(
            ge[:, nch * 8:(nch + 1) * 8], psr[:, nch * 8:(nch + 1) * 8],
            mcol[:, nch:nch + 1], None, ALU.is_ge),
            wait=[(bd.VE, bd.VE.n)], inc=(bd.VE if nch == 15 else None))
    ph.on("vector", lambda e: e.tensor_mul(ge[:, :], ge[:, :], sb["iotam"][:, :]),
          wait=[(bd.VE, bd.VE.n)], inc=bd.VE)
    ph.on("vector", lambda e: e.tensor_scalar(ge[:, :], ge[:, :], _c(BIGI), None, ALU.add),
          wait=[(bd.VE, bd.VE.n)], inc=bd.VE)
    for nch in range(16):
        ph.on("vector", lambda e, nch=nch: e.tensor_reduce(
            idxt[:, nch:nch + 1], ge[:, nch * 8:(nch + 1) * 8], AX.X, ALU.min),
            wait=[(bd.VE, bd.VE.n)], inc=(bd.VE if nch == 15 else None))
    ph.on("tensor", lambda e: e.transpose(pstb[0:16, 0:128], idxt[:, :], sb["ident"][:, :]),
          wait=[(bd.VE, bd.VE.n)], inc=bd.MM)
    ph.on("vector", lambda e: e.tensor_copy(idxs[:, :], pstb[0:16, 0:128]),
          wait=[(bd.MM, bd.MM.n)], inc=bd.VE)
    ph.on("sync", lambda e: e.dma_start(
        out=io["idxd"][0:1, :].rearrange("a (p f) -> (a p) f", p=16), in_=idxs[:, :]),
        wait=[(bd.VE, bd.VE.n)], inc=bd.AD, inc_k=16)
    ph.on("sync", lambda e: e.dma_start(out=idxrow[0:1, :], in_=io["idxd"][0:1, :]),
          wait=[(bd.AD, bd.AD.n)], inc=bd.AD, inc_k=16)
    idx_ready = (bd.AD, bd.AD.n)
    if MO_DEBUG:
        ph.barrier()
        ph.on("vector", lambda e: e.tensor_copy(sel[:, 0:128], psr[:, :]), inc=bd.VE)
        ph.barrier()
        for nm2, srcap in (("dbgm_idxrow", idxrow[0:1, :]), ("dbgm_idxt", idxt[:, :]),
                           ("dbgm_logits", sel[:, 0:128])):
            ph.on("sync", lambda e, nm2=nm2, srcap=srcap: e.dma_start(
                out=_full(dbg[nm2]), in_=srcap), inc=bd.AD, inc_k=16)
        ph.barrier()

    for ex in range(8):
        ph.barrier()
        for nt in range(4):
            ph.on("tensor", lambda e, nt=nt: e.matmul(
                pss[:, :], sb["ones_b"][0:1, 0:128],
                idxrow[0:1, nt * 512:(nt + 1) * 512], start=True, stop=True),
                wait=[idx_ready, (bd.VE, bd.VE.n)], inc=bd.MM)
            ph.on("vector", lambda e, nt=nt, ex=ex: e.tensor_scalar(
                sel[:, nt * 512:(nt + 1) * 512], pss[:, :], float(ex), None, ALU.is_equal),
                wait=[(bd.MM, bd.MM.n)], inc=bd.VE)
        for k in range(8):
            ph.on("sync", lambda e, k=k: e.dma_start(
                out=xe[:, k, :], in_=io["ag1in"][k * 128:(k + 1) * 128, :]),
                wait=[(bd.MM, bd.MM.n)], inc=bd.AD, inc_k=16)
            ph.on("vector", lambda e, k=k: e.tensor_mul(xe[:, k, :], xe[:, k, :], sel[:, :]),
                  wait=[(bd.AD, bd.AD.n), (bd.VE, bd.VE.n)], inc=bd.VE)
        xe_ready = (bd.VE, bd.VE.n)
        if MO_DEBUG and ex == 2:
            ph.barrier()
            ph.on("sync", lambda e: e.dma_start(out=_full(dbg["dbgm_sel"]), in_=sel[:, :]),
                  inc=bd.AD, inc_k=16)
            ph.on("sync", lambda e: e.dma_start(out=_full(dbg["dbgm_xe"]), in_=xe[:, 0, :]),
                  inc=bd.AD, inc_k=16)
            ph.barrier()
        gd_marks = []
        wf_base = bd.WF.n
        for i in range(22):
            w = [(bd.WF, wf_base + 2 * (i - 1))] if i >= 2 else []
            ph.on("gpsimd", lambda e, i=i, ex=ex: e.dma_start(
                out=_full(wg[i % 2]),
                in_=io["gateT"][ex, :, i * 128:(i + 1) * 128].rearrange("(a p) o -> p a o", p=128)),
                wait=w, inc=bd.GD, inc_k=16)
            ph.on("gpsimd", lambda e, i=i, ex=ex: e.dma_start(
                out=_full(wu[i % 2]),
                in_=io["upT"][ex, :, i * 128:(i + 1) * 128].rearrange("(a p) o -> p a o", p=128)),
                inc=bd.GD, inc_k=16)
            gd_marks.append(bd.GD.n)
        ev_marks = []
        for i in range(22):
            for nt in range(4):
                j = nt % 2
                sl = slice(nt * 512, (nt + 1) * 512)
                wq = [(bd.GD, gd_marks[i]), xe_ready]
                idx = len(ev_marks) - 2
                if idx >= 0:
                    wq.append((bd.SE, ev_marks[idx][0]))
                    wq.append((bd.VE, ev_marks[idx][1]))
                for k in range(8):
                    ph.on("tensor", lambda e, i=i, k=k, sl=sl, j=j: e.matmul(
                        psA[j][:, :], wg[i % 2][:, k, :], xe[:, k, sl],
                        start=(k == 0), stop=(k == 7)),
                        wait=wq if k == 0 else [], inc=None)
                for k in range(8):
                    last = (k == 7)
                    ph.on("tensor", lambda e, i=i, k=k, sl=sl, j=j: e.matmul(
                        psB[j][:, :], wu[i % 2][:, k, :], xe[:, k, sl],
                        start=(k == 0), stop=(k == 7)),
                        inc=(bd.MM if last else None))
                mmn = bd.MM.n
                if nt == 3:
                    ph.on("tensor", lambda e: e.nop(), wait=[(bd.MM, mmn)], inc=bd.WF, inc_k=2)
                ph.on("scalar", lambda e, j=j: e.activation(
                    eg[j][:, :], psA[j][:, :], AF.Silu),
                    wait=[(bd.MM, mmn)], inc=bd.SE)
                ph.on("vector", lambda e, j=j: e.tensor_copy(eu[j][:, :], psB[j][:, :]),
                      wait=[(bd.MM, mmn)], inc=bd.VE)
                ph.on("vector", lambda e, i=i, j=j, sl=sl: e.tensor_mul(
                    msb[:, i, sl], eg[j][:, :], eu[j][:, :]),
                    wait=[(bd.SE, bd.SE.n), (bd.VE, bd.VE.n)], inc=bd.VE)
                ev_marks.append((bd.SE.n, bd.VE.n))
        m_ready = (bd.VE, bd.VE.n)
        gd_marks = []
        wf_base = bd.WF.n
        for i in range(8):
            w = [(bd.WF, wf_base + (i - 1))] if i >= 2 else []
            ph.on("gpsimd", lambda e, i=i, ex=ex: e.dma_start(
                out=_full(wd[i % 2]),
                in_=io["downT"][ex, :, i * 128:(i + 1) * 128].rearrange("(a p) o -> p a o", p=128)),
                wait=w, inc=bd.GD, inc_k=16)
            gd_marks.append(bd.GD.n)
        ve_marks = []
        for i in range(8):
            for nt in range(4):
                j = nt % 2
                sl = slice(nt * 512, (nt + 1) * 512)
                wq = [(bd.GD, gd_marks[i]), m_ready]
                idx = len(ve_marks) - 2
                if idx >= 0:
                    wq.append((bd.VE, ve_marks[idx]))
                for k in range(22):
                    last = (k == 21)
                    ph.on("tensor", lambda e, i=i, k=k, sl=sl, j=j: e.matmul(
                        psA[j][:, :], wd[i % 2][:, k, :], msb[:, k, sl],
                        start=(k == 0), stop=(k == 21)),
                        wait=wq if k == 0 else [], inc=(bd.MM if last else None))
                mmn = bd.MM.n
                if nt == 3:
                    ph.on("tensor", lambda e: e.nop(), wait=[(bd.MM, mmn)], inc=bd.WF)
                if ex == 0:
                    ph.on("vector", lambda e, i=i, sl=sl, j=j: e.tensor_copy(
                        g1a[:, i, sl], psA[j][:, :]),
                        wait=[(bd.MM, mmn)], inc=bd.VE)
                else:
                    ph.on("vector", lambda e, i=i, sl=sl, j=j: e.tensor_add(
                        g1a[:, i, sl], g1a[:, i, sl], psA[j][:, :]),
                        wait=[(bd.MM, mmn)], inc=bd.VE)
                ve_marks.append(bd.VE.n)
    ph.barrier()
    for i in range(8):
        ph.on("scalar", lambda e, i=i: e.activation(g1a[:, i, :], g1a[:, i, :], AF.Silu),
              inc=bd.SE)
        ph.on("sync", lambda e, i=i: e.dma_start(
            out=io["g1"][i * 128:(i + 1) * 128, :], in_=g1a[:, i, :]),
            wait=[(bd.SE, bd.SE.n)], inc=bd.AD, inc_k=16)
    ph.emit()
    st2.close()
    bd.ev["g1"] = [(bd.AD, bd.AD.n)]


def emit_attn1(bd):
    nc, io, sb = bd.nc, bd.io, bd.sb
    ph = Ph(bd)
    ph.barrier(bd.ev.get("ag1out", []) + bd.snapshot())
    st2 = ExitStack()
    xnf = st2.enter_context(nc.sbuf_tensor("a1_xn", [128, 8, TF], BF16))
    qk = {}
    for nmx in ("q1", "q2", "k1", "k2"):
        qk[nmx] = st2.enter_context(nc.sbuf_tensor(f"a1_{nmx}", [128, TF], BF16))
    vt = st2.enter_context(nc.sbuf_tensor("a1_vt", [128, 32, 260], BF16))
    wt = st2.enter_context(nc.sbuf_tensor("a1_wt", [128, 8, 128], BF16))
    wv = st2.enter_context(nc.sbuf_tensor("a1_wv", [128, 8, 256], BF16))
    sqv = st2.enter_context(nc.sbuf_tensor("a1_sq", [128, 512], F32))
    sqb = st2.enter_context(nc.sbuf_tensor("a1_sqb", [128, TF], BF16))
    nrm = st2.enter_context(nc.sbuf_tensor("a1_nrm", [1, 16], F32))
    ccol = st2.enter_context(nc.sbuf_tensor("a1_ccol", [128, 4], F32))
    lcol = st2.enter_context(nc.sbuf_tensor("a1_lcol", [128, 2], F32))
    ptb = [st2.enter_context(nc.sbuf_tensor(f"a1_ptb{j}", [128, 512], BF16)) for j in range(3)]
    smm = [st2.enter_context(nc.sbuf_tensor(f"a1_smm{j}", [128, 512], F32)) for j in range(3)]
    evs = st2.enter_context(nc.sbuf_tensor("a1_evs", [128, 48], F32))
    yc = [st2.enter_context(nc.sbuf_tensor(f"a1_yc{q}", [128, 128], F32)) for q in range(4)]
    yc2a = st2.enter_context(nc.sbuf_tensor("a1_yc2a", [128, 512], F32))
    ytda = st2.enter_context(nc.sbuf_tensor("a1_ytda", [128, TF], BF16))
    yfh = [st2.enter_context(nc.sbuf_tensor(f"a1_yf{hl}", [128, TF], BF16)) for hl in range(2)]
    psS = [st2.enter_context(nc.psum_tensor(f"a1_psS{j}", [128, 512], F32)) for j in range(3)]
    psY = [st2.enter_context(nc.psum_tensor(f"a1_psY{j}", [128, 512], F32)) for j in range(4)]
    
    psTb = st2.enter_context(nc.psum_tensor("a1_psTb", [128, 256], BF16))

    for k in range(8):
        ph.on("sync", lambda e, k=k: e.dma_start(
            out=xnf[:, k, 0:TM], in_=io["ag1out"][k * 128:(k + 1) * 128, :]),
            inc=bd.AD, inc_k=16)
        ph.on("sync", lambda e, k=k: e.dma_start(
            out=xnf[:, k, TM:TF], in_=io["ag1out"][D + k * 128:D + (k + 1) * 128, :]),
            inc=bd.AD, inc_k=16)
    xn_ready = (bd.AD, bd.AD.n)

    for hp in range(4):
        ph.barrier()
        for nmx, wn in (("q1", "wq1T"), ("q2", "wq2T"), ("k1", "wk1T"), ("k2", "wk2T")):
            ph.on("gpsimd", lambda e, wn=wn, hp=hp: e.dma_start(
                out=_full(wt),
                in_=io[wn][:, hp * 128:(hp + 1) * 128].rearrange("(a p) o -> p a o", p=128)),
                wait=[(bd.MM, bd.MM.n)], inc=bd.GD, inc_k=16)
            gdn = bd.GD.n
            for nt in range(8):
                sl = slice(nt * 512, (nt + 1) * 512)
                pj = psS[nt % 2]
                wq2 = [(bd.GD, gdn), xn_ready, (bd.VE, bd.VE.n)]
                for k in range(8):
                    ph.on("tensor", lambda e, k=k, sl=sl, pj=pj: e.matmul(
                        pj[:, :], wt[:, k, :], xnf[:, k, sl],
                        start=(k == 0), stop=(k == 7)),
                        wait=wq2 if k == 0 else [], inc=(bd.MM if k == 7 else None))
                ph.on("vector", lambda e, sl=sl, pj=pj, nmx=nmx: e.tensor_copy(
                    qk[nmx][:, sl], pj[:, :]),
                    wait=[(bd.MM, bd.MM.n)], inc=bd.VE)
        ph.on("gpsimd", lambda e, hp=hp: e.dma_start(
            out=_full(wv),
            in_=io["wvT"][:, hp * 256:(hp + 1) * 256].rearrange("(a p) o -> p a o", p=128)),
            wait=[(bd.MM, bd.MM.n)], inc=bd.GD, inc_k=16)
        gdv = bd.GD.n
        ph.on("vector", lambda e: e.memset(vt[:, :, 128:129], 1.0), inc=bd.VE)
        ph.on("vector", lambda e: e.memset(vt[:, :, 258:259], 1.0), inc=bd.VE)
        for c in range(32):
            pj = psS[c % 2]
            wq2 = [(bd.GD, gdv), xn_ready, (bd.VE, bd.VE.n)]
            for k in range(8):
                ph.on("tensor", lambda e, c=c, k=k, pj=pj: e.matmul(
                    pj[:, 0:256], xnf[:, k, c * 128:(c + 1) * 128], wv[:, k, :],
                    start=(k == 0), stop=(k == 7)),
                    wait=wq2 if k == 0 else [], inc=(bd.MM if k == 7 else None))
            mmn = bd.MM.n
            ph.on("vector", lambda e, c=c, pj=pj: e.tensor_copy(vt[:, c, 0:128], pj[:, 0:128]),
                  wait=[(bd.MM, mmn)], inc=None)
            ph.on("vector", lambda e, c=c, pj=pj: e.tensor_copy(vt[:, c, 130:258], pj[:, 128:256]),
                  inc=bd.VE)
        ph.barrier()

        for hl in range(2):
            rows = slice(hl * 64, (hl + 1) * 64)
            for si, pairnames in enumerate((("q1", "k1"), ("q2", "k2"))):
                for ti, tn in enumerate(pairnames):
                    slot = 4 * hl + 2 * si + ti
                    ph.on("scalar", lambda e, tn=tn, rows=rows: e.activation(
                        sqb[rows, :], qk[tn][rows, :], AF.Square),
                        wait=[(bd.MM, bd.MM.n), (bd.VE, bd.VE.n)], inc=bd.SE)
                    sq_done = bd.SE.n
                    for nt in range(8):
                        sl = slice(nt * 512, (nt + 1) * 512)
                        ph.on("tensor", lambda e, rows=rows, sl=sl: e.matmul(
                            psS[2][0:1, 0:512], sb["ones_b"][rows, 0:1], sqb[rows, sl],
                            start=True, stop=True),
                            wait=[(bd.SE, sq_done), (bd.VE, bd.VE.n)], inc=bd.MM)
                        tgt = nrm[0:1, slot:slot + 1] if nt == 0 else nrm[0:1, 15:16]
                        ph.on("vector", lambda e, tgt=tgt: e.tensor_reduce(
                            tgt, psS[2][0:1, 0:512], AX.X, ALU.max),
                            wait=[(bd.MM, bd.MM.n)], inc=bd.VE)
                        if nt > 0:
                            ph.on("vector", lambda e, slot=slot: e.tensor_max(
                                nrm[0:1, slot:slot + 1], nrm[0:1, slot:slot + 1], nrm[0:1, 15:16]),
                                wait=[(bd.VE, bd.VE.n)], inc=bd.VE)
            for si in range(2):
                qslot = 4 * hl + 2 * si
                ph.on("scalar", lambda e, qslot=qslot: e.activation(
                    nrm[0:1, qslot:qslot + 1], nrm[0:1, qslot:qslot + 1], AF.Sqrt),
                    wait=[(bd.VE, bd.VE.n)], inc=bd.SE)
                ph.on("scalar", lambda e, qslot=qslot: e.activation(
                    nrm[0:1, qslot + 1:qslot + 2], nrm[0:1, qslot + 1:qslot + 2], AF.Sqrt),
                    wait=[(bd.SE, bd.SE.n)], inc=bd.SE)
                ph.on("vector", lambda e, qslot=qslot: e.tensor_mul(
                    nrm[0:1, qslot:qslot + 1], nrm[0:1, qslot:qslot + 1],
                    nrm[0:1, qslot + 1:qslot + 2]),
                    wait=[(bd.SE, bd.SE.n)], inc=bd.VE)
                ph.on("vector", lambda e, qslot=qslot: e.tensor_scalar(
                    nrm[0:1, qslot:qslot + 1], nrm[0:1, qslot:qslot + 1],
                    _c(-0.125), None, ALU.mult),
                    wait=[(bd.VE, bd.VE.n)], inc=bd.VE)
                ph.on("tensor", lambda e, qslot=qslot: e.matmul(
                    psS[2][0:128, 0:1], sb["ones_f"][0:1, 0:128], nrm[0:1, qslot:qslot + 1],
                    start=True, stop=True),
                    wait=[(bd.VE, bd.VE.n)], inc=bd.MM)
                ph.on("vector", lambda e, hl=hl, si=si: e.tensor_copy(
                    ccol[:, 2 * hl + si:2 * hl + si + 1], psS[2][0:128, 0:1]),
                    wait=[(bd.MM, bd.MM.n)], inc=bd.VE)
            ph.on("tensor", lambda e, hl=hl, hp=hp: e.matmul(
                psS[2][0:128, 1:2], sb["ones_f"][0:1, 0:128],
                sb["lamrow"][0:1, hp * 2 + hl:hp * 2 + hl + 1],
                start=True, stop=True),
                wait=[(bd.VE, bd.VE.n)], inc=bd.MM)
            ph.on("vector", lambda e, hl=hl: e.tensor_copy(
                lcol[:, hl:hl + 1], psS[2][0:128, 1:2]),
                wait=[(bd.MM, bd.MM.n)], inc=bd.VE)
        ph.barrier()

        exp_mark = {0: 0, 1: 0, 2: 0}
        evac_free = 0
        for hl in range(2):
            rows = slice(hl * 64, (hl + 1) * 64)
            vsl = slice(hl * 130, hl * 130 + 129)
            tr_marks = []
            for W in range(8):
                pend = []
                pvm = {0: 0, 1: 0, 2: 0}

                def emit_pv(kt, sen, nkt, si, vsl=vsl):
                    for qs in range(4):
                        bank = psY[2 * si + qs // 2]
                        off = (qs % 2) * 129
                        par = kt % 3
                        ph.on("tensor", lambda e, qs=qs, kt=kt, bank=bank, off=off, par=par, nkt=nkt: e.matmul(
                            bank[:, off:off + 129], ptb[par][:, qs * 128:(qs + 1) * 128],
                            vt[:, kt, vsl],
                            start=(kt == nkt - 1 and qs % 2 == 0), stop=(kt == 0),
                            skip_group_check=True),
                            wait=[(bd.SE, sen)] if qs == 0 else [],
                            inc=(bd.MM if qs == 3 else None))
                    pvm[kt % 3] = bd.MM.n

                for si, (qn, kn) in enumerate((("q1", "k1"), ("q2", "k2"))):
                    nkt = 4 * W + 4

                    for kt in list(range(nkt - 1, -1, -1)):
                        par = kt % 3
                        pj = psS[par]
                        wq2 = []
                        if exp_mark[par]:
                            wq2.append((bd.SE, exp_mark[par]))
                        if kt >= nkt - 3:
                            wq2.append((bd.VE, evac_free))
                            if kt == nkt - 1 and si == 0 and W == 0:
                                wq2.append((bd.VE, bd.VE.n))
                                wq2.append((bd.SE, bd.SE.n))
                        ph.on("tensor", lambda e, kt=kt, W=W, pj=pj, rows=rows, qn=qn, kn=kn: e.matmul(
                            pj[:, :], qk[kn][rows, kt * 128:(kt + 1) * 128],
                            qk[qn][rows, W * 512:(W + 1) * 512],
                            start=True, stop=True),
                            wait=wq2, inc=bd.MM)
                        mmn = bd.MM.n
                        while any(p0[0] % 3 == par for p0 in pend):
                            k0, s0, n0, si0 = pend.pop(0)
                            emit_pv(k0, s0, n0, si0)
                        dd = kt - 4 * W
                        if dd >= 0:
                            ph.on("vector", lambda e, dd=dd, pj=pj, par=par: e.tensor_tensor(
                                smm[par][:, :], pj[:, :], sb["masks"][:, dd, :], ALU.add),
                                wait=[(bd.MM, mmn)], inc=bd.VE)
                            ph.on("scalar", lambda e, hl=hl, si=si, par=par: e.activation(
                                ptb[par][:, :], smm[par][:, :], AF.Exp, scale=_c(0.125),
                                bias=ccol[:, 2 * hl + si:2 * hl + si + 1]),
                                wait=[(bd.VE, bd.VE.n), (bd.MM, pvm[par])], inc=bd.SE)
                        else:
                            ph.on("scalar", lambda e, pj=pj, hl=hl, si=si, par=par: e.activation(
                                ptb[par][:, :], pj[:, :], AF.Exp, scale=_c(0.125),
                                bias=ccol[:, 2 * hl + si:2 * hl + si + 1]),
                                wait=[(bd.MM, mmn), (bd.MM, pvm[par])], inc=bd.SE)
                        exp_mark[par] = bd.SE.n
                        pend.append((kt, bd.SE.n, nkt, si))
                        if len(pend) > 2:
                            k0, s0, n0, si0 = pend.pop(0)
                            emit_pv(k0, s0, n0, si0)
                for (k0, s0, n0, si0) in pend:
                    emit_pv(k0, s0, n0, si0)
                pv_done = bd.MM.n
                for qs in range(4):
                    off = (qs % 2) * 129
                    ph.on("vector", lambda e, qs=qs, off=off: e.tensor_copy(
                        evs[:, qs:qs + 1], psY[qs // 2][:, off + 128:off + 129]),
                        wait=[(bd.MM, pv_done)], inc=None)
                    ph.on("vector", lambda e, qs=qs, off=off: e.tensor_copy(
                        evs[:, 4 + qs:5 + qs], psY[2 + qs // 2][:, off + 128:off + 129]),
                        inc=(bd.VE if qs == 3 else None))
                ph.on("vector", lambda e: e.tensor_scalar(
                    evs[:, 8:16], evs[:, 0:8], _c(1e-30), None, ALU.max),
                    wait=[(bd.VE, bd.VE.n)], inc=bd.VE)
                ph.on("vector", lambda e: e.reciprocal(evs[:, 8:16], evs[:, 8:16]),
                      wait=[(bd.VE, bd.VE.n)], inc=bd.VE)
                ph.on("vector", lambda e, hl=hl: e.tensor_scalar(
                    evs[:, 12:16], evs[:, 12:16], lcol[:, hl:hl + 1], None, ALU.mult),
                    wait=[(bd.VE, bd.VE.n)], inc=bd.VE)
                ph.on("vector", lambda e: e.tensor_scalar(
                    evs[:, 12:16], evs[:, 12:16], _c(-1.0), None, ALU.mult),
                    wait=[(bd.VE, bd.VE.n)], inc=bd.VE)
                rdy = bd.VE.n
                for qs in range(4):
                    off = (qs % 2) * 129
                    ph.on("vector", lambda e, qs=qs, off=off: e.tensor_scalar(
                        yc[qs][:, :], psY[qs // 2][:, off:off + 128], evs[:, 8 + qs:9 + qs], None, ALU.mult),
                        wait=[(bd.VE, rdy)], inc=(bd.VE if qs == 3 else None))
                ycd = bd.VE.n
                for qs in range(4):
                    off = (qs % 2) * 129
                    ph.on("vector", lambda e, qs=qs, off=off: e.scalar_tensor_tensor(
                        yc2a[:, qs * 128:(qs + 1) * 128], psY[2 + qs // 2][:, off:off + 128],
                        evs[:, 12 + qs:13 + qs], yc[qs][:, :],
                        ALU.mult, ALU.add),
                        wait=[(bd.VE, ycd)] if qs == 0 else [], inc=bd.VE)
                evac_free = bd.VE.n
                ph.on("vector", lambda e: e.tensor_reduce(
                    evs[:, 16:20], yc2a[:, :].rearrange("p (a b) -> p a b", a=4), AX.X, ALU.add),
                    wait=[(bd.VE, bd.VE.n)], inc=bd.VE)
                ph.on("vector", lambda e: e.tensor_scalar(
                    evs[:, 16:20], evs[:, 16:20], _c(1.0 / 128.0), None, ALU.mult),
                    wait=[(bd.VE, bd.VE.n)], inc=bd.VE)
                for qs in range(4):
                    ph.on("vector", lambda e, qs=qs: e.tensor_scalar(
                        yc2a[:, qs * 128:(qs + 1) * 128], yc2a[:, qs * 128:(qs + 1) * 128],
                        evs[:, 16 + qs:17 + qs], None, ALU.subtract),
                        wait=[(bd.VE, bd.VE.n)] if qs == 0 else [], inc=(bd.VE if qs == 3 else None))
                ph.on("scalar", lambda e: e.activation(sqv[:, 0:512], yc2a[:, :], AF.Square),
                      wait=[(bd.VE, bd.VE.n), (bd.SE, bd.SE.n)], inc=bd.SE)
                ph.on("vector", lambda e: e.tensor_reduce(
                    evs[:, 20:24], sqv[:, 0:512].rearrange("p (a b) -> p a b", a=4), AX.X, ALU.add),
                    wait=[(bd.SE, bd.SE.n)], inc=bd.VE)
                ph.on("scalar", lambda e: e.activation(
                    evs[:, 24:28], evs[:, 20:24], AF.Sqrt, scale=_c(1.0 / 128.0), bias=_c(EPS_LN)),
                    wait=[(bd.VE, bd.VE.n)], inc=bd.SE)
                ph.on("vector", lambda e: e.reciprocal(evs[:, 24:28], evs[:, 24:28]),
                      wait=[(bd.SE, bd.SE.n)], inc=bd.VE)
                ph.on("vector", lambda e: e.tensor_scalar(
                    evs[:, 24:28], evs[:, 24:28], _c(1.0 - LAMBDA_INIT), None, ALU.mult),
                    wait=[(bd.VE, bd.VE.n)], inc=bd.VE)
                for qs in range(4):
                    ph.on("vector", lambda e, qs=qs, W=W: e.tensor_scalar(
                        ytda[:, W * 512 + qs * 128:W * 512 + (qs + 1) * 128],
                        yc2a[:, qs * 128:(qs + 1) * 128], evs[:, 24 + qs:25 + qs], None, ALU.mult),
                        wait=[(bd.VE, bd.VE.n)] if qs == 0 else [], inc=(bd.VE if qs == 3 else None))
                    tr_marks.append(bd.VE.n)
            for ci in range(32):
                ph.on("tensor", lambda e, ci=ci: e.transpose(
                    psTb[:, 0:128], ytda[:, ci * 128:(ci + 1) * 128], sb["ident"][:, :]),
                    wait=[(bd.VE, tr_marks[ci]), (bd.VE, bd.VE.n)], inc=bd.MM)
                ph.on("vector", lambda e, ci=ci, hl=hl: e.tensor_copy(
                    yfh[hl][:, ci * 128:(ci + 1) * 128], psTb[:, 0:128]),
                    wait=[(bd.MM, bd.MM.n)], inc=bd.VE)
            ph.on("sync", lambda e, hl=hl, hp=hp: e.dma_start(
                out=io["y1in"][(hp * 2 + hl) * 128:(hp * 2 + hl + 1) * 128, :], in_=yfh[hl][:, :]),
                wait=[(bd.VE, bd.VE.n)], inc=bd.AD, inc_k=16)
            if hl == 1:
                bd.ev[f"y1p{hp}"] = [(bd.AD, bd.AD.n)]
    ph.emit()
    st2.close()
    bd.ev["y1in"] = [(bd.AD, bd.AD.n)]


def emit_cc(bd, in_name, out_names):
    ph = Ph(bd)
    if len(out_names) == 1:
        s = bd.ccsem()
        ph.on("gpsimd", lambda e: e.collective_compute(
            "AllGather", ALU.bypass, replica_groups=GROUPS,
            ins=[bd.io[in_name].ap().opt()], outs=[bd.io[out_names[0]].ap().opt()]),
            wait=bd.ev.get(in_name, []), inc=s)
        bd.ev[out_names[0]] = [(s, 1)]
    else:
        nr = bd.io[in_name].shape[0] // 4
        evp = "y0p" if in_name == "y0in" else "y1p"
        for h in range(4):
            s = bd.ccsem()
            ph.on("gpsimd", lambda e, h=h: e.collective_compute(
                "AllGather", ALU.bypass, replica_groups=GROUPS,
                ins=[bd.io[in_name][h * nr:(h + 1) * nr, :].opt()],
                outs=[bd.io[out_names[h]].ap().opt()]),
                wait=bd.ev.get(f"{evp}{h}", []), inc=s)
            bd.ev[out_names[h]] = [(s, 1)]
    ph.emit()


def emit_final_wait(bd):
    ph = Ph(bd)
    ph.on("sync", lambda e: e.nop(), wait=[(bd.AD, bd.AD.n)], inc=None)
    ph.emit()



MID = {
    "ag0in": ([D, TM], BF16), "ag0out": ([2 * D, TM], BF16),
    "g0": ([D, TM], BF16),
    "y0in": ([512, TF], BF16),
    "y0outp0": ([256, TF], BF16), "y0outp1": ([256, TF], BF16),
    "y0outp2": ([256, TF], BF16), "y0outp3": ([256, TF], BF16),
    "x1": ([D, TM], F32),
    "ag1in": ([D, TM], BF16), "ag1out": ([2 * D, TM], BF16),
    "g1": ([D, TM], BF16),
    "y1in": ([1024, TF], BF16),
    "y1outp0": ([512, TF], BF16), "y1outp1": ([512, TF], BF16),
    "y1outp2": ([512, TF], BF16), "y1outp3": ([512, TF], BF16),
}

INPUTS_BASE = [
    ("xT", [D, TM], F32),
    ("l0_wqT", [D, 512], F32), ("l0_wkT", [D, 512], F32), ("l0_wvT", [D, 512], F32),
    ("l0_projT", [64, 8, 64], F32),
    ("l0_woT", [D, D], F32),
    ("w1T", [D, DHID := 4096], F32), ("w2T", [D, 4096], F32),
    ("h1T", [4096, 4096], F32), ("h2T", [4096, 4096], F32),
    ("w3T", [4096, D], F32),
    ("wq1T", [D, 512], F32), ("wq2T", [D, 512], F32),
    ("wk1T", [D, 512], F32), ("wk2T", [D, 512], F32),
    ("wvT", [D, 1024], F32), ("wcT", [2048, D], F32),
    ("routerT", [D, 8], F32),
    ("gateT", [8, D, 2816], F32), ("upT", [8, D, 2816], F32),
    ("downT", [8, 2816, D], F32),
]

PRELOAD_IN = [(nm, ([128, max(1, len(_CONST_VALS))] if shp is None else shp), dt)
              for nm, shp, dt in PRELOAD_TENS] + [("l0_projT", [128, 4, 64], F32)]

PHASE_READS = {
    "p0": ["xT"],
    "p1": ["ag0in", "w1T", "w2T", "h1T", "h2T", "w3T"],
    "p2a": ["ag0out", "l0_wqT", "l0_wkT", "l0_wvT", "l0_projT"],
    "p2b": ["y0outp0", "y0outp1", "y0outp2", "y0outp3", "l0_woT", "g0", "xT"],
    "p3": ["ag1in", "routerT", "gateT", "upT", "downT"],
    "p4a": ["ag1out", "wq1T", "wq2T", "wk1T", "wk2T", "wvT"],
    "p4b": ["y1outp0", "y1outp1", "y1outp2", "y1outp3", "wcT", "g1", "x1"],
}
PHASE_WRITES = {
    "p0": ["ag0in"], "p1": ["g0"], "p2a": ["y0in"], "p2b": ["x1", "ag1in"],
    "p3": ["g1"], "p4a": ["y1in"], "p4b": ["outT"],
}
UNFUSED_LAUNCHES = ["p0", "p1", "p2a", "p2b", "p3", "p4a", "p4b"]
CC_ROUTES = {"ag0in": ["ag0out"], "ag1in": ["ag1out"],
             "y0in": [f"y0outp{h}" for h in range(4)],
             "y1in": [f"y1outp{h}" for h in range(4)]}


def _emit_phase(bd, name):
    if name == "p0":
        emit_ln(bd, "xT", "ag0in")
    elif name == "p1":
        emit_ffn0(bd)
    elif name == "p2a":
        emit_attn0(bd)
    elif name == "p2b":
        st3 = ExitStack()
        s0 = st3.enter_context(bd.nc.sbuf_tensor("s0sb", [128, 8, TM], BF16))
        sdone = emit_proj_out(bd, "y0in", "l0_woT", 8, "l0_bo", s0)
        bd.ev["x_s0"] = [sdone]
        emit_combine(bd, "xT", lambda k: s0[:, k, :], "g0", "x1")
        emit_ln(bd, "x1", "ag1in")
        st3.close()
    elif name == "p3":
        emit_moe(bd)
    elif name == "p4a":
        emit_attn1(bd)
    elif name == "p4b":
        st3 = ExitStack()
        s1 = st3.enter_context(bd.nc.sbuf_tensor("s1sb", [128, 8, TM], BF16))
        emit_proj_out(bd, "y1in", "wcT", 16, None, s1)
        emit_combine(bd, "x1", lambda k: s1[:, k, :], "g1", "outT")
        st3.close()
    else:
        raise ValueError(name)


def build_program(phases, fused):
    nc = bass.Bass("TRN2", target_bir_lowering=False, debug=False)
    st = ExitStack()
    io = {}
    reads = set()
    writes = set()
    for p in phases:
        reads.update(PHASE_READS[p])
        writes.update(PHASE_WRITES[p])
    if fused:
        for a, outs2 in CC_ROUTES.items():
            if a in writes:
                for b in outs2:
                    reads.discard(b)
    base_shapes = {nm: (shp, dt) for nm, shp, dt in INPUTS_BASE + PRELOAD_IN}
    innames = []
    outnames = []
    for nm, shp, dt in PRELOAD_IN:
        io[nm] = nc.dram_tensor(nm, shp, dt, kind="ExternalInput")
        innames.append(nm)
    for nm in sorted(reads):
        if nm in io:
            continue
        if nm in MID:
            shp, dt = MID[nm]
            kind = "Internal" if fused else "ExternalInput"
        else:
            shp, dt = base_shapes[nm]
            kind = "ExternalInput"
        io[nm] = nc.dram_tensor(nm, shp, dt, kind=kind)
        if kind == "ExternalInput":
            innames.append(nm)
    for nm in sorted(writes):
        if nm in io:
            continue
        if nm == "outT":
            io[nm] = nc.dram_tensor(nm, [D, TM], F32, kind="ExternalOutput")
            outnames.append(nm)
            continue
        shp, dt = MID[nm]
        kind = "Internal" if fused else "ExternalOutput"
        io[nm] = nc.dram_tensor(nm, shp, dt, kind=kind)
        if kind == "ExternalOutput":
            outnames.append(nm)
    if fused:
        for a, outs2 in CC_ROUTES.items():
            if a in writes:
                for b in outs2:
                    if b not in io:
                        shp, dt = MID[b]
                        io[b] = nc.dram_tensor(b, shp, dt, kind="Internal")
    if "p3" in phases:
        io["idxd"] = nc.dram_tensor("idxd", [1, TM], BF16, kind="Internal")

    bd = Bld(nc, st, io)
    emit_preload(bd)
    for p in phases:
        _emit_phase(bd, p)
        if fused and p in ("p0", "p2a", "p2b", "p4a"):
            for a in PHASE_WRITES[p]:
                if a in CC_ROUTES:
                    emit_cc(bd, a, CC_ROUTES[a])
    emit_final_wait(bd)
    st.close()
    return nc, innames, outnames



def host_prep(inputs):
    fT = lambda a: np.ascontiguousarray(np.asarray(a, dtype=np.float32).T)
    f = lambda a: np.ascontiguousarray(np.asarray(a, dtype=np.float32))
    shared = {}
    shared["consts"] = np.ascontiguousarray(
        np.tile(np.array(_CONST_VALS, np.float32), (128, 1)))
    shared["ident"] = np.eye(128, dtype=BD16)
    shared["identf"] = np.eye(128, dtype=np.float32)
    shared["iotam"] = np.ascontiguousarray(
        np.tile(np.arange(8, dtype=np.float32) - BIGI, (128, 16)))
    masks = np.zeros((128, 4, 512), np.float32)
    kk = np.arange(128)[:, None]
    qq = np.arange(512)[None, :]
    for dd in range(4):
        masks[:, dd, :] = np.where(kk + 128 * dd <= qq, 0.0, NEGBIG)
    shared["masks"] = masks
    shared["w1T"] = fT(inputs["L0_ffn_w1_w"])
    shared["w2T"] = fT(inputs["L0_ffn_w2_w"])
    shared["h1T"] = fT(inputs["L0_ffn_h1_w"])
    shared["h2T"] = fT(inputs["L0_ffn_h2_w"])
    shared["w3T"] = fT(inputs["L0_ffn_w3_w"])
    shared["s1c"] = f(inputs["L0_ffn_w1_s"]).reshape(32, 128).T.copy()
    shared["s2c"] = f(inputs["L0_ffn_w2_s"]).reshape(32, 128).T.copy()
    shared["sh1c"] = f(inputs["L0_ffn_h1_s"]).reshape(32, 128).T.copy()
    shared["sh2c"] = f(inputs["L0_ffn_h2_s"]).reshape(32, 128).T.copy()
    shared["s3c"] = f(inputs["L0_ffn_w3_s"]).reshape(8, 128).T.copy()
    shared["l0_woT"] = fT(inputs["L0_attn_Wo"])
    shared["l0_bo"] = f(inputs["L0_attn_bo"]).reshape(8, 128).T.copy()
    shared["wcT"] = fT(inputs["L1_attn_Wc"])
    shared["routerT"] = fT(inputs["L1_ffn_router_w"])
    shared["gateT"] = np.ascontiguousarray(
        np.asarray(inputs["L1_ffn_gate_w"], np.float32).transpose(0, 2, 1))
    shared["upT"] = np.ascontiguousarray(
        np.asarray(inputs["L1_ffn_up_w"], np.float32).transpose(0, 2, 1))
    shared["downT"] = np.ascontiguousarray(
        np.asarray(inputs["L1_ffn_down_w"], np.float32).transpose(0, 2, 1))

    x = np.asarray(inputs["L0_attn_query"], np.float32)
    percore = []
    for c in range(8):
        b, half = c // 2, c % 2
        hs = slice(half * 512, (half + 1) * 512)
        hv = slice(half * 1024, (half + 1) * 1024)
        h8 = slice(half * 8, (half + 1) * 8)
        d = {}
        d["xT"] = np.ascontiguousarray(x[b, half * TM:(half + 1) * TM, :].T)
        d["l0_wqT"] = fT(np.asarray(inputs["L0_attn_Wq"], np.float32)[hs, :])
        d["l0_wkT"] = fT(np.asarray(inputs["L0_attn_Wk"], np.float32)[hs, :])
        d["l0_wvT"] = fT(np.asarray(inputs["L0_attn_Wv"], np.float32)[hs, :])
        d["l0_bq"] = f(inputs["L0_attn_bq"])[hs].reshape(4, 128).T.copy()
        d["l0_bk"] = f(inputs["L0_attn_bk"])[hs].reshape(4, 128).T.copy()
        d["l0_bvr"] = f(inputs["L0_attn_bv"])[hs].reshape(1, 512).copy()
        pmy = np.asarray(inputs["L0_attn_proj"], np.float32)[h8]
        pT = np.zeros((128, 4, 64), np.float32)
        for hp in range(4):
            for hl in range(2):
                pT[hl * 64:(hl + 1) * 64, hp, :] = pmy[2 * hp + hl].T
        d["l0_projT"] = pT
        d["wq1T"] = fT(np.asarray(inputs["L1_attn_Wq1"], np.float32)[hs, :])
        d["wq2T"] = fT(np.asarray(inputs["L1_attn_Wq2"], np.float32)[hs, :])
        d["wk1T"] = fT(np.asarray(inputs["L1_attn_Wk1"], np.float32)[hs, :])
        d["wk2T"] = fT(np.asarray(inputs["L1_attn_Wk2"], np.float32)[hs, :])
        d["wvT"] = fT(np.asarray(inputs["L1_attn_Wv"], np.float32)[hv, :])
        d["lq1"] = f(inputs["L1_attn_lq1"])[h8].copy()
        d["lk1"] = f(inputs["L1_attn_lk1"])[h8].copy()
        d["lq2"] = f(inputs["L1_attn_lq2"])[h8].copy()
        d["lk2"] = f(inputs["L1_attn_lk2"])[h8].copy()
        d["coloff"] = np.array([[half * TM]], np.uint32)
        percore.append(d)
    return shared, percore


def _route_cc(vals, produced):
    for a, outs2 in CC_ROUTES.items():
        if a not in produced:
            continue
        for m in range(4):
            if len(outs2) == 1:
                cat = np.concatenate([vals[2 * m][a], vals[2 * m + 1][a]], axis=0)
                vals[2 * m][outs2[0]] = cat
                vals[2 * m + 1][outs2[0]] = cat
            else:
                nr = vals[2 * m][a].shape[0] // 4
                for h in range(4):
                    cat = np.concatenate([vals[2 * m][a][h * nr:(h + 1) * nr],
                                          vals[2 * m + 1][a][h * nr:(h + 1) * nr]], axis=0)
                    vals[2 * m][outs2[h]] = cat
                    vals[2 * m + 1][outs2[h]] = cat


def kernel(**inputs):
    shared, percore = host_prep(inputs)
    outs = [None] * 8
    if FUSED:
        nc, innames, outnames = build_program(UNFUSED_LAUNCHES, fused=True)
        maps = []
        for c in range(8):
            m = {}
            for n in innames:
                v = percore[c].get(n)
                if v is None:
                    v = shared[n]
                m[n] = v
            maps.append(m)
        res = run_bass_kernel_spmd(nc, maps, core_ids=list(range(8)))
        for c in range(8):
            outs[c] = res.results[c]["outT"]
    else:
        vals = [dict() for _ in range(8)]
        for launch in UNFUSED_LAUNCHES:
            nc, innames, outnames = build_program([launch], fused=False)
            maps = []
            for c in range(8):
                m = {}
                for n in innames:
                    v = percore[c].get(n)
                    if v is None:
                        v = shared.get(n)
                    if v is None:
                        v = vals[c][n]
                    m[n] = v
                maps.append(m)
            res = run_bass_kernel_spmd(nc, maps, core_ids=list(range(8)))
            for c in range(8):
                for n in outnames:
                    vals[c][n] = res.results[c][n]
            _route_cc(vals, PHASE_WRITES[launch])
        for c in range(8):
            outs[c] = vals[c]["outT"]
    out = np.empty((4, TF, D), np.float32)
    for c in range(8):
        b, half = c // 2, c % 2
        out[b, half * TM:(half + 1) * TM, :] = outs[c].T
    return out
